# version 09
# baseline (speedup 1.0000x reference)
import math
from contextlib import ExitStack

import numpy as np

import concourse.bass as bass
import concourse.mybir as mybir
from concourse.bass_utils import run_bass_kernel_spmd

F32 = mybir.dt.float32
AF = mybir.ActivationFunctionType
OP = mybir.AluOpType
AX = mybir.AxisListType.X

B, T, C = 4, 4096, 1024
H, HD = 16, 64
HL = 8
CL = HL * HD
NT = T // 128
NCH = T // 512
E = 8
CAP = 4096
TH = T // 2
FF = 4096
EXPC = 12.0
LN_EPS = 1e-5



def _gammas():
    lo = math.log(1.0 - 0.999)
    hi = math.log(1.0 - 0.9)
    return (1.0 - np.exp(np.linspace(lo, hi, H))).astype(np.float32)


def _tile_kmaj(a):
    K, M = a.shape
    return np.ascontiguousarray(
        a.reshape(K // 128, 128, M).transpose(1, 0, 2).reshape(128, (K // 128) * M)
    ).astype(np.float32)


def _host_consts(core):
    hg = core % 2
    g = _gammas()[hg * HL:(hg + 1) * HL].astype(np.float64)
    iloc = np.arange(128)

    dist = iloc[None, :] - iloc[:, None]
    d0t = np.zeros((128, HL * 128), np.float32)
    for h in range(HL):
        d0t[:, h * 128:(h + 1) * 128] = np.where(
            dist >= 0, g[h] ** np.maximum(dist, 0), 0.0).astype(np.float32)

    u2 = np.zeros((128, 4 * 128), np.float32)
    qs = np.zeros((128, 4 * 128), np.float32)
    c128 = np.zeros((128, 4 * 64), np.float32)
    for rt in range(4):
        for hl in range(2):
            gf = g[2 * rt + hl]
            u2[:, rt * 128 + hl * 64:rt * 128 + (hl + 1) * 64] = \
                (gf ** (127 - iloc))[:, None].astype(np.float32)
            qs[hl * 64:(hl + 1) * 64, rt * 128:(rt + 1) * 128] = \
                (gf ** (iloc + 1))[None, :].astype(np.float32)
            c128[hl * 64:(hl + 1) * 64, rt * 64:(rt + 1) * 64] = \
                np.float32(gf ** 128)

    ident = np.eye(128, dtype=np.float32)

    invf = 1.0 / (10000.0 ** (np.arange(0, HD, 2, dtype=np.float64) / HD))
    ang = np.arange(T, dtype=np.float64)[:, None] * invf[None, :]
    crep = np.tile(np.cos(ang).astype(np.float32).T, (4, 1))
    srep = np.tile(np.sin(ang).astype(np.float32).T, (4, 1))

    dmask = np.zeros((128, 4 * 512), np.float32)
    for kj in range(4):
        kg = kj * 128 + iloc[:, None]
        qg = np.arange(512)[None, :]
        dmask[:, kj * 512:(kj + 1) * 512] = (qg >= kg).astype(np.float32)

    triu_incl = np.triu(np.ones((128, 128), np.float32), 0)
    triu_strict = np.triu(np.ones((128, 128), np.float32), 1)

    sel2 = np.zeros((128, 2), np.float32)
    sel2[:64, 0] = 1.0
    sel2[64:, 1] = 1.0
    sel2T = np.ascontiguousarray(sel2.T)

    ones1_128 = np.ones((1, 128), np.float32)
    ones1_64 = np.ones((1, 64), np.float32)

    return {
        "d0t": d0t, "u2": u2, "qs": qs, "c128": c128, "ident": ident,
        "crep": crep, "srep": srep, "dmask": dmask,
        "triu_incl": triu_incl, "triu_strict": triu_strict,
        "sel2": sel2, "sel2T": sel2T,
        "ones1_128": ones1_128, "ones1_64": ones1_64,
        "evec128": np.broadcast_to(np.arange(8, dtype=np.float32)[None, :],
                                   (128, 8)).copy(),
    }


def _core_inputs(ins, core):
    b = core // 2
    hg = core % 2
    hsl = slice(hg * CL, (hg + 1) * CL)

    x = np.ascontiguousarray(np.asarray(ins["L0_attn_x"])[b]).astype(np.float32)

    wq = np.asarray(ins["L0_attn_wq"]); wk = np.asarray(ins["L0_attn_wk"])
    wv = np.asarray(ins["L0_attn_wv"]); wo = np.asarray(ins["L0_attn_wo"])
    wqC = _tile_kmaj(np.ascontiguousarray(wq[hsl].T))
    wkC = _tile_kmaj(np.ascontiguousarray(wk[hsl].T))
    wvC = _tile_kmaj(np.ascontiguousarray(wv[hsl].T))
    woC = _tile_kmaj(np.ascontiguousarray(wo[:, hsl].T))

    gw = np.asarray(ins["L0_attn_gn_w"])[hsl].reshape(HL, HD)
    gb = np.asarray(ins["L0_attn_gn_b"])[hsl].reshape(HL, HD)
    gnw128 = np.zeros((128, 4), np.float32)
    gnb128 = np.zeros((128, 4), np.float32)
    for rt in range(4):
        for hl in range(2):
            gnw128[hl * 64:(hl + 1) * 64, rt] = gw[2 * rt + hl]
            gnb128[hl * 64:(hl + 1) * 64, rt] = gb[2 * rt + hl]

    Wqkv = np.asarray(ins["L0_ffn_Wqkv"])
    heads = np.arange(hg * HL, (hg + 1) * HL)
    colq = heads[:, None] * 192 + np.arange(64)[None, :]
    colk = colq + 64
    colv = colq + 128
    cols = np.concatenate([colq[:, :32].ravel(), colq[:, 32:].ravel(),
                           colk[:, :32].ravel(), colk[:, 32:].ravel(),
                           colv.ravel()])
    WqkvC = _tile_kmaj(np.ascontiguousarray(Wqkv[:, cols]))

    WprojR = _tile_kmaj(np.ascontiguousarray(
        np.asarray(ins["L0_ffn_Wproj"])[hsl, :]))

    WgateC = _tile_kmaj(np.asarray(ins["L0_ffn_Wgate"]).astype(np.float32))

    W1 = np.asarray(ins["L0_ffn_W1"]).astype(np.float32)
    W1t = np.ascontiguousarray(
        W1.reshape(E, 8, 128, 32, 128).transpose(0, 3, 2, 1, 4)).reshape(
        E, 32, 128, 8 * 128)
    W2 = np.ascontiguousarray(np.asarray(ins["L0_ffn_W2"])).astype(np.float32)
    b1t = np.ascontiguousarray(
        np.asarray(ins["L0_ffn_b1"]).reshape(E, 32, 128).transpose(2, 0, 1)
    ).reshape(128, E * 32).astype(np.float32)
    b2sb = np.asarray(ins["L0_ffn_b2"]).astype(np.float32)

    m = {
        "x": x, "wqC": wqC, "wkC": wkC, "wvC": wvC, "woC": woC,
        "gnw128": gnw128, "gnb128": gnb128,
        "WqkvC": WqkvC, "WprojR": WprojR,
        "ln1g": np.asarray(ins["L0_ffn_ln1_g"]).reshape(1, C).astype(np.float32),
        "ln1b": np.asarray(ins["L0_ffn_ln1_b"]).reshape(1, C).astype(np.float32),
        "ln2g": np.asarray(ins["L0_ffn_ln2_g"]).reshape(1, C).astype(np.float32),
        "ln2b": np.asarray(ins["L0_ffn_ln2_b"]).reshape(1, C).astype(np.float32),
        "WgateC": WgateC, "W1t": W1t, "W2": W2, "b1t": b1t, "b2sb": b2sb,
    }
    m.update(_host_consts(core))
    return m



ENGS = ("sync", "tensor", "vector", "scalar", "gpsimd")


CHAINED = ("vector", "scalar")


class Sched:

    def __init__(self, nc, stack):
        self.nc = nc
        self.q = {e: [] for e in ENGS}
        self.cnt = {}
        self.h = {}
        self.stack = stack
        self.names = {}
        self.hist = {}

    def sem(self, name):
        if name not in self.h:
            self.h[name] = self.stack.enter_context(self.nc.semaphore(name))
            self.cnt[name] = 0
        return self.h[name]

    def emit(self, eng, fn):
        self.q[eng].append(fn)

    def inc(self, eng, sname, k, build):
        if eng in CHAINED:
            ch = "ch_" + eng
            self.sem(ch)
            t = self.cnt[ch]
            h = self.h[ch]
            if t > 0:
                self.q[eng].append(lambda e, h=h, t=t: e.wait_ge(h, t))
            self.cnt[ch] += 1
            self.names[sname] = (ch, self.cnt[ch])
            self.hist.setdefault(sname, []).append(self.cnt[ch])
            self.q[eng].append(lambda e, b=build, h=h: b(e).then_inc(h, 1))
        else:
            self.sem(sname)
            self.cnt[sname] += k
            self.names[sname] = (sname, self.cnt[sname])
            h = self.h[sname]
            self.q[eng].append(lambda e, b=build, h=h, k=k: b(e).then_inc(h, k))

    def wait(self, eng, sname, target=None):
        if sname.startswith("ch_"):
            if sname not in self.cnt or self.cnt[sname] <= 0:
                return
            h = self.h[sname]
            t = self.cnt[sname]
            self.q[eng].append(lambda e, h=h, t=t: e.wait_ge(h, t))
            return
        if sname not in self.names:
            if target is None or target <= 0:
                return
            self.sem(sname)
            self.names[sname] = (sname, 0)
        s, v = self.names[sname]
        if target is not None:
            assert s == sname, f"explicit target on chained name {sname}"
            v = target
        if v <= 0:
            return
        h = self.h[s]
        self.q[eng].append(lambda e, h=h, t=v: e.wait_ge(h, t))

    def wait_lag(self, eng, sname, lag):
        hist = self.hist.get(sname, [])
        if len(hist) <= lag:
            return
        s, _ = self.names[sname]
        v = hist[-1 - lag]
        h = self.h[s]
        self.q[eng].append(lambda e, h=h, t=v: e.wait_ge(h, t))

    def join(self, engines, names):
        for e in engines:
            for n in names:
                self.wait(e, n)

    def dma(self, eng, sname, dst, src):
        self.inc(eng, sname, 16, lambda e, d=dst, s=src: e.dma_start(out=d, in_=s))

    def run(self, block):
        for eng in ENGS:
            ops = self.q[eng]

            def make(ops):
                def body(e):
                    for f in ops:
                        f(e)
                return body
            getattr(block, eng)(make(ops))



IN_SPECS = [
    ("x", (T, C)), ("wqC", (128, 8 * 512)), ("wkC", (128, 8 * 512)),
    ("wvC", (128, 8 * 512)), ("woC", (128, 4 * 1024)),
    ("gnw128", (128, 4)), ("gnb128", (128, 4)),
    ("WqkvC", (128, 8 * 1536)), ("WprojR", (128, 4 * 1024)),
    ("ln1g", (1, C)), ("ln1b", (1, C)), ("ln2g", (1, C)), ("ln2b", (1, C)),
    ("WgateC", (128, 8 * 8)), ("W1t", (E, 32, 128, 8 * 128)),
    ("W2", (E, FF, C)), ("b1t", (128, E * 32)), ("b2sb", (8, C)),
    ("d0t", (128, HL * 128)), ("u2", (128, 4 * 128)), ("qs", (128, 4 * 128)),
    ("c128", (128, 4 * 64)), ("ident", (128, 128)),
    ("crep", (128, T)), ("srep", (128, T)), ("dmask", (128, 4 * 512)),
    ("triu_incl", (128, 128)), ("triu_strict", (128, 128)),
    ("sel2", (128, 2)), ("sel2T", (2, 128)), ("evec128", (128, 8)),
    ("ones1_128", (1, 128)), ("ones1_64", (1, 64)),
]


def build_program(dbg=(), stop_after=None):
    nc = bass.Bass(trn_type="TRN2", target_bir_lowering=False)

    di = {}
    for n, s in IN_SPECS:
        di[n] = nc.dram_tensor(n, list(s), F32, kind="ExternalInput").ap()

    out = nc.dram_tensor("out", [TH, C], F32, kind="ExternalOutput").ap()

    sc = {}
    for name, shape in [
        ("h1C", (C, T)), ("x2", (T, C)), ("h2", (T, C)), ("h3C", (C, T)),
        ("q2r", (CL, T)), ("k2r", (CL, T)), ("v2a", (T, HL, 65)),
        ("att", (CL, T)), ("x3", (TH, C)),
        ("a1p", (T, C)), ("a2p", (T, C)),
        ("h2own", (TH, C)), ("x2own", (TH, C)),
        ("keepown", (TH, E)), ("lposown", (TH, E)), ("h4tok", (TH, C)),
        ("bkt", (E * 768 + 40, C)), ("eout", (E * 768 + 40, C)),
        ("cc_a1in", (T, C)), ("cc_a1outA", (8, 1024, C)),
        ("cc_a1outB", (8, 1024, C)), ("cc_a1outC", (8, 1024, C)),
        ("cc_a1outD", (8, 1024, C)),
        ("cc_a2in", (T, C)), ("cc_a2outA", (8, TH, C)),
        ("cc_a2outB", (8, TH, C)),
        ("cc_mkin", (TH, E)), ("cc_mkout", (8 * TH, E)),
    ]:
        sc[name] = nc.dram_tensor(name, list(shape), F32).ap()

    dbg_out = {}
    for name, shape in dbg:
        dbg_out[name] = nc.dram_tensor("dbg_" + name, list(shape), F32,
                                       kind="ExternalOutput").ap()

    stack = ExitStack()
    regions = []
    S = Sched(nc, stack)

    def close_all():
        for st in reversed(regions):
            st.close()
        stack.close()

    def sb(name, p, f, st=None):
        return (st or stack).enter_context(nc.sbuf_tensor(name, [p, f], F32))

    cb = {}
    for n, p, f in [
        ("ident", 128, 128), ("ones1_128", 1, 128), ("ones1_64", 1, 64),
        ("d0t", 128, HL * 128), ("u2", 128, 4 * 128), ("qs", 128, 4 * 128),
        ("c128", 128, 4 * 64), ("gnw128", 128, 4), ("gnb128", 128, 4),
        ("sel2", 128, 2), ("sel2T", 2, 128),
        ("triu_incl", 128, 128), ("triu_strict", 128, 128),
    ]:
        cb[n] = sb("cb_" + n, p, f)

    ps_big = stack.enter_context(nc.psum_tensor("ps_big", [128, 2048], F32))
    ps_mid = stack.enter_context(nc.psum_tensor("ps_mid", [128, 1024], F32))
    ps_a = stack.enter_context(nc.psum_tensor("ps_a", [128, 512], F32))
    ps_b = stack.enter_context(nc.psum_tensor("ps_b", [128, 512], F32))

    for n in cb:
        S.dma("sync", "c_ld", cb[n][:, :], di[n][:, :])

    lnrow = sb("lnrow", 1, C)

    def emit_bcast(dstbuf, srcname):
        S.wait("sync", "p0b_mm")
        S.dma("sync", "p0b_ld", lnrow[:, :], di[srcname][:, :])
        S.wait("tensor", "p0b_ld")
        S.wait("tensor", "c_ld")
        S.wait("tensor", "p0b_ev")
        for oh in range(2):
            S.inc("tensor", "p0b_mm", 1,
                  lambda e, oh=oh: e.matmul(
                      ps_mid[:, oh * 512:(oh + 1) * 512], cb["ones1_128"][:, :],
                      lnrow[:, oh * 512:(oh + 1) * 512], start=True, stop=True))
        S.wait("scalar", "p0b_mm")
        S.inc("scalar", "p0b_ev", 1,
              lambda e, d=dstbuf: e.copy(d[:, :], ps_mid[:, :]))

    st_p1 = ExitStack()
    regions.append(st_p1)
    xt = sb("p1_x", 128, 2 * C, st_p1)
    h1t = sb("p1_h1", 128, C, st_p1)
    sqd = sb("p1_sqd", 128, C, st_p1)
    lnb = sb("p1_lnb", 128, 8 * NT, st_p1)
    tpo = sb("p1_tpo", 128, 2 * 128, st_p1)

    def emit_ln_batched(pfx, n, load_fn, sums, sq, work):
        for i in range(n):
            par = i % 2
            ap = load_fn(i, par)
            S.wait("vector", pfx + "ld%d" % par)
            S.inc("vector", pfx + "rs", 1,
                  lambda e, ap=ap, i=i: e.reduce_sum(sums[:, i:i + 1], ap, axis=AX))
            S.wait("scalar", pfx + "ld%d" % par)
            S.inc("scalar", pfx + "sq", 1,
                  lambda e, ap=ap, i=i: e.activation(sqd[:, :], ap, AF.Square,
                                                     accum_out=sq[:, i:i + 1]))

    def emit_ln_finalize(pfx, n, sums, sq, negmean, rstd, work):
        S.wait("vector", pfx + "sq")
        S.inc("vector", pfx + "fz", 1,
              lambda e: e.tensor_scalar(negmean, sums, -1.0 / C, None, OP.mult))
        S.inc("vector", pfx + "fz", 1,
              lambda e: e.tensor_scalar(work[0], sq, 1.0 / C, None, OP.mult))
        S.inc("vector", pfx + "fz", 1,
              lambda e: e.tensor_tensor(work[1], negmean, negmean, OP.mult))
        S.inc("vector", pfx + "fz", 1,
              lambda e: e.tensor_tensor(work[1], work[0], work[1], OP.subtract))
        S.inc("vector", pfx + "fz", 1,
              lambda e: e.tensor_scalar(work[0], work[1], LN_EPS, None, OP.add))
        S.inc("vector", pfx + "fz", 1, lambda e: e.reciprocal(work[1], work[0]))
        S.wait("scalar", pfx + "fz")
        S.inc("scalar", pfx + "rst", 1,
              lambda e: e.activation(rstd, work[1], AF.Sqrt))

    def p1_load(i, par):
        S.wait_lag("sync", "p1rs", 1)
        S.wait_lag("sync", "p1sq", 1)
        S.dma("sync", "p1ld%d" % par, xt[:, par * C:(par + 1) * C],
              di["x"][i * 128:(i + 1) * 128, :])
        return xt[:, par * C:(par + 1) * C]

    S.wait("tensor", "c_ld")
    S.wait("vector", "c_ld")
    emit_ln_batched("p1", NT, p1_load,
                    lnb[:, 0:NT], lnb[:, NT:2 * NT], None)
    emit_ln_finalize("p1", NT, lnb[:, 0:NT], lnb[:, NT:2 * NT],
                     lnb[:, 2 * NT:3 * NT], lnb[:, 3 * NT:4 * NT],
                     (lnb[:, 4 * NT:5 * NT], lnb[:, 5 * NT:6 * NT]))
    S.wait("sync", "p1rs")
    S.wait("sync", "p1sq")
    for i in range(NT):
        par = i % 2
        S.wait_lag("sync", "p1ap", 1)
        S.dma("sync", "p1ld2%d" % par, xt[:, par * C:(par + 1) * C],
              di["x"][i * 128:(i + 1) * 128, :])
        S.wait("vector", "p1ld2%d" % par)
        S.wait("vector", "p1rst")
        S.wait("vector", "p1_mm")
        S.inc("vector", "p1ap", 1,
              lambda e, i=i, par=par: e.tensor_scalar(
                  h1t[:, :], xt[:, par * C:(par + 1) * C],
                  lnb[:, 2 * NT + i:2 * NT + i + 1],
                  lnb[:, 3 * NT + i:3 * NT + i + 1], OP.add, OP.mult))
        for ccj in range(8):
            tp = ccj % 2
            tps = ps_a if tp == 0 else ps_b
            S.wait("tensor", "p1ap")
            S.wait_lag("tensor", "p1_ev", 1)
            S.inc("tensor", "p1_mm", 1,
                  lambda e, ccj=ccj, tps=tps: e.transpose(
                      tps[:, 0:128],
                      h1t[:, ccj * 128:(ccj + 1) * 128], cb["ident"][:, :]))
            S.wait("scalar", "p1_mm")
            S.wait("scalar", "p1st%d" % tp)
            S.inc("scalar", "p1_ev", 1,
                  lambda e, tp=tp, tps=tps: e.copy(
                      tpo[:, tp * 128:(tp + 1) * 128], tps[:, 0:128]))
            S.wait("gpsimd", "p1_ev")
            S.dma("gpsimd", "p1st%d" % tp,
                  sc["h1C"][ccj * 128:(ccj + 1) * 128, i * 128:(i + 1) * 128],
                  tpo[:, tp * 128:(tp + 1) * 128])

    if "h1C" in dbg_out:
        S.wait("gpsimd", "p1st0")
        S.wait("gpsimd", "p1st1")
        for cc in range(8):
            S.dma("gpsimd", "dbg", dbg_out["h1C"][cc * 128:(cc + 1) * 128, :],
                  sc["h1C"][cc * 128:(cc + 1) * 128, :])

    if stop_after == "P1":
        with nc.Block() as blk:
            S.run(blk)
        close_all()
        return nc, dbg_out

    st_p1.close()
    S.join(("sync", "tensor", "vector", "scalar", "gpsimd"),
           ("p1_mm", "p1_ev", "p1st0", "p1st1", "ch_vector", "ch_scalar",
            "p1ld0", "p1ld1", "p1ld20", "p1ld21"))
    st_y = ExitStack()
    regions.append(st_y)
    yC = sb("yC", 128, 4 * T, st_y)
    state = sb("state", 128, 4 * 64, st_y)

    st_p23 = ExitStack()
    regions.append(st_p23)
    wq_rt = sb("wq_rt", 128, 8 * 128, st_p23)
    wk_rt = sb("wk_rt", 128, 8 * 128, st_p23)
    wkv_rt = sb("wkv_rt", 128, 8 * 256, st_p23)
    h1c_ch = sb("h1c_ch", 128, 2 * 8 * 512, st_p23)

    qd_p = sb("qd_p", 128, T, st_p23)
    kd_p = sb("kd_p", 128, T, st_p23)
    kt_p = sb("kt_p", 128, 32 * 128, st_p23)
    v_p = sb("v_p", 128, 32 * 128, st_p23)
    ploc = sb("ploc", 128, 2 * 128, st_p23)
    vs_t = sb("vs_t", 128, 128, st_p23)
    qs_t = sb("qs_t", 128, 128, st_p23)

    S.wait("gpsimd", "p1_ev")

    for rt in range(4):
        S.wait("sync", "p2mm")
        for cc in range(8):
            S.dma("sync", "p2w", wq_rt[:, cc * 128:(cc + 1) * 128],
                  di["wqC"][:, cc * 512 + rt * 128: cc * 512 + (rt + 1) * 128])
            S.dma("sync", "p2w", wk_rt[:, cc * 128:(cc + 1) * 128],
                  di["wkC"][:, cc * 512 + rt * 128: cc * 512 + (rt + 1) * 128])
            S.dma("sync", "p2w", wkv_rt[:, cc * 256:cc * 256 + 128],
                  di["wkC"][:, cc * 512 + rt * 128: cc * 512 + (rt + 1) * 128])
            S.dma("sync", "p2w", wkv_rt[:, cc * 256 + 128:(cc + 1) * 256],
                  di["wvC"][:, cc * 512 + rt * 128: cc * 512 + (rt + 1) * 128])

        S.wait("scalar", "p3vs")
        S.wait("scalar", "p3qs")

        for tc in range(NCH):
            tp = tc % 2
            S.wait("sync", "p2mm", target=max(0, S.cnt.get("p2mm", 0) - 48))
            if rt == 0 and tc == 0:
                S.wait("sync", "p1st0")
                S.wait("sync", "p1st1")
            for cc in range(8):
                S.dma("sync", "p2h%d" % tp,
                      h1c_ch[:, tp * 4096 + cc * 512:tp * 4096 + (cc + 1) * 512],
                      sc["h1C"][cc * 128:(cc + 1) * 128, tc * 512:(tc + 1) * 512])
            S.wait("tensor", "p2h%d" % tp)
            S.wait("tensor", "p2w")
            S.wait("tensor", "p2ev")
            S.wait("tensor", "p3pl")
            S.wait("tensor", "p3ev")
            S.wait("tensor", "p3st")
            for pj, wbuf in [(0, wq_rt), (1, wk_rt)]:
                for cc in range(8):
                    S.inc("tensor", "p2mm", 1,
                          lambda e, w=wbuf, cc=cc, pj=pj, tp=tp: e.matmul(
                              ps_big[:, pj * 512:(pj + 1) * 512],
                              w[:, cc * 128:(cc + 1) * 128],
                              h1c_ch[:, tp * 4096 + cc * 512:
                                     tp * 4096 + (cc + 1) * 512],
                              start=(cc == 0), stop=(cc == 7)))
            ktv_ps = [ps_mid[:, 0:256], ps_mid[:, 512:768],
                      ps_a[:, 0:256], ps_b[:, 0:256]]
            for ts in range(4):
                for cc in range(8):
                    S.inc("tensor", "p2mm", 1,
                          lambda e, ts=ts, cc=cc, tp=tp: e.matmul(
                              ktv_ps[ts],
                              h1c_ch[:, tp * 4096 + cc * 512 + ts * 128:
                                     tp * 4096 + cc * 512 + (ts + 1) * 128],
                              wkv_rt[:, cc * 256:(cc + 1) * 256],
                              start=(cc == 0), stop=(cc == 7)))
            S.wait("scalar", "p2mm")
            S.inc("scalar", "p2ev", 1,
                  lambda e, tc=tc: e.copy(qd_p[:, tc * 512:(tc + 1) * 512],
                                          ps_big[:, 0:512]))
            S.inc("scalar", "p2ev", 1,
                  lambda e, tc=tc: e.copy(kd_p[:, tc * 512:(tc + 1) * 512],
                                          ps_big[:, 512:1024]))
            for ts in range(4):
                j = tc * 4 + ts
                S.inc("scalar", "p2ev", 1,
                      lambda e, ts=ts, j=j: e.copy(
                          kt_p[:, j * 128:(j + 1) * 128],
                          ktv_ps[ts][:, 0:128]))
                S.inc("scalar", "p2ev", 1,
                      lambda e, ts=ts, j=j: e.copy(
                          v_p[:, j * 128:(j + 1) * 128],
                          ktv_ps[ts][:, 128:256]))
            S.wait("tensor", "p2ev")

        if rt == 0 and "q1d_rt0" in dbg_out:
            S.wait("gpsimd", "p2ev")
            S.dma("gpsimd", "dbg", dbg_out["q1d_rt0"][:, :], qd_p[:, :])
            S.dma("gpsimd", "dbg", dbg_out["k1d_rt0"][:, :], kd_p[:, :])
            S.dma("gpsimd", "dbg", dbg_out["v1_rt0"][:, :], v_p[:, :])
            S.dma("gpsimd", "dbg", dbg_out["kt_rt0"][:, :], kt_p[:, :])

        S.wait("vector", "p2ev")
        S.inc("vector", "p3st", 1,
              lambda e, rt=rt: e.memset(state[:, rt * 64:(rt + 1) * 64], 0.0))

        for j in range(NT):
            jb = (j % 2) * 1024
            S.wait_lag("tensor", "p3pl", 2)
            for hl in (0, 1):
                S.inc("tensor", "p3mm1", 1,
                      lambda e, hl=hl, j=j, jb=jb: e.matmul(
                          ps_big[:, jb + hl * 512:jb + hl * 512 + 128],
                          kd_p[hl * 64:(hl + 1) * 64, j * 128:(j + 1) * 128],
                          qd_p[hl * 64:(hl + 1) * 64, j * 128:(j + 1) * 128],
                          start=True, stop=True))
            S.wait("vector", "p3mm2")
            S.wait("vector", "p3mm1")
            for hl in (0, 1):
                h = 2 * rt + hl
                S.inc("vector", "p3pl", 1,
                      lambda e, hl=hl, h=h, jb=jb: e.tensor_tensor(
                          ploc[:, hl * 128:(hl + 1) * 128],
                          ps_big[:, jb + hl * 512:jb + hl * 512 + 128],
                          cb["d0t"][:, h * 128:(h + 1) * 128], OP.mult))
            S.inc("vector", "p3qs", 1,
                  lambda e, j=j, rt=rt: e.tensor_tensor(
                      qs_t[:, :], qd_p[:, j * 128:(j + 1) * 128],
                      cb["qs"][:, rt * 128:(rt + 1) * 128], OP.mult))
            S.wait("vector", "p3mma")
            S.inc("vector", "p3vs", 1,
                  lambda e, j=j, rt=rt: e.tensor_tensor(
                      vs_t[:, :], v_p[:, j * 128:(j + 1) * 128],
                      cb["u2"][:, rt * 128:(rt + 1) * 128], OP.mult))
            S.wait("tensor", "p3pl")
            S.wait("tensor", "p3qs")
            S.wait("tensor", "p3ev")
            S.wait("tensor", "p3st")
            for hl in range(2):
                S.inc("tensor", "p3mm2", 1,
                      lambda e, hl=hl, j=j: e.matmul(
                          ps_mid[hl * 64:(hl + 1) * 64, 0:128],
                          v_p[:, j * 128 + hl * 64: j * 128 + (hl + 1) * 64],
                          ploc[:, hl * 128:(hl + 1) * 128],
                          start=True, stop=False))
                S.inc("tensor", "p3mm2", 1,
                      lambda e, hl=hl, rt=rt: e.matmul(
                          ps_mid[hl * 64:(hl + 1) * 64, 0:128],
                          state[hl * 64:(hl + 1) * 64, rt * 64:(rt + 1) * 64],
                          qs_t[hl * 64:(hl + 1) * 64, :],
                          start=False, stop=True))
            S.wait("tensor", "p3vs")
            for hl in range(2):
                S.inc("tensor", "p3mma", 1,
                      lambda e, hl=hl, j=j: e.matmul(
                          ps_mid[hl * 64:(hl + 1) * 64, 512:576],
                          kt_p[:, j * 128 + hl * 64: j * 128 + (hl + 1) * 64],
                          vs_t[:, hl * 64:(hl + 1) * 64],
                          start=True, stop=True))
            S.wait("scalar", "p3mm2")
            S.inc("scalar", "p3ev", 1,
                  lambda e, rt=rt, j=j: e.copy(
                      yC[:, rt * T + j * 128:rt * T + (j + 1) * 128],
                      ps_mid[:, 0:128]))
            S.wait("vector", "p3mma")
            S.wait("vector", "p3mm2")
            S.inc("vector", "p3st", 1,
                  lambda e, rt=rt: e.tensor_tensor(
                      state[:, rt * 64:(rt + 1) * 64],
                      state[:, rt * 64:(rt + 1) * 64],
                      cb["c128"][:, rt * 64:(rt + 1) * 64], OP.mult))
            S.inc("vector", "p3st", 1,
                  lambda e, rt=rt: e.tensor_tensor(
                      state[:, rt * 64:(rt + 1) * 64],
                      state[:, rt * 64:(rt + 1) * 64],
                      ps_mid[:, 512:576], OP.add))

    if "yC" in dbg_out:
        S.wait("gpsimd", "p3ev")
        for rt in range(4):
            S.dma("gpsimd", "dbg", dbg_out["yC"][rt * 128:(rt + 1) * 128, :],
                  yC[:, rt * T:(rt + 1) * T])

    if stop_after == "P3":
        with nc.Block() as blk:
            S.run(blk)
        close_all()
        return nc, dbg_out

    st_p23.close()
    S.join(("sync", "tensor", "vector", "scalar", "gpsimd"),
           ("p3mm1", "p3mm2", "p3mma", "ch_vector", "ch_scalar",
            "p2w", "p2h", "p1_ld", "p1_stst"))
    st_p4 = ExitStack()
    regions.append(st_p4)
    ysq = sb("ysq", 128, T, st_p4)
    rowred = sb("rowred", 128, 2, st_p4)
    p4s = sb("p4s", 2, 32, st_p4)
    mean128 = sb("mean128", 128, 4, st_p4)
    rstd128 = sb("rstd128", 128, 4, st_p4)
    ab = sb("ab", 128, 8, st_p4)
    woC_sb = sb("woC_sb", 128, 4 * 1024, st_p4)
    a1tile = sb("a1tile", 128, 1024, st_p4)

    for cc in range(4):
        S.dma("sync", "p4w", woC_sb[:, cc * 1024:(cc + 1) * 1024],
              di["woC"][:, cc * 1024:(cc + 1) * 1024])

    S.wait("scalar", "p3ev")
    S.wait("vector", "p3ev")
    for rt in range(4):
        S.wait("scalar", "p4red")
        S.inc("scalar", "p4sq", 1,
              lambda e, rt=rt: e.activation(ysq[:, :], yC[:, rt * T:(rt + 1) * T],
                                            AF.Square))
        S.wait("vector", "p4smm")
        S.inc("vector", "p4red", 1,
              lambda e, rt=rt: e.reduce_sum(rowred[:, 0:1],
                                            yC[:, rt * T:(rt + 1) * T], axis=AX))
        S.wait("vector", "p4sq")
        S.inc("vector", "p4red", 1,
              lambda e: e.reduce_sum(rowred[:, 1:2], ysq[:, :], axis=AX))
        S.wait("tensor", "p4red")
        S.wait("tensor", "p4sev")
        S.inc("tensor", "p4smm", 1,
              lambda e, rt=rt: e.matmul(ps_a[0:2, 0:2], cb["sel2"][:, :],
                                        rowred[:, 0:2], start=True, stop=True))
        S.wait("scalar", "p4smm")
        S.inc("scalar", "p4sev", 1,
              lambda e, rt=rt: e.copy(p4s[:, rt:rt + 1], ps_a[0:2, 0:1]))
        S.inc("scalar", "p4sev", 1,
              lambda e, rt=rt: e.copy(p4s[:, 4 + rt:5 + rt], ps_a[0:2, 1:2]))
    nfac = 1.0 / (T * 64)
    S.wait("vector", "p4sev")
    S.inc("vector", "p4m", 1,
          lambda e: e.tensor_scalar(p4s[:, 8:12], p4s[:, 0:4], nfac, None, OP.mult))
    S.inc("vector", "p4m", 1,
          lambda e: e.tensor_scalar(p4s[:, 12:16], p4s[:, 4:8], nfac, None, OP.mult))
    S.inc("vector", "p4m", 1,
          lambda e: e.tensor_tensor(p4s[:, 16:20], p4s[:, 8:12], p4s[:, 8:12],
                                    OP.mult))
    S.inc("vector", "p4m", 1,
          lambda e: e.tensor_tensor(p4s[:, 16:20], p4s[:, 12:16], p4s[:, 16:20],
                                    OP.subtract))
    S.inc("vector", "p4m", 1,
          lambda e: e.tensor_scalar(p4s[:, 20:24], p4s[:, 16:20], LN_EPS, None,
                                    OP.add))
    S.inc("vector", "p4m", 1, lambda e: e.reciprocal(p4s[:, 24:28], p4s[:, 20:24]))
    S.wait("scalar", "p4m")
    S.inc("scalar", "p4rs", 1,
          lambda e: e.activation(p4s[:, 28:32], p4s[:, 24:28], AF.Sqrt))
    S.wait("tensor", "p4rs")
    S.wait("tensor", "p4sev")
    S.inc("tensor", "p4bmm", 1,
          lambda e: e.matmul(ps_b[:, 0:4], cb["sel2T"][:, :], p4s[:, 8:12],
                             start=True, stop=True))
    S.inc("tensor", "p4bmm", 1,
          lambda e: e.matmul(ps_mid[:, 0:4], cb["sel2T"][:, :], p4s[:, 28:32],
                             start=True, stop=True))
    S.wait("scalar", "p4bmm")
    S.inc("scalar", "p4bev", 1, lambda e: e.copy(mean128[:, :], ps_b[:, 0:4]))
    S.inc("scalar", "p4bev", 1, lambda e: e.copy(rstd128[:, :], ps_mid[:, 0:4]))
    S.wait("vector", "p4bev")
    S.inc("vector", "p4ab", 1,
          lambda e: e.tensor_tensor(ab[:, 0:4], rstd128[:, :], cb["gnw128"][:, :],
                                    OP.mult))
    S.inc("vector", "p4ab", 1,
          lambda e: e.tensor_tensor(ab[:, 4:8], mean128[:, :], ab[:, 0:4], OP.mult))
    S.inc("vector", "p4ab", 1,
          lambda e: e.tensor_tensor(ab[:, 4:8], cb["gnb128"][:, :], ab[:, 4:8],
                                    OP.subtract))
    p4st_q = [0, 0, 0, 0]
    ygt = sb("ygt", 128, 2 * 128, st_p4)
    for tt in range(NT):
        S.wait("tensor", "p4w")
        S.wait("tensor", "p4ev")
        for cc in range(4):
            par = cc % 2
            S.wait("vector", "p4mm")
            S.wait("vector", "p4ab")
            S.inc("vector", "p4ap", 1,
                  lambda e, tt=tt, cc=cc, par=par: e.tensor_scalar(
                      ygt[:, par * 128:(par + 1) * 128],
                      yC[:, cc * T + tt * 128:cc * T + (tt + 1) * 128],
                      ab[:, cc:cc + 1], ab[:, 4 + cc:5 + cc],
                      OP.mult, OP.add))
            S.wait("tensor", "p4ap")
            for oh in range(2):
                S.inc("tensor", "p4mm", 1,
                      lambda e, oh=oh, cc=cc, par=par: e.matmul(
                          ps_mid[:, oh * 512:(oh + 1) * 512],
                          ygt[:, par * 128:(par + 1) * 128],
                          woC_sb[:, cc * 1024 + oh * 512:cc * 1024 + (oh + 1) * 512],
                          start=(cc == 0), stop=(cc == 3)))
        S.wait("scalar", "p4mm")
        S.wait("scalar", "p4st")
        S.inc("scalar", "p4ev", 1,
              lambda e: e.copy(a1tile[:, :], ps_mid[:, :]))
        S.wait("gpsimd", "p4ev")
        S.dma("gpsimd", "p4st", sc["cc_a1in"][tt * 128:(tt + 1) * 128, :],
              a1tile[:, :])
        if tt % 8 == 7:
            p4st_q[tt // 8] = S.cnt["p4st"]

    if "attn1p" in dbg_out:
        S.wait("gpsimd", "p4st")
        S.dma("gpsimd", "dbg", dbg_out["attn1p"][:, :], sc["cc_a1in"][:, :])

    if stop_after == "P4":
        with nc.Block() as blk:
            S.run(blk)
        close_all()
        return nc, dbg_out

    st_p4.close()
    st_y.close()
    S.join(ENGS, ("p4mm", "p4red", "p4sq", "ch_vector", "ch_scalar"))
    gst = {}

    def pid_reg(e):
        if "pid" not in gst:
            gst["pid"] = e.to_reg(e.partition_id())
        return gst["pid"]

    def partner_copy(ccname, dsts_srcs_fn):
        pairs0 = dsts_srcs_fn(0)
        n = len(pairs0)
        S.sem(ccname)
        S.cnt[ccname] += 16 * n
        S.names[ccname] = (ccname, S.cnt[ccname])
        h = S.h[ccname]

        def go(e, fn=dsts_srcs_fn, h=h):
            reg = pid_reg(e)
            for j in range(8):
                with e.If_eq(reg, j):
                    for dst, srcap in fn(j):
                        e.dma_start(out=dst, in_=srcap).then_inc(h, 16)
        S.emit("gpsimd", go)

    for qq, onm in enumerate(("A", "B", "C", "D")):
        S.wait("gpsimd", "p4st", target=p4st_q[qq])
        S.inc("gpsimd", "p5cc" + onm, 1,
              lambda e, qq=qq, onm=onm: e.collective_compute(
                  "AllGather", OP.bypass, replica_groups=[list(range(8))],
                  ins=[sc["cc_a1in"][qq * 1024:(qq + 1) * 1024, :].opt()],
                  outs=[sc["cc_a1out" + onm][:, :, :].opt()]))
        S.wait("gpsimd", "p5cc" + onm)
        partner_copy("p5cp" + onm,
                     lambda j, qq=qq, onm=onm: [
                         (sc["a1p"][qq * 1024:(qq + 1) * 1024, :],
                          sc["cc_a1out" + onm][j ^ 1])])

    st_p5 = ExitStack()
    regions.append(st_p5)
    g1bc = sb("g1bc", 128, C, st_p5)
    b1bc = sb("b1bc", 128, C, st_p5)
    p5a = sb("p5_a", 128, 2 * C, st_p5)
    p5b = sb("p5_b", 128, 2 * C, st_p5)
    p5c = sb("p5_c", 128, 2 * C, st_p5)
    p5d = sb("p5_d", 128, 2 * C, st_p5)
    p5e = sb("p5_e", 128, 2 * C, st_p5)
    sq5 = sb("p5_sqd", 128, C, st_p5)
    ln5 = sb("p5_lnb", 128, 12 * NT, st_p5)
    tp5 = sb("p5_tp", 128, 2 * 128, st_p5)

    emit_bcast(g1bc, "ln1g")
    emit_bcast(b1bc, "ln1b")

    def p5_finalize(pfx, sums, sq, negmean, rstd, w0, w1):
        S.wait("vector", pfx + "sq")
        S.inc("vector", pfx + "fz", 1,
              lambda e: e.tensor_scalar(negmean, sums, -1.0 / C, None, OP.mult))
        S.inc("vector", pfx + "fz", 1,
              lambda e: e.tensor_scalar(w0, sq, 1.0 / C, None, OP.mult))
        S.inc("vector", pfx + "fz", 1,
              lambda e: e.tensor_tensor(w1, negmean, negmean, OP.mult))
        S.inc("vector", pfx + "fz", 1,
              lambda e: e.tensor_tensor(w1, w0, w1, OP.subtract))
        S.inc("vector", pfx + "fz", 1,
              lambda e: e.tensor_scalar(w0, w1, LN_EPS, None, OP.add))
        S.inc("vector", pfx + "fz", 1, lambda e: e.reciprocal(w1, w0))
        S.wait("scalar", pfx + "fz")
        S.inc("scalar", pfx + "rst", 1, lambda e: e.activation(rstd, w1, AF.Sqrt))

    for i in range(NT):
        par = i % 2
        S.wait("sync", "p5cp" + "ABCD"[i // 8])
        S.wait_lag("sync", "p5tt", 2)
        S.dma("sync", "p5ld%d" % par, p5a[:, par * C:(par + 1) * C],
              di["x"][i * 128:(i + 1) * 128, :])
        S.dma("sync", "p5ld%d" % par, p5b[:, par * C:(par + 1) * C],
              sc["cc_a1in"][i * 128:(i + 1) * 128, :])
        S.dma("sync", "p5ld%d" % par, p5c[:, par * C:(par + 1) * C],
              sc["a1p"][i * 128:(i + 1) * 128, :])
        S.wait("vector", "p5ld%d" % par)
        S.wait("vector", "p5x2st%d" % par)
        S.wait_lag("vector", "p5xsq", 1)
        S.inc("vector", "p5tt", 1,
              lambda e, par=par: e.tensor_tensor(
                  p5d[:, par * C:(par + 1) * C], p5b[:, par * C:(par + 1) * C],
                  p5c[:, par * C:(par + 1) * C], OP.add))
        S.inc("vector", "p5tt", 1,
              lambda e, par=par: e.tensor_tensor(
                  p5d[:, par * C:(par + 1) * C], p5a[:, par * C:(par + 1) * C],
                  p5d[:, par * C:(par + 1) * C], OP.add))
        S.inc("vector", "p5xrs", 1,
              lambda e, par=par, i=i: e.reduce_sum(
                  ln5[:, i:i + 1], p5d[:, par * C:(par + 1) * C], axis=AX))
        S.wait("scalar", "p5tt")
        S.inc("scalar", "p5xsq", 1,
              lambda e, par=par, i=i: e.activation(
                  sq5[:, :], p5d[:, par * C:(par + 1) * C], AF.Square,
                  accum_out=ln5[:, NT + i:NT + i + 1]))
        S.wait("gpsimd", "p5tt")
        S.dma("gpsimd", "p5x2st%d" % par, sc["x2"][i * 128:(i + 1) * 128, :],
              p5d[:, par * C:(par + 1) * C])
    p5_finalize("p5x", ln5[:, 0:NT], ln5[:, NT:2 * NT], ln5[:, 2 * NT:3 * NT],
                ln5[:, 3 * NT:4 * NT], ln5[:, 8 * NT:9 * NT],
                ln5[:, 9 * NT:10 * NT])
    S.wait("sync", "p5tt")
    S.wait("sync", "p5xrs")
    S.wait("sync", "p5xsq")
    for i in range(NT):
        par = i % 2
        S.wait_lag("sync", "p5ap", 1)
        S.wait("sync", "p5x2st%d" % par)
        S.dma("sync", "p5ld2%d" % par, p5d[:, par * C:(par + 1) * C],
              sc["x2"][i * 128:(i + 1) * 128, :])
        S.wait("vector", "p5ld2%d" % par)
        S.wait("vector", "p5xrst")
        S.wait("vector", "p5h2st%d" % par)
        S.wait_lag("vector", "p5ysq", 1)
        S.inc("vector", "p5ap", 1,
              lambda e, i=i, par=par: e.tensor_scalar(
                  p5e[:, par * C:(par + 1) * C], p5d[:, par * C:(par + 1) * C],
                  ln5[:, 2 * NT + i:2 * NT + i + 1],
                  ln5[:, 3 * NT + i:3 * NT + i + 1], OP.add, OP.mult))
        S.inc("vector", "p5yrs", 1,
              lambda e, par=par, i=i: e.reduce_sum(
                  ln5[:, 4 * NT + i:4 * NT + i + 1],
                  p5e[:, par * C:(par + 1) * C], axis=AX))
        S.wait("scalar", "p5ap")
        S.inc("scalar", "p5ysq", 1,
              lambda e, par=par, i=i: e.activation(
                  sq5[:, :], p5e[:, par * C:(par + 1) * C], AF.Square,
                  accum_out=ln5[:, 5 * NT + i:5 * NT + i + 1]))
        S.wait("gpsimd", "p5ap")
        S.dma("gpsimd", "p5h2st%d" % par, sc["h2"][i * 128:(i + 1) * 128, :],
              p5e[:, par * C:(par + 1) * C])
    p5_finalize("p5y", ln5[:, 4 * NT:5 * NT], ln5[:, 5 * NT:6 * NT],
                ln5[:, 6 * NT:7 * NT], ln5[:, 7 * NT:8 * NT],
                ln5[:, 8 * NT:9 * NT], ln5[:, 9 * NT:10 * NT])
    S.wait("sync", "p5ap")
    S.wait("sync", "p5yrs")
    S.wait("sync", "p5ysq")
    for i in range(NT):
        par = i % 2
        S.wait_lag("sync", "p5af", 1)
        S.wait("sync", "p5h2st%d" % par)
        S.dma("sync", "p5ld3%d" % par, p5d[:, par * C:(par + 1) * C],
              sc["h2"][i * 128:(i + 1) * 128, :])
        S.wait("vector", "p5ld3%d" % par)
        S.wait("vector", "p5yrst")
        S.wait("vector", "p5_mm")
        S.wait("vector", "p0b_ev")
        S.inc("vector", "p5af", 1,
              lambda e, i=i, par=par: e.tensor_scalar(
                  p5e[:, par * C:(par + 1) * C], p5d[:, par * C:(par + 1) * C],
                  ln5[:, 6 * NT + i:6 * NT + i + 1],
                  ln5[:, 7 * NT + i:7 * NT + i + 1], OP.add, OP.mult))
        S.inc("vector", "p5af", 1,
              lambda e, par=par: e.tensor_tensor(
                  p5e[:, par * C:(par + 1) * C], p5e[:, par * C:(par + 1) * C],
                  g1bc[:, :], OP.mult))
        S.inc("vector", "p5af", 1,
              lambda e, par=par: e.tensor_tensor(
                  p5e[:, par * C:(par + 1) * C], p5e[:, par * C:(par + 1) * C],
                  b1bc[:, :], OP.add))
        for ccj in range(8):
            tp = ccj % 2
            tps = ps_a if tp == 0 else ps_b
            S.wait("tensor", "p5af")
            S.wait_lag("tensor", "p5_ev", 1)
            S.inc("tensor", "p5_mm", 1,
                  lambda e, ccj=ccj, tps=tps, par=par: e.transpose(
                      tps[:, 0:128],
                      p5e[:, par * C + ccj * 128:par * C + (ccj + 1) * 128],
                      cb["ident"][:, :]))
            S.wait("scalar", "p5_mm")
            S.wait("scalar", "p5st%d" % tp)
            S.inc("scalar", "p5_ev", 1,
                  lambda e, tp=tp, tps=tps: e.copy(
                      tp5[:, tp * 128:(tp + 1) * 128], tps[:, 0:128]))
            S.wait("gpsimd", "p5_ev")
            S.dma("gpsimd", "p5st%d" % tp,
                  sc["h3C"][ccj * 128:(ccj + 1) * 128, i * 128:(i + 1) * 128],
                  tp5[:, tp * 128:(tp + 1) * 128])

    if "x2" in dbg_out:
        S.wait("gpsimd", "p5x2st0")
        S.wait("gpsimd", "p5x2st1")
        S.wait("gpsimd", "p5h2st0")
        S.wait("gpsimd", "p5h2st1")
        S.dma("gpsimd", "dbg", dbg_out["x2"][:, :], sc["x2"][:, :])
        S.dma("gpsimd", "dbg", dbg_out["h2"][:, :], sc["h2"][:, :])
    if "h3C" in dbg_out:
        S.wait("gpsimd", "p5st0")
        S.wait("gpsimd", "p5st1")
        for cc in range(8):
            S.dma("gpsimd", "dbg", dbg_out["h3C"][cc * 128:(cc + 1) * 128, :],
                  sc["h3C"][cc * 128:(cc + 1) * 128, :])

    if stop_after == "P5":
        with nc.Block() as blk:
            S.run(blk)
        close_all()
        return nc, dbg_out

    st_p5.close()
    S.join(ENGS, ("p5_mm", "p5_ev", "p5st0", "p5st1", "p5ld0", "p5ld1",
                  "p5ld20", "p5ld21", "p5ld30", "p5ld31",
                  "p5x2st0", "p5x2st1", "p5h2st0", "p5h2st1",
                  "ch_vector", "ch_scalar"))
    st_p6 = ExitStack()
    regions.append(st_p6)
    crep_sb = sb("crep_sb", 128, T, st_p6)
    srep_sb = sb("srep_sb", 128, T, st_p6)
    wqkv_sb = sb("wqkv_sb", 128, 8 * 1536, st_p6)
    h3c_ch6 = sb("h3c_ch6", 128, 2 * 8 * 512, st_p6)
    qraw = sb("qraw", 128, 4 * 512, st_p6)
    rt1 = sb("rt1", 128, 512, st_p6)
    rt2 = sb("rt2", 128, 512, st_p6)
    ro1 = sb("ro1", 128, 512, st_p6)
    v2at = sb("v2at6", 128, 8 * 65, st_p6)

    S.dma("sync", "p6c", crep_sb[:, :], di["crep"][:, :])
    S.dma("sync", "p6c", srep_sb[:, :], di["srep"][:, :])
    for cc in range(8):
        S.dma("sync", "p6c", wqkv_sb[:, cc * 1536:(cc + 1) * 1536],
              di["WqkvC"][:, cc * 1536:(cc + 1) * 1536])
    S.inc("vector", "p6one", 1,
          lambda e: e.memset(v2at[:, :].rearrange("a (h c) -> a h c", c=65)[:, :, 64:65], 1.0))

    def rope_phase(base, dst):
        for tc in range(NCH):
            pass

    for tc in range(NCH):
        t6 = tc % 2
        S.wait("sync", "p6mm", target=max(0, S.cnt.get("p6mm", 0) - 96))
        for cc in range(8):
            S.dma("sync", "p6h%d" % t6,
                  h3c_ch6[:, t6 * 4096 + cc * 512:t6 * 4096 + (cc + 1) * 512],
                  sc["h3C"][cc * 128:(cc + 1) * 128, tc * 512:(tc + 1) * 512])
        for base, dst in ((0, sc["q2r"]), (512, sc["k2r"])):
            S.wait("tensor", "p6h%d" % t6)
            S.wait("tensor", "p6c")
            S.wait("tensor", "p6ev")
            for rb in range(4):
                for cc in range(8):
                    S.inc("tensor", "p6mm", 1,
                          lambda e, rb=rb, cc=cc, base=base, t6=t6: e.matmul(
                              ps_big[:, rb * 512:(rb + 1) * 512],
                              wqkv_sb[:, cc * 1536 + base + rb * 128:
                                      cc * 1536 + base + (rb + 1) * 128],
                              h3c_ch6[:, t6 * 4096 + cc * 512:
                                      t6 * 4096 + (cc + 1) * 512],
                              start=(cc == 0), stop=(cc == 7)))
            S.wait("scalar", "p6mm")
            S.wait("scalar", "p6rp")
            for rb in range(4):
                S.inc("scalar", "p6ev", 1,
                      lambda e, rb=rb: e.copy(qraw[:, rb * 512:(rb + 1) * 512],
                                              ps_big[:, rb * 512:(rb + 1) * 512]))
            for p in range(2):
                cs = crep_sb[:, tc * 512:(tc + 1) * 512]
                sn = srep_sb[:, tc * 512:(tc + 1) * 512]
                S.wait("vector", "p6ev")
                S.wait("vector", "p6vst")
                S.inc("vector", "p6rp", 1,
                      lambda e, p=p, cs=cs: e.tensor_tensor(
                          rt1[:, :], qraw[:, p * 512:(p + 1) * 512], cs, OP.mult))
                S.inc("vector", "p6rp", 1,
                      lambda e, p=p, sn=sn: e.tensor_tensor(
                          rt2[:, :], qraw[:, (p + 2) * 512:(p + 3) * 512], sn,
                          OP.mult))
                S.wait("vector", "p6vst")
                S.inc("vector", "p6rp", 1,
                      lambda e: e.tensor_tensor(ro1[:, :], rt1[:, :], rt2[:, :],
                                                OP.subtract))
                S.wait("gpsimd", "p6rp")
                for hi in range(4):
                    S.dma("gpsimd", "p6vst",
                          dst[(p * 4 + hi) * 64 + 0:(p * 4 + hi) * 64 + 32,
                              tc * 512:(tc + 1) * 512],
                          ro1[hi * 32:(hi + 1) * 32, :])
                S.inc("vector", "p6rp", 1,
                      lambda e, p=p, sn=sn: e.tensor_tensor(
                          rt1[:, :], qraw[:, p * 512:(p + 1) * 512], sn, OP.mult))
                S.inc("vector", "p6rp", 1,
                      lambda e, p=p, cs=cs: e.tensor_tensor(
                          rt2[:, :], qraw[:, (p + 2) * 512:(p + 3) * 512], cs,
                          OP.mult))
                S.wait("vector", "p6vst")
                S.inc("vector", "p6rp", 1,
                      lambda e: e.tensor_tensor(ro1[:, :], rt1[:, :], rt2[:, :],
                                                OP.add))
                S.wait("gpsimd", "p6rp")
                for hi in range(4):
                    S.dma("gpsimd", "p6vst",
                          dst[(p * 4 + hi) * 64 + 32:(p * 4 + hi) * 64 + 64,
                              tc * 512:(tc + 1) * 512],
                          ro1[hi * 32:(hi + 1) * 32, :])
        S.wait("tensor", "p6rp")
        S.wait("tensor", "p6ev")
        for ts in range(4):
            for cc in range(8):
                S.inc("tensor", "p6mm", 1,
                      lambda e, ts=ts, cc=cc, t6=t6: e.matmul(
                          ps_big[:, ts * 512:(ts + 1) * 512],
                          h3c_ch6[:, t6 * 4096 + cc * 512 + ts * 128:
                                  t6 * 4096 + cc * 512 + (ts + 1) * 128],
                          wqkv_sb[:, cc * 1536 + 1024:cc * 1536 + 1536],
                          start=(cc == 0), stop=(cc == 7)))
        for ts in range(4):
            S.wait("scalar", "p6mm")
            S.wait("scalar", "p6vst")
            S.wait("scalar", "p6one")
            S.inc("scalar", "p6ev", 1,
                  lambda e, ts=ts: e.copy(
                      v2at[:, :].rearrange("a (h c) -> a h c", c=65)[:, :, 0:64],
                      ps_big[:, ts * 512:(ts + 1) * 512].rearrange(
                          "a (h c) -> a h c", c=64)))
            S.wait("gpsimd", "p6ev")
            S.dma("gpsimd", "p6vst",
                  sc["v2a"][tc * 512 + ts * 128:tc * 512 + (ts + 1) * 128, :, :],
                  v2at[:, :].rearrange("a (h c) -> a h c", c=65))

    if "q2r" in dbg_out:
        S.wait("gpsimd", "p6vst")
        for rr in range(4):
            S.dma("gpsimd", "dbg", dbg_out["q2r"][rr * 128:(rr + 1) * 128, :],
                  sc["q2r"][rr * 128:(rr + 1) * 128, :])
            S.dma("gpsimd", "dbg", dbg_out["k2r"][rr * 128:(rr + 1) * 128, :],
                  sc["k2r"][rr * 128:(rr + 1) * 128, :])
        S.dma("gpsimd", "dbg", dbg_out["v2a"][:, :],
              sc["v2a"][:, :, :].rearrange("a h c -> a (h c)"))

    if stop_after == "P6":
        with nc.Block() as blk:
            S.run(blk)
        close_all()
        return nc, dbg_out

    st_p6.close()
    S.join(ENGS, ("p6mm", "p6ev", "p6rp", "p6vst", "p6h", "p6c",
                  "ch_vector", "ch_scalar"))
    st_p7 = ExitStack()
    regions.append(st_p7)
    dmask_sb = sb("dmask_sb", 128, 4 * 512, st_p7)
    ebias = sb("ebias", 128, 1, st_p7)
    wproj_sb = sb("wproj_sb", 128, 4 * 1024, st_p7)
    q2h = sb("q2h", 128, T, st_p7)
    k2h = sb("k2h", 128, T, st_p7)
    v2h = sb("v2h", 128, 2 * 32 * 65, st_p7)
    pexp = sb("pexp", 128, 2048, st_p7)
    rrow = sb("rrow", 1, 512, st_p7)
    rbc = sb("rbc", 64, 512, st_p7)
    rbc2 = sb("rbc2", 64, 512, st_p7)
    attd = sb("attd", 64, 512, st_p7)
    atile = sb("atile", 128, 2 * 128, st_p7)
    a2tile = sb("a2tile", 128, 1024, st_p7)

    S.dma("sync", "p7c", dmask_sb[:, :], di["dmask"][:, :])
    S.inc("vector", "p7eb", 1, lambda e: e.memset(ebias[:, :], -EXPC))
    for cc in range(4):
        S.dma("sync", "p7c", wproj_sb[:, cc * 1024:(cc + 1) * 1024],
              di["WprojR"][:, cc * 1024:(cc + 1) * 1024])

    for h in range(HL):
        hp = h % 2
        nmm1 = 2 * sum(range(2, 2 * NCH + 2, 2))
        S.wait("sync", "p7mm1",
               target=max(0, S.cnt.get("p7mm1", 0) - nmm1))
        S.wait("sync", "p7mm2",
               target=max(0, S.cnt.get("p7mm2", 0) - nmm1))
        S.dma("sync", "p7ldh%d" % hp, q2h[hp * 64:(hp + 1) * 64, :],
              sc["q2r"][h * 64:(h + 1) * 64, :])
        S.dma("sync", "p7ldh%d" % hp, k2h[hp * 64:(hp + 1) * 64, :],
              sc["k2r"][h * 64:(h + 1) * 64, :])
        S.dma("sync", "p7ldh%d" % hp,
              v2h[:, hp * 2080:(hp + 1) * 2080].rearrange(
                  "a (j c) -> a j c", c=65),
              sc["v2a"][:, h, :].rearrange("(j p) c -> p j c", p=128))
        for qc in range(NCH):
            ngr = 2 * (qc + 1)

            def mm1_group(g, h=h, qc=qc, hp=hp):
                st = g % 2
                S.wait("tensor", "p7ldh%d" % hp)
                S.wait_lag("tensor", "p7ex", 1)
                for kj2 in range(2):
                    f = 2 * g + kj2
                    S.inc("tensor", "p7mm1", 1,
                          lambda e, qc=qc, f=f, st=st, kj2=kj2, hp=hp: e.matmul(
                              ps_big[:, st * 1024 + kj2 * 512:
                                     st * 1024 + (kj2 + 1) * 512],
                              k2h[hp * 64:(hp + 1) * 64, f * 128:(f + 1) * 128],
                              q2h[hp * 64:(hp + 1) * 64,
                                  qc * 512:(qc + 1) * 512],
                              start=True, stop=True))

            yb = qc % 2
            psY = ps_mid[0:65, yb * 512:(yb + 1) * 512]
            mm1_group(0)
            for g in range(ngr):
                st = g % 2
                S.wait("scalar", "p7mm1")
                S.wait("scalar", "p7mm2",
                       target=max(0, S.cnt.get("p7mm2", 0) - 2))
                S.wait("scalar", "p7msk")
                S.wait("scalar", "p7eb")
                S.inc("scalar", "p7ex", 1,
                      lambda e, st=st: e.activation(
                          pexp[:, st * 1024:(st + 1) * 1024],
                          ps_big[:, st * 1024:(st + 1) * 1024], AF.Exp,
                          bias=ebias[:, 0:1], scale=0.125))
                diag = (2 * g >= 4 * qc)
                if diag:
                    mo = (2 * g - 4 * qc) * 512
                    S.wait("vector", "p7ex")
                    S.wait("vector", "p7c")
                    S.inc("vector", "p7msk", 1,
                          lambda e, st=st, mo=mo: e.tensor_tensor(
                              pexp[:, st * 1024:(st + 1) * 1024],
                              pexp[:, st * 1024:(st + 1) * 1024],
                              dmask_sb[:, mo:mo + 1024], OP.mult))
                if g + 1 < ngr:
                    mm1_group(g + 1)
                if diag:
                    S.wait("tensor", "p7msk")
                S.wait("tensor", "p7ex")
                if g == 0:
                    S.wait_lag("tensor", "p7bcv", 1)
                for kj2 in range(2):
                    f = 2 * g + kj2
                    S.inc("tensor", "p7mm2", 1,
                          lambda e, f=f, st=st, kj2=kj2, g=g, ngr=ngr, psY=psY,
                          hp=hp: e.matmul(
                              psY,
                              v2h[:, hp * 2080 + f * 65:
                                  hp * 2080 + (f + 1) * 65],
                              pexp[:, st * 1024 + kj2 * 512:
                                   st * 1024 + (kj2 + 1) * 512],
                              start=(g == 0 and kj2 == 0),
                              stop=(g == ngr - 1 and kj2 == 1)))
            S.wait("scalar", "p7mm2")
            S.wait("scalar", "p7bc")
            S.inc("scalar", "p7rr", 1,
                  lambda e, psY=psY: e.copy(rrow[:, :], psY[64:65, 0:512]))
            S.wait("tensor", "p7rr")
            S.wait("tensor", "p7bce")
            S.inc("tensor", "p7bc", 1,
                  lambda e: e.matmul(ps_b[0:64, 0:512], cb["ones1_64"][:, :],
                                     rrow[:, :], start=True, stop=True))
            S.wait("scalar", "p7bc")
            S.wait("scalar", "p7rcp")
            S.inc("scalar", "p7bce", 1,
                  lambda e: e.copy(rbc[:, :], ps_b[0:64, 0:512]))
            S.wait("vector", "p7bce")
            S.wait("vector", "p7rcp")
            S.inc("vector", "p7rcp", 1,
                  lambda e: e.reciprocal(rbc2[:, :], rbc[:, :]))
            S.wait("vector", "p7mm2")
            S.wait("vector", "p7ast")
            S.inc("vector", "p7bcv", 1,
                  lambda e, psY=psY: e.tensor_tensor(
                      attd[:, :], psY[0:64, 0:512], rbc2[:, :], OP.mult))
            S.wait("gpsimd", "p7bcv")
            S.dma("gpsimd", "p7ast",
                  sc["att"][h * 64:(h + 1) * 64, qc * 512:(qc + 1) * 512],
                  attd[:, :])

    if "att" in dbg_out:
        S.wait("gpsimd", "p7ast")
        for rr in range(4):
            S.dma("gpsimd", "dbg", dbg_out["att"][rr * 128:(rr + 1) * 128, :],
                  sc["att"][rr * 128:(rr + 1) * 128, :])

    S.wait("sync", "p7ast")
    for tt in range(NT):
        for cc in range(4):
            par = cc % 2
            S.wait("sync", "p7pmm")
            S.dma("sync", "p7pld", atile[:, par * 128:(par + 1) * 128],
                  sc["att"][cc * 128:(cc + 1) * 128, tt * 128:(tt + 1) * 128])
            S.wait("tensor", "p7pld")
            S.wait("tensor", "p7pev")
            for oh in range(2):
                S.inc("tensor", "p7pmm", 1,
                      lambda e, cc=cc, oh=oh, par=par: e.matmul(
                          ps_mid[:, oh * 512:(oh + 1) * 512],
                          atile[:, par * 128:(par + 1) * 128],
                          wproj_sb[:, cc * 1024 + oh * 512:
                                   cc * 1024 + (oh + 1) * 512],
                          start=(cc == 0), stop=(cc == 3)))
        S.wait("scalar", "p7pmm")
        S.wait("scalar", "p7pst")
        S.inc("scalar", "p7pev", 1,
              lambda e: e.copy(a2tile[:, :], ps_mid[:, :]))
        S.wait("gpsimd", "p7pev")
        S.dma("gpsimd", "p7pst", sc["cc_a2in"][tt * 128:(tt + 1) * 128, :],
              a2tile[:, :])
        if tt == 15:
            p7pst_half = S.cnt["p7pst"]

    if "attn2p" in dbg_out:
        S.wait("gpsimd", "p7pst")
        S.dma("gpsimd", "dbg", dbg_out["attn2p"][:, :], sc["cc_a2in"][:, :])

    if stop_after == "P7":
        with nc.Block() as blk:
            S.run(blk)
        close_all()
        return nc, dbg_out

    st_p7.close()
    S.join(ENGS, ("p7mm1", "p7mm2", "p7pmm", "p7pld", "p7ldh0", "p7ldh1", "p7pst",
                  "p7ast", "p7c", "ch_vector", "ch_scalar"))

    S.wait("gpsimd", "p7pst", target=p7pst_half)
    S.inc("gpsimd", "p8ccA", 1,
          lambda e: e.collective_compute(
              "AllGather", OP.bypass, replica_groups=[list(range(8))],
              ins=[sc["cc_a2in"][0:TH, :].opt()],
              outs=[sc["cc_a2outA"][:, :, :].opt()]))
    S.wait("gpsimd", "p8ccA")

    def p8pairsA(j):
        hg = j % 2
        sl = slice(hg * TH, (hg + 1) * TH)
        pairs = [(sc["h2own"][:, :], sc["h2"][sl, :]),
                 (sc["x2own"][:, :], sc["x2"][sl, :])]
        if hg == 0:
            pairs.append((sc["a2p"][0:TH, :], sc["cc_a2outA"][j ^ 1]))
        else:
            pairs.append((sc["a1p"][0:TH, :], sc["cc_a2in"][sl, :]))
        return pairs
    partner_copy("p8cpA", p8pairsA)
    S.wait("gpsimd", "p7pst")
    S.inc("gpsimd", "p8ccB", 1,
          lambda e: e.collective_compute(
              "AllGather", OP.bypass, replica_groups=[list(range(8))],
              ins=[sc["cc_a2in"][TH:T, :].opt()],
              outs=[sc["cc_a2outB"][:, :, :].opt()]))
    S.wait("gpsimd", "p8ccB")

    def p8pairsB(j):
        hg = j % 2
        sl = slice(hg * TH, (hg + 1) * TH)
        if hg == 0:
            return [(sc["a1p"][0:TH, :], sc["cc_a2in"][sl, :])]
        return [(sc["a2p"][0:TH, :], sc["cc_a2outB"][j ^ 1])]
    partner_copy("p8cpB", p8pairsB)

    gates_all = sb("gates_all", 128, 16 * 8)
    gk_all = sb("gk_all", 128, 16 * 8)

    st_p8 = ExitStack()
    regions.append(st_p8)
    h4C = sb("h4C", 128, 8 * TH, st_p8)
    g2bc = sb("g2bc", 128, C, st_p8)
    b2bc = sb("b2bc", 128, C, st_p8)
    p8a = sb("p8_a", 128, 2 * C, st_p8)
    p8b = sb("p8_b", 128, 2 * C, st_p8)
    p8c = sb("p8_c", 128, 2 * C, st_p8)
    p8d = sb("p8_d", 128, 2 * C, st_p8)
    p8e = sb("p8_e", 128, 2 * C, st_p8)
    sq8 = sb("p8_sqd", 128, C, st_p8)
    ln8 = sb("p8_lnb", 128, 8 * 16, st_p8)
    tp8 = sb("p8_tp", 128, 2 * 128, st_p8)
    wg_sb = sb("wg_sb", 128, 8 * 8, st_p8)
    gl8 = sb("p8_gl", 128, 8, st_p8)
    ge8 = sb("p8_ge", 128, 8, st_p8)
    gs8 = sb("p8_gs", 128, 4, st_p8)
    gm8 = sb("p8_gm", 128, 8, st_p8)
    mk8 = sb("p8_mk", 128, 8, st_p8)
    w8a = sb("p8_w8a", 128, 16, st_p8)
    w8b = sb("p8_w8b", 128, 16, st_p8)

    S.dma("sync", "p8c", wg_sb[:, :], di["WgateC"][:, :])
    emit_bcast(g2bc, "ln2g")
    emit_bcast(b2bc, "ln2b")
    NTH = TH // 128
    for i in range(NTH):
        par = i % 2
        S.wait("sync", "p8cpA")
        S.wait("sync", "p8cpB")
        S.wait_lag("sync", "p8tt", 2)
        S.dma("sync", "p8ld%d" % par, p8a[:, par * C:(par + 1) * C],
              sc["h2own"][i * 128:(i + 1) * 128, :])
        S.dma("sync", "p8ld%d" % par, p8b[:, par * C:(par + 1) * C],
              sc["a1p"][i * 128:(i + 1) * 128, :])
        S.dma("sync", "p8ld%d" % par, p8c[:, par * C:(par + 1) * C],
              sc["a2p"][i * 128:(i + 1) * 128, :])
        S.wait("vector", "p8ld%d" % par)
        S.wait("vector", "p8x3st%d" % par)
        S.wait_lag("vector", "p8xsq", 1)
        S.inc("vector", "p8tt", 1,
              lambda e, par=par: e.tensor_tensor(
                  p8d[:, par * C:(par + 1) * C], p8b[:, par * C:(par + 1) * C],
                  p8c[:, par * C:(par + 1) * C], OP.add))
        S.inc("vector", "p8tt", 1,
              lambda e, par=par: e.tensor_tensor(
                  p8d[:, par * C:(par + 1) * C], p8a[:, par * C:(par + 1) * C],
                  p8d[:, par * C:(par + 1) * C], OP.add))
        S.inc("vector", "p8xrs", 1,
              lambda e, par=par, i=i: e.reduce_sum(
                  ln8[:, i:i + 1], p8d[:, par * C:(par + 1) * C], axis=AX))
        S.wait("scalar", "p8tt")
        S.inc("scalar", "p8xsq", 1,
              lambda e, par=par, i=i: e.activation(
                  sq8[:, :], p8d[:, par * C:(par + 1) * C], AF.Square,
                  accum_out=ln8[:, 16 + i:16 + i + 1]))
        S.wait("gpsimd", "p8tt")
        S.dma("gpsimd", "p8x3st%d" % par, sc["x3"][i * 128:(i + 1) * 128, :],
              p8d[:, par * C:(par + 1) * C])
    p5_finalize("p8x", ln8[:, 0:16], ln8[:, 16:32], ln8[:, 32:48],
                ln8[:, 48:64], w8a[:, :], w8b[:, :])
    S.wait("sync", "p8tt")
    S.wait("sync", "p8xrs")
    S.wait("sync", "p8xsq")
    for i in range(NTH):
        par = i % 2
        S.wait_lag("sync", "p8af", 4)
        S.wait("sync", "p8x3st%d" % par)
        S.dma("sync", "p8ld2%d" % par, p8d[:, par * C:(par + 1) * C],
              sc["x3"][i * 128:(i + 1) * 128, :])
        S.wait("vector", "p8ld2%d" % par)
        S.wait("vector", "p8xrst")
        S.wait("vector", "p8_mm")
        S.wait("vector", "p8h4st%d" % par)
        S.wait("vector", "p0b_ev")
        S.inc("vector", "p8af", 1,
              lambda e, i=i, par=par: e.tensor_scalar(
                  p8e[:, par * C:(par + 1) * C], p8d[:, par * C:(par + 1) * C],
                  ln8[:, 32 + i:32 + i + 1], ln8[:, 48 + i:48 + i + 1],
                  OP.add, OP.mult))
        S.inc("vector", "p8af", 1,
              lambda e, par=par: e.tensor_tensor(
                  p8e[:, par * C:(par + 1) * C], p8e[:, par * C:(par + 1) * C],
                  g2bc[:, :], OP.mult))
        S.inc("vector", "p8af", 1,
              lambda e, par=par: e.tensor_tensor(
                  p8e[:, par * C:(par + 1) * C], p8e[:, par * C:(par + 1) * C],
                  b2bc[:, :], OP.add))
        S.wait("gpsimd", "p8af")
        S.dma("gpsimd", "p8h4st%d" % par,
              sc["h4tok"][i * 128:(i + 1) * 128, :], p8e[:, par * C:(par + 1) * C])
        for ccj in range(8):
            tp = ccj % 2
            tps = ps_a if tp == 0 else ps_b
            S.wait("tensor", "p8af")
            S.wait_lag("tensor", "p8_ev", 1)
            S.inc("tensor", "p8_mm", 1,
                  lambda e, ccj=ccj, tps=tps, par=par: e.transpose(
                      tps[:, 0:128],
                      p8e[:, par * C + ccj * 128:par * C + (ccj + 1) * 128],
                      cb["ident"][:, :]))
            S.wait("scalar", "p8_mm")
            S.inc("scalar", "p8_ev", 1,
                  lambda e, ccj=ccj, i=i, tps=tps: e.copy(
                      h4C[:, ccj * TH + i * 128:ccj * TH + (i + 1) * 128],
                      tps[:, 0:128]))
        S.wait("tensor", "p8_ev")
        S.wait("tensor", "p8c")
        S.wait("tensor", "p8gev")
        for cc in range(8):
            S.inc("tensor", "p8gmm", 1,
                  lambda e, cc=cc, i=i: e.matmul(
                      ps_mid[:, 0:8],
                      h4C[:, cc * TH + i * 128:cc * TH + (i + 1) * 128],
                      wg_sb[:, cc * 8:(cc + 1) * 8],
                      start=(cc == 0), stop=(cc == 7)))
        S.wait("scalar", "p8gmm")
        S.inc("scalar", "p8gev", 1, lambda e: e.copy(gl8[:, :], ps_mid[:, 0:8]))
        S.wait("scalar", "p8gv")
        S.inc("scalar", "p8ge", 1,
              lambda e: e.activation(ge8[:, :], gl8[:, :], AF.Exp,
                                     accum_out=gs8[:, 0:1]))
        S.wait("vector", "p8ge")
        S.inc("vector", "p8gv", 1, lambda e: e.reciprocal(gs8[:, 1:2], gs8[:, 0:1]))
        S.inc("vector", "p8gv", 1,
              lambda e, i=i: e.tensor_scalar(gates_all[:, i * 8:(i + 1) * 8],
                                             ge8[:, :], gs8[:, 1:2], None,
                                             OP.mult))
        S.inc("vector", "p8gv", 1,
              lambda e, i=i: e.reduce_max(gs8[:, 2:3],
                                          gates_all[:, i * 8:(i + 1) * 8], axis=AX))
        S.inc("vector", "p8gv", 1,
              lambda e, i=i: e.tensor_scalar(gm8[:, :],
                                             gates_all[:, i * 8:(i + 1) * 8],
                                             gs8[:, 2:3], None, OP.is_ge))
        S.inc("vector", "p8gv", 1,
              lambda e, i=i: e.scalar_tensor_tensor(
                  gm8[:, :], gm8[:, :], -1e9,
                  gates_all[:, i * 8:(i + 1) * 8], OP.mult, OP.add))
        S.inc("vector", "p8gv", 1,
              lambda e: e.reduce_max(gs8[:, 3:4], gm8[:, :], axis=AX))
        S.wait("vector", "p8mst")
        S.inc("vector", "p8gv", 1,
              lambda e, i=i: e.tensor_scalar(mk8[:, :],
                                             gates_all[:, i * 8:(i + 1) * 8],
                                             gs8[:, 3:4], None, OP.is_ge))
        S.wait("gpsimd", "p8gv")
        S.dma("gpsimd", "p8mst", sc["cc_mkin"][i * 128:(i + 1) * 128, :],
              mk8[:, :])

    S.wait("gpsimd", "p8mst")
    S.inc("gpsimd", "p9cc", 1,
          lambda e: e.collective_compute(
              "AllGather", OP.bypass, replica_groups=[list(range(8))],
              ins=[sc["cc_mkin"][:, :].opt()], outs=[sc["cc_mkout"][:, :].opt()]))
    S.wait("gpsimd", "p9cc")

    if stop_after == "P8":
        with nc.Block() as blk:
            S.run(blk)
        close_all()
        return nc, dbg_out

    st_p8.close()
    S.join(ENGS, ("p8_mm", "p8gmm", "p8ld0", "p8ld1", "p8ld20", "p8ld21", "p8c",
                  "p8x3st0", "p8x3st1", "p8h4st0", "p8h4st1", "p8mst",
                  "ch_vector", "ch_scalar"))
    st_p9 = ExitStack()
    regions.append(st_p9)
    mA = sb("p9_mA", 128, 1024, st_p9)
    mB = sb("p9_mB", 128, 1024, st_p9)
    tot = sb("p9_tot", 128, 8, st_p9)
    offr = sb("p9_off", 128, 8, st_p9)
    kb9 = sb("p9_kb", 128, 1024, st_p9)
    neg1 = sb("p9_neg1", 1, 8, st_p9)

    S.wait("sync", "p9cc")
    S.dma("sync", "p9ld", mA[:, :],
          sc["cc_mkout"][:, :].rearrange("(g p) e -> g p e", p=128))
    S.inc("vector", "p9v", 1, lambda e: e.memset(neg1[:, :], -1.0))
    cur, nxt = mA, mB
    S.wait("vector", "p9ld")
    for k in [1, 2, 4, 8, 16, 32, 64]:
        S.inc("vector", "p9v", 1,
              lambda e, cur=cur, nxt=nxt, k=k: e.tensor_copy(
                  nxt[:, 0:k * 8], cur[:, 0:k * 8]))
        S.inc("vector", "p9v", 1,
              lambda e, cur=cur, nxt=nxt, k=k: e.tensor_tensor(
                  nxt[:, k * 8:1024], cur[:, k * 8:1024],
                  cur[:, 0:1024 - k * 8], OP.add))
        cur, nxt = nxt, cur
    cum = cur
    S.wait("tensor", "p9v")
    S.wait("tensor", "p9ld")
    S.inc("tensor", "p9mm", 1,
          lambda e: e.matmul(ps_a[:, 0:8], cb["triu_strict"][:, :],
                             cum[:, 127 * 8:128 * 8], start=True, stop=False))
    S.inc("tensor", "p9mm", 1,
          lambda e: e.matmul(ps_a[:, 0:8], cb["ones1_128"][:, :], neg1[:, :],
                             start=False, stop=True))
    S.wait("scalar", "p9mm")
    S.inc("scalar", "p9ev", 1, lambda e: e.copy(offr[:, :], ps_a[:, 0:8]))
    S.wait("vector", "p9ev")
    S.inc("vector", "p9v", 1,
          lambda e, cum=cum: e.tensor_tensor(
              cum[:, :].rearrange("a (p c) -> a p c", c=8), cum[:, :].rearrange(
                  "a (p c) -> a p c", c=8),
              offr[:, :].unsqueeze(1).broadcast_to([128, 128, 8]), OP.add))
    S.inc("vector", "p9v", 1,
          lambda e, cum=cum: e.tensor_scalar(kb9[:, :], cum[:, :], float(CAP),
                                             None, OP.is_lt))
    S.wait("sync", "p9v")
    S.dma("sync", "p9ld2", mA[:, :],
          sc["cc_mkout"][:, :].rearrange("(g p) e -> g p e", p=128))
    S.wait("vector", "p9ld2")
    S.inc("vector", "p9v", 1,
          lambda e: e.tensor_tensor(kb9[:, :], kb9[:, :], mA[:, :], OP.mult))
    sbase = sb("p9_sbase", 128, 8, st_p9)
    shmask = [(i // 16) * 16 for i in range(32)]
    S.inc("vector", "p9v", 1,
          lambda e: e.stream_shuffle(sbase[:, :], offr[:, :], shmask))
    S.inc("vector", "p9v", 1,
          lambda e, cum=cum: e.tensor_tensor(
              mA[:, :].rearrange("a (p c) -> a p c", c=8),
              cum[:, :].rearrange("a (p c) -> a p c", c=8),
              sbase[:, :].unsqueeze(1).broadcast_to([128, 128, 8]), OP.subtract))
    S.inc("vector", "p9v", 1,
          lambda e: e.tensor_scalar(mB[:, :], mA[:, :], 1.0, None, OP.subtract))
    def p9pairs_sb(e):
        reg = pid_reg(e)
        for j in range(8):
            with e.If_eq(reg, j):
                e.dma_start(
                    out=sc["keepown"][:, :].rearrange("(g p) e -> g (p e)", g=16),
                    in_=kb9[j * 16:(j + 1) * 16, :]).then_inc(
                    S.h["p9kst"], 16)
                e.dma_start(
                    out=sc["lposown"][:, :].rearrange("(g p) e -> g (p e)", g=16),
                    in_=mB[j * 16:(j + 1) * 16, :]).then_inc(
                    S.h["p9kst"], 16)
    S.sem("p9kst")
    S.wait("gpsimd", "p9v")
    S.cnt["p9kst"] += 32
    S.names["p9kst"] = ("p9kst", S.cnt["p9kst"])
    S.emit("gpsimd", p9pairs_sb)

    if "keep" in dbg_out:
        S.wait("gpsimd", "p9kst")
        S.dma("gpsimd", "dbg", dbg_out["keep"][:, :], sc["keepown"][:, :])
        S.dma("gpsimd", "dbg", dbg_out["gatesall"][:, :], gates_all[:, :])

    if stop_after == "P9":
        with nc.Block() as blk:
            S.run(blk)
        close_all()
        return nc, dbg_out

    BCP = 768
    DUMP = E * BCP
    st_p9.close()
    S.join(ENGS, ("p9mm", "p9ld", "p9ld2", "p9kst", "ch_vector", "ch_scalar"))
    st_pA = ExitStack()
    regions.append(st_pA)
    hidb = sb("hidb", 128, 2 * 32 * 384, st_pA)
    moeC = sb("moeC", 128, 2 * 1024, st_pA)
    w1b = sb("w1b", 128, 2 * 1024, st_pA)
    w2b = sb("w2b", 128, 2 * 1024, st_pA)
    b1t_sb = sb("b1t_sb", 128, E * 32, st_pA)
    b2_sb = sb("b2_sb", 8, C, st_pA)
    kt8 = sb("p10_kt", 128, 8, st_pA)
    lp8 = sb("p10_lp", 128, 8, st_pA)
    ev8 = sb("p10_ev", 128, 8 * 4, st_pA)
    sc8 = sb("p10_sc", 128, 16, st_pA)
    idxf = sb("p10_idxf", 128, 2, st_pA)
    idxi = st_pA.enter_context(nc.sbuf_tensor("p10_idxi", [128, 2 * 16],
                                               mybir.dt.int32))
    g12 = sb("p10_g12", 128, 2 * 16, st_pA)
    gkT = sb("p10_gkT", 8, 128, st_pA)
    evec_sb = sb("p10_evec", 128, 8, st_pA)
    bktt = sb("p10_bktt", 128, 2 * 1024, st_pA)
    bktC = sb("p10_bktC", 128, 8 * 384, st_pA)
    eo_sb = sb("p10_eo", 128, 2 * 1024, st_pA)
    ga1 = sb("p10_ga1", 128, C, st_pA)
    ga2 = sb("p10_ga2", 128, C, st_pA)
    xf1 = sb("p10_x1", 128, C, st_pA)
    xf2 = sb("p10_x2", 128, C, st_pA)

    S.dma("sync", "pAc", b1t_sb[:, :], di["b1t"][:, :])
    S.dma("sync", "pAc", b2_sb[:, :], di["b2sb"][:, :])
    S.dma("sync", "pAc", evec_sb[:, :], di["evec128"][:, :])

    S.inc("vector", "pAz", 1, lambda e: e.memset(ga1[:, :], 0.0))
    S.wait("gpsimd", "pAz")
    NBK = (E * BCP + 40 + 127) // 128
    for j in range(NBK):
        r0 = j * 128
        r1 = min(r0 + 128, E * BCP + 40)
        S.dma("gpsimd", "pAzst", sc["bkt"][r0:r1, :], ga1[0:r1 - r0, :])
    S.dma("gpsimd", "pAzst", sc["eout"][E * BCP:E * BCP + 40, :], ga1[0:40, :])

    for i in range(NTH):
        S.wait("sync", "pAdsp")
        S.dma("sync", "pAkld", kt8[:, :], sc["keepown"][i * 128:(i + 1) * 128, :])
        S.dma("sync", "pAkld", lp8[:, :], sc["lposown"][i * 128:(i + 1) * 128, :])
        S.wait("vector", "pAkld")
        S.wait("vector", "pAc")
        gsl = gates_all[:, i * 8:(i + 1) * 8]
        S.inc("vector", "pAdsp", 1,
              lambda e, i=i, gsl=gsl: e.tensor_tensor(
                  gk_all[:, i * 8:(i + 1) * 8], gsl, kt8[:, :], OP.mult))
        S.inc("vector", "pAdsp", 1,
              lambda e, gsl=gsl: e.reduce_max(sc8[:, 0:1], gsl, axis=AX))
        S.inc("vector", "pAdsp", 1,
              lambda e, gsl=gsl: e.tensor_scalar(ev8[:, 0:8], gsl, sc8[:, 0:1],
                                                 None, OP.is_ge))
        S.inc("vector", "pAdsp", 1,
              lambda e, gsl=gsl: e.scalar_tensor_tensor(
                  ev8[:, 16:24], ev8[:, 0:8], -1e9, gsl, OP.mult, OP.add))
        S.inc("vector", "pAdsp", 1,
              lambda e: e.reduce_max(sc8[:, 1:2], ev8[:, 16:24], axis=AX))
        S.inc("vector", "pAdsp", 1,
              lambda e, gsl=gsl: e.tensor_scalar(ev8[:, 8:16], gsl, sc8[:, 1:2],
                                                 None, OP.is_ge))
        S.inc("vector", "pAdsp", 1,
              lambda e: e.tensor_tensor(ev8[:, 8:16], ev8[:, 8:16], ev8[:, 0:8],
                                        OP.subtract))
        for k in range(2):
            eq = lambda k=k: ev8[:, k * 8:(k + 1) * 8]
            col = 2 + k * 4
            S.inc("vector", "pAdsp", 1,
                  lambda e, k=k, col=col: e.tensor_tensor(
                      ev8[:, 16:24], ev8[:, k * 8:(k + 1) * 8], evec_sb[:, :],
                      OP.mult))
            S.inc("vector", "pAdsp", 1,
                  lambda e, col=col: e.reduce_sum(sc8[:, col:col + 1],
                                                  ev8[:, 16:24], axis=AX))
            S.inc("vector", "pAdsp", 1,
                  lambda e, k=k: e.tensor_tensor(
                      ev8[:, 16:24], ev8[:, k * 8:(k + 1) * 8], lp8[:, :],
                      OP.mult))
            S.inc("vector", "pAdsp", 1,
                  lambda e, col=col: e.reduce_sum(sc8[:, col + 1:col + 2],
                                                  ev8[:, 16:24], axis=AX))
            S.inc("vector", "pAdsp", 1,
                  lambda e, k=k: e.tensor_tensor(
                      ev8[:, 16:24], ev8[:, k * 8:(k + 1) * 8], kt8[:, :],
                      OP.mult))
            S.inc("vector", "pAdsp", 1,
                  lambda e, col=col: e.reduce_sum(sc8[:, col + 2:col + 3],
                                                  ev8[:, 16:24], axis=AX))
            S.inc("vector", "pAdsp", 1,
                  lambda e, i=i, k=k: e.tensor_tensor(
                      ev8[:, 24:32], ev8[:, k * 8:(k + 1) * 8],
                      gk_all[:, i * 8:(i + 1) * 8], OP.mult))
            S.inc("vector", "pAdsp", 1,
                  lambda e, i=i, k=k: e.reduce_sum(g12[:, i * 2 + k:i * 2 + k + 1],
                                                   ev8[:, 24:32], axis=AX))
            dumpk = float(DUMP + i * 2 + k)
            S.inc("vector", "pAdsp", 1,
                  lambda e, col=col: e.tensor_scalar(
                      idxf[:, 0:1], sc8[:, col:col + 1], float(BCP), None,
                      OP.mult))
            S.inc("vector", "pAdsp", 1,
                  lambda e, col=col: e.tensor_tensor(
                      idxf[:, 0:1], idxf[:, 0:1], sc8[:, col + 1:col + 2],
                      OP.add))
            S.inc("vector", "pAdsp", 1,
                  lambda e, dumpk=dumpk: e.tensor_scalar(
                      idxf[:, 1:2], idxf[:, 0:1], -dumpk, None, OP.add))
            S.inc("vector", "pAdsp", 1,
                  lambda e, col=col: e.tensor_tensor(
                      idxf[:, 1:2], idxf[:, 1:2], sc8[:, col + 2:col + 3],
                      OP.mult))
            S.inc("vector", "pAdsp", 1,
                  lambda e, dumpk=dumpk: e.tensor_scalar(
                      idxf[:, 0:1], idxf[:, 1:2], dumpk, None, OP.add))
            S.inc("vector", "pAdsp", 1,
                  lambda e, i=i, k=k: e.tensor_copy(
                      idxi[:, i * 2 + k:i * 2 + k + 1], idxf[:, 0:1]))
    h4sb = bktt
    S.wait("gpsimd", "pAdsp")
    S.wait("gpsimd", "pAzst")
    S.wait("sync", "p8h4st0")
    S.wait("sync", "p8h4st1")
    for i in range(NTH):
        par = i % 2
        S.wait("sync", "pAsc")
        S.dma("sync", "pAh4ld", h4sb[:, par * 1024:(par + 1) * 1024],
              sc["h4tok"][i * 128:(i + 1) * 128, :])
        S.wait("gpsimd", "pAh4ld")
        for k in range(2):
            S.inc("gpsimd", "pAsc", 16,
                  lambda e, i=i, k=k, par=par: e.indirect_dma_start(
                      out=sc["bkt"][:, :],
                      out_offset=bass.IndirectOffsetOnAxis(
                          ap=idxi[:, i * 2 + k:i * 2 + k + 1], axis=0),
                      in_=h4sb[:, par * 1024:(par + 1) * 1024],
                      in_offset=None))
    nphase = 0
    for ee in range(E):
        for hf in range(2):
            hpar = nphase % 2
            hb0 = hpar * 32 * 384
            nphase += 1
            for jt in range(3):
                par = jt % 2
                S.wait("sync", "pAtr")
                S.wait("sync", "pAsc")
                S.dma("sync", "pAbld", bktt[:, par * 1024:(par + 1) * 1024],
                      sc["bkt"][ee * BCP + hf * 384 + jt * 128:
                                ee * BCP + hf * 384 + (jt + 1) * 128, :])
                S.wait("tensor", "pAbld")
                S.wait("tensor", "pAge")
                for ccj in range(8):
                    S.wait("tensor", "pAbev")
                    S.inc("tensor", "pAtr", 1,
                          lambda e, ccj=ccj, par=par: e.transpose(
                              ps_b[:, 0:128],
                              bktt[:, par * 1024 + ccj * 128:
                                   par * 1024 + (ccj + 1) * 128],
                              cb["ident"][:, :]))
                    S.wait("scalar", "pAtr")
                    S.inc("scalar", "pAbev", 1,
                          lambda e, ccj=ccj, jt=jt: e.copy(
                              bktC[:, ccj * 384 + jt * 128:
                                   ccj * 384 + (jt + 1) * 128],
                              ps_b[:, 0:128]))
            for ft in range(32):
                par = ft % 2
                S.wait("sync", "pAh1", target=max(0, S.cnt.get("pAh1", 0) - 8))
                S.dma("sync", "pAw1%d" % par,
                      w1b[:, par * 1024:(par + 1) * 1024], di["W1t"][ee, ft])
                S.wait("tensor", "pAw1%d" % par)
                S.wait("tensor", "pAbev")
                hps = ps_a if ft % 2 == 0 else ps_b
                S.wait_lag("tensor", "pAge", 1)
                for cc in range(8):
                    S.inc("tensor", "pAh1", 1,
                          lambda e, cc=cc, par=par, hps=hps: e.matmul(
                              hps[:, 0:384],
                              w1b[:, par * 1024 + cc * 128:
                                  par * 1024 + (cc + 1) * 128],
                              bktC[:, cc * 384:(cc + 1) * 384],
                              start=(cc == 0), stop=(cc == 7)))
                S.wait("scalar", "pAh1")
                S.wait("scalar", "pAw2mm",
                       target=max(0, S.cnt.get("pAw2mm", 0) - 192))
                S.inc("scalar", "pAge", 1,
                      lambda e, ft=ft, ee=ee, hb0=hb0, hps=hps: e.activation(
                          hidb[:, hb0 + ft * 384:hb0 + (ft + 1) * 384],
                          hps[:, 0:384], AF.Gelu,
                          bias=b1t_sb[:, ee * 32 + ft:ee * 32 + ft + 1],
                          scale=1.0))
            w2ps = [ps_big[:, 0:512], ps_big[:, 512:1024],
                    ps_big[:, 1024:1536], ps_big[:, 1536:2048],
                    ps_mid[:, 0:512], ps_mid[:, 512:1024]]
            for ft in range(32):
                par = ft % 2
                S.wait("sync", "pAw2mm",
                       target=max(0, S.cnt.get("pAw2mm", 0) - 6))
                S.dma("sync", "pAw2%d" % par,
                      w2b[:, par * 1024:(par + 1) * 1024],
                      di["W2"][ee, ft * 128:(ft + 1) * 128, :])
                S.wait("tensor", "pAw2%d" % par)
                S.wait("tensor", "pAge")
                if ft == 0:
                    S.wait("tensor", "pAeev")
                for jt in range(3):
                    for oh in range(2):
                        S.inc("tensor", "pAw2mm", 1,
                              lambda e, jt=jt, oh=oh, ft=ft, par=par, hb0=hb0:
                              e.matmul(
                                  w2ps[jt * 2 + oh],
                                  hidb[:, hb0 + ft * 384 + jt * 128:
                                       hb0 + ft * 384 + (jt + 1) * 128],
                                  w2b[:, par * 1024 + oh * 512:
                                      par * 1024 + (oh + 1) * 512],
                                  start=(ft == 0), stop=(ft == 31)))
            S.wait("scalar", "pAw2mm")
            for jt in range(3):
                jp = jt % 2
                S.wait("scalar", "pAest%d" % jp)
                S.inc("scalar", "pAeev", 1,
                      lambda e, jt=jt, jp=jp: e.copy(
                          eo_sb[:, jp * 1024:(jp + 1) * 1024],
                          ps_mid[:, :] if jt == 2 else
                          ps_big[:, jt * 1024:(jt + 1) * 1024]))
                S.wait("gpsimd", "pAeev")
                S.dma("gpsimd", "pAest%d" % jp,
                      sc["eout"][ee * BCP + hf * 384 + jt * 128:
                                 ee * BCP + hf * 384 + (jt + 1) * 128, :],
                      eo_sb[:, jp * 1024:(jp + 1) * 1024])
    S.wait("gpsimd", "pAest0")
    S.wait("gpsimd", "pAest1")
    for i in range(NTH):
        S.wait("tensor", "pAgkT")
        S.wait("tensor", "pAfin")
        S.inc("tensor", "pAtr2", 1,
              lambda e, i=i: e.transpose(ps_b[0:8, 0:128],
                                         gk_all[:, i * 8:(i + 1) * 8],
                                         cb["ident"][:, :]))
        S.wait("scalar", "pAtr2")
        S.inc("scalar", "pAgkT", 1, lambda e: e.copy(gkT[:, :], ps_b[0:8, 0:128]))
        S.wait("tensor", "pAgkT")
        S.wait("tensor", "pAc")
        S.wait("tensor", "pAinev")
        for oh in range(2):
            S.inc("tensor", "pAinmm", 1,
                  lambda e, oh=oh: e.matmul(
                      ps_mid[:, oh * 512:(oh + 1) * 512], gkT[:, :],
                      b2_sb[:, oh * 512:(oh + 1) * 512], start=True, stop=True))
        S.wait("scalar", "pAinmm")
        S.wait("scalar", "pAfin")
        S.inc("scalar", "pAinev", 1,
              lambda e, i=i: e.copy(moeC[:, (i % 2) * 1024:(i % 2 + 1) * 1024],
                                    ps_mid[:, :]))
        S.wait("gpsimd", "pAfin")
        for k, gbuf in ((0, ga1), (1, ga2)):
            S.inc("gpsimd", "pAga", 16,
                  lambda e, i=i, k=k, gbuf=gbuf: e.indirect_dma_start(
                      out=gbuf[:, :], out_offset=None,
                      in_=sc["eout"][:, :],
                      in_offset=bass.IndirectOffsetOnAxis(
                          ap=idxi[:, i * 2 + k:i * 2 + k + 1], axis=0)))
        S.wait("sync", "pAfin")
        S.wait("sync", "pAost")
        S.dma("sync", "pAfld", xf1[:, :], sc["x2own"][i * 128:(i + 1) * 128, :])
        S.dma("sync", "pAfld", xf2[:, :], sc["x3"][i * 128:(i + 1) * 128, :])
        S.wait("vector", "pAfld")
        S.wait("vector", "pAga")
        S.wait("vector", "pAinev")
        S.inc("vector", "pAfin", 1,
              lambda e, i=i: e.scalar_tensor_tensor(
                  moeC[:, (i % 2) * 1024:(i % 2 + 1) * 1024], ga1[:, :],
                  g12[:, i * 2:i * 2 + 1],
                  moeC[:, (i % 2) * 1024:(i % 2 + 1) * 1024], OP.mult, OP.add))
        S.inc("vector", "pAfin", 1,
              lambda e, i=i: e.scalar_tensor_tensor(
                  moeC[:, (i % 2) * 1024:(i % 2 + 1) * 1024], ga2[:, :],
                  g12[:, i * 2 + 1:i * 2 + 2],
                  moeC[:, (i % 2) * 1024:(i % 2 + 1) * 1024], OP.mult, OP.add))
        S.inc("vector", "pAfin", 1,
              lambda e, i=i: e.tensor_tensor(
                  xf2[:, :], moeC[:, (i % 2) * 1024:(i % 2 + 1) * 1024],
                  xf2[:, :], OP.add))
        S.inc("vector", "pAfin", 1,
              lambda e: e.tensor_tensor(xf2[:, :], xf1[:, :], xf2[:, :], OP.add))
        S.wait("gpsimd", "pAfin")
        S.dma("gpsimd", "pAost", out[i * 128:(i + 1) * 128, :], xf2[:, :])

    S.wait("gpsimd", "pAost")
    with nc.Block() as blk:
        S.run(blk)
    close_all()
    return nc, dbg_out


def _final_inputs(ins):
    return [
        _core_inputs(ins, core) for core in range(8)
    ]


def kernel(**inputs):
    nc, _ = build_program()
    in_maps = _final_inputs(inputs)
    res = run_bass_kernel_spmd(nc, in_maps, core_ids=list(range(8)))
    o = np.zeros((B, T, C), np.float32)
    for core in range(8):
        b = core // 2
        hg = core % 2
        o[b, hg * TH:(hg + 1) * TH, :] = np.asarray(res.results[core]["out"])
    return o


# version 10
# speedup vs baseline: 1.0068x; 1.0068x over previous
import math
from contextlib import ExitStack

import numpy as np

import concourse.bass as bass
import concourse.mybir as mybir
from concourse.bass_utils import run_bass_kernel_spmd

F32 = mybir.dt.float32
AF = mybir.ActivationFunctionType
OP = mybir.AluOpType
AX = mybir.AxisListType.X

B, T, C = 4, 4096, 1024
H, HD = 16, 64
HL = 8
CL = HL * HD
NT = T // 128
NCH = T // 512
E = 8
CAP = 4096
TH = T // 2
FF = 4096
EXPC = 12.0
LN_EPS = 1e-5



def _gammas():
    lo = math.log(1.0 - 0.999)
    hi = math.log(1.0 - 0.9)
    return (1.0 - np.exp(np.linspace(lo, hi, H))).astype(np.float32)


def _tile_kmaj(a):
    K, M = a.shape
    return np.ascontiguousarray(
        a.reshape(K // 128, 128, M).transpose(1, 0, 2).reshape(128, (K // 128) * M)
    ).astype(np.float32)


def _host_consts(core):
    hg = core % 2
    g = _gammas()[hg * HL:(hg + 1) * HL].astype(np.float64)
    iloc = np.arange(128)

    dist = iloc[None, :] - iloc[:, None]
    d0t = np.zeros((128, HL * 128), np.float32)
    for h in range(HL):
        d0t[:, h * 128:(h + 1) * 128] = np.where(
            dist >= 0, g[h] ** np.maximum(dist, 0), 0.0).astype(np.float32)

    u2 = np.zeros((128, 4 * 128), np.float32)
    qs = np.zeros((128, 4 * 128), np.float32)
    c128 = np.zeros((128, 4 * 64), np.float32)
    for rt in range(4):
        for hl in range(2):
            gf = g[2 * rt + hl]
            u2[:, rt * 128 + hl * 64:rt * 128 + (hl + 1) * 64] = \
                (gf ** (127 - iloc))[:, None].astype(np.float32)
            qs[hl * 64:(hl + 1) * 64, rt * 128:(rt + 1) * 128] = \
                (gf ** (iloc + 1))[None, :].astype(np.float32)
            c128[hl * 64:(hl + 1) * 64, rt * 64:(rt + 1) * 64] = \
                np.float32(gf ** 128)

    ident = np.eye(128, dtype=np.float32)

    invf = 1.0 / (10000.0 ** (np.arange(0, HD, 2, dtype=np.float64) / HD))
    ang = np.arange(T, dtype=np.float64)[:, None] * invf[None, :]
    crep = np.tile(np.cos(ang).astype(np.float32).T, (4, 1))
    srep = np.tile(np.sin(ang).astype(np.float32).T, (4, 1))

    dmask = np.zeros((128, 4 * 512), np.float32)
    for kj in range(4):
        kg = kj * 128 + iloc[:, None]
        qg = np.arange(512)[None, :]
        dmask[:, kj * 512:(kj + 1) * 512] = (qg >= kg).astype(np.float32)

    triu_incl = np.triu(np.ones((128, 128), np.float32), 0)
    triu_strict = np.triu(np.ones((128, 128), np.float32), 1)

    sel2 = np.zeros((128, 2), np.float32)
    sel2[:64, 0] = 1.0
    sel2[64:, 1] = 1.0
    sel2T = np.ascontiguousarray(sel2.T)

    ones1_128 = np.ones((1, 128), np.float32)
    ones1_64 = np.ones((1, 64), np.float32)

    return {
        "d0t": d0t, "u2": u2, "qs": qs, "c128": c128, "ident": ident,
        "crep": crep, "srep": srep, "dmask": dmask,
        "triu_incl": triu_incl, "triu_strict": triu_strict,
        "sel2": sel2, "sel2T": sel2T,
        "ones1_128": ones1_128, "ones1_64": ones1_64,
        "evec128": np.broadcast_to(np.arange(8, dtype=np.float32)[None, :],
                                   (128, 8)).copy(),
    }


def _core_inputs(ins, core):
    b = core // 2
    hg = core % 2
    hsl = slice(hg * CL, (hg + 1) * CL)

    x = np.ascontiguousarray(np.asarray(ins["L0_attn_x"])[b]).astype(np.float32)

    wq = np.asarray(ins["L0_attn_wq"]); wk = np.asarray(ins["L0_attn_wk"])
    wv = np.asarray(ins["L0_attn_wv"]); wo = np.asarray(ins["L0_attn_wo"])
    wqC = _tile_kmaj(np.ascontiguousarray(wq[hsl].T))
    wkC = _tile_kmaj(np.ascontiguousarray(wk[hsl].T))
    wvC = _tile_kmaj(np.ascontiguousarray(wv[hsl].T))
    woC = _tile_kmaj(np.ascontiguousarray(wo[:, hsl].T))

    gw = np.asarray(ins["L0_attn_gn_w"])[hsl].reshape(HL, HD)
    gb = np.asarray(ins["L0_attn_gn_b"])[hsl].reshape(HL, HD)
    gnw128 = np.zeros((128, 4), np.float32)
    gnb128 = np.zeros((128, 4), np.float32)
    for rt in range(4):
        for hl in range(2):
            gnw128[hl * 64:(hl + 1) * 64, rt] = gw[2 * rt + hl]
            gnb128[hl * 64:(hl + 1) * 64, rt] = gb[2 * rt + hl]

    Wqkv = np.asarray(ins["L0_ffn_Wqkv"])
    heads = np.arange(hg * HL, (hg + 1) * HL)
    colq = heads[:, None] * 192 + np.arange(64)[None, :]
    colk = colq + 64
    colv = colq + 128
    cols = np.concatenate([colq[:, :32].ravel(), colq[:, 32:].ravel(),
                           colk[:, :32].ravel(), colk[:, 32:].ravel(),
                           colv.ravel()])
    WqkvC = _tile_kmaj(np.ascontiguousarray(Wqkv[:, cols]))

    WprojR = _tile_kmaj(np.ascontiguousarray(
        np.asarray(ins["L0_ffn_Wproj"])[hsl, :]))

    WgateC = _tile_kmaj(np.asarray(ins["L0_ffn_Wgate"]).astype(np.float32))

    W1 = np.asarray(ins["L0_ffn_W1"]).astype(np.float32)
    W1t = np.ascontiguousarray(
        W1.reshape(E, 8, 128, 32, 128).transpose(0, 3, 2, 1, 4)).reshape(
        E, 32, 128, 8 * 128)
    W2 = np.ascontiguousarray(np.asarray(ins["L0_ffn_W2"])).astype(np.float32)
    b1t = np.ascontiguousarray(
        np.asarray(ins["L0_ffn_b1"]).reshape(E, 32, 128).transpose(2, 0, 1)
    ).reshape(128, E * 32).astype(np.float32)
    b2sb = np.asarray(ins["L0_ffn_b2"]).astype(np.float32)

    m = {
        "x": x, "wqC": wqC, "wkC": wkC, "wvC": wvC, "woC": woC,
        "gnw128": gnw128, "gnb128": gnb128,
        "WqkvC": WqkvC, "WprojR": WprojR,
        "ln1g": np.asarray(ins["L0_ffn_ln1_g"]).reshape(1, C).astype(np.float32),
        "ln1b": np.asarray(ins["L0_ffn_ln1_b"]).reshape(1, C).astype(np.float32),
        "ln2g": np.asarray(ins["L0_ffn_ln2_g"]).reshape(1, C).astype(np.float32),
        "ln2b": np.asarray(ins["L0_ffn_ln2_b"]).reshape(1, C).astype(np.float32),
        "WgateC": WgateC, "W1t": W1t, "W2": W2, "b1t": b1t, "b2sb": b2sb,
    }
    m.update(_host_consts(core))
    return m



ENGS = ("sync", "tensor", "vector", "scalar", "gpsimd")


CHAINED = ("vector", "scalar")


class Sched:

    def __init__(self, nc, stack):
        self.nc = nc
        self.q = {e: [] for e in ENGS}
        self.cnt = {}
        self.h = {}
        self.stack = stack
        self.names = {}
        self.hist = {}

    def sem(self, name):
        if name not in self.h:
            self.h[name] = self.stack.enter_context(self.nc.semaphore(name))
            self.cnt[name] = 0
        return self.h[name]

    def emit(self, eng, fn):
        self.q[eng].append(fn)

    def inc(self, eng, sname, k, build):
        if eng in CHAINED:
            ch = "ch_" + eng
            self.sem(ch)
            t = self.cnt[ch]
            h = self.h[ch]
            if t > 0:
                self.q[eng].append(lambda e, h=h, t=t: e.wait_ge(h, t))
            self.cnt[ch] += 1
            self.names[sname] = (ch, self.cnt[ch])
            self.hist.setdefault(sname, []).append(self.cnt[ch])
            self.q[eng].append(lambda e, b=build, h=h: b(e).then_inc(h, 1))
        else:
            self.sem(sname)
            self.cnt[sname] += k
            self.names[sname] = (sname, self.cnt[sname])
            h = self.h[sname]
            self.q[eng].append(lambda e, b=build, h=h, k=k: b(e).then_inc(h, k))

    def wait(self, eng, sname, target=None):
        if sname.startswith("ch_"):
            if sname not in self.cnt or self.cnt[sname] <= 0:
                return
            h = self.h[sname]
            t = self.cnt[sname]
            self.q[eng].append(lambda e, h=h, t=t: e.wait_ge(h, t))
            return
        if sname not in self.names:
            if target is None or target <= 0:
                return
            self.sem(sname)
            self.names[sname] = (sname, 0)
        s, v = self.names[sname]
        if target is not None:
            assert s == sname, f"explicit target on chained name {sname}"
            v = target
        if v <= 0:
            return
        h = self.h[s]
        self.q[eng].append(lambda e, h=h, t=v: e.wait_ge(h, t))

    def wait_lag(self, eng, sname, lag):
        hist = self.hist.get(sname, [])
        if len(hist) <= lag:
            return
        s, _ = self.names[sname]
        v = hist[-1 - lag]
        h = self.h[s]
        self.q[eng].append(lambda e, h=h, t=v: e.wait_ge(h, t))

    def join(self, engines, names):
        for e in engines:
            for n in names:
                self.wait(e, n)

    def dma(self, eng, sname, dst, src):
        self.inc(eng, sname, 16, lambda e, d=dst, s=src: e.dma_start(out=d, in_=s))

    def run(self, block):
        for eng in ENGS:
            ops = self.q[eng]

            def make(ops):
                def body(e):
                    for f in ops:
                        f(e)
                return body
            getattr(block, eng)(make(ops))



IN_SPECS = [
    ("x", (T, C)), ("wqC", (128, 8 * 512)), ("wkC", (128, 8 * 512)),
    ("wvC", (128, 8 * 512)), ("woC", (128, 4 * 1024)),
    ("gnw128", (128, 4)), ("gnb128", (128, 4)),
    ("WqkvC", (128, 8 * 1536)), ("WprojR", (128, 4 * 1024)),
    ("ln1g", (1, C)), ("ln1b", (1, C)), ("ln2g", (1, C)), ("ln2b", (1, C)),
    ("WgateC", (128, 8 * 8)), ("W1t", (E, 32, 128, 8 * 128)),
    ("W2", (E, FF, C)), ("b1t", (128, E * 32)), ("b2sb", (8, C)),
    ("d0t", (128, HL * 128)), ("u2", (128, 4 * 128)), ("qs", (128, 4 * 128)),
    ("c128", (128, 4 * 64)), ("ident", (128, 128)),
    ("crep", (128, T)), ("srep", (128, T)), ("dmask", (128, 4 * 512)),
    ("triu_incl", (128, 128)), ("triu_strict", (128, 128)),
    ("sel2", (128, 2)), ("sel2T", (2, 128)), ("evec128", (128, 8)),
    ("ones1_128", (1, 128)), ("ones1_64", (1, 64)),
]


def build_program(dbg=(), stop_after=None):
    nc = bass.Bass(trn_type="TRN2", target_bir_lowering=False)

    di = {}
    for n, s in IN_SPECS:
        di[n] = nc.dram_tensor(n, list(s), F32, kind="ExternalInput").ap()

    out = nc.dram_tensor("out", [TH, C], F32, kind="ExternalOutput").ap()

    sc = {}
    for name, shape in [
        ("h1C", (C, T)), ("x2", (T, C)), ("h2", (T, C)), ("h3C", (C, T)),
        ("q2r", (CL, T)), ("k2r", (CL, T)), ("v2a", (T, HL, 65)),
        ("att", (CL, T)), ("x3", (TH, C)),
        ("a1p", (T, C)), ("a2p", (T, C)),
        ("h2own", (TH, C)), ("x2own", (TH, C)),
        ("keepown", (TH, E)), ("lposown", (TH, E)), ("h4tok", (TH, C)),
        ("bkt", (E * 768 + 40, C)), ("eout", (E * 768 + 40, C)),
        ("cc_a1in", (T, C)), ("cc_a1outA", (8, 1024, C)),
        ("cc_a1outB", (8, 1024, C)), ("cc_a1outC", (8, 1024, C)),
        ("cc_a1outD", (8, 1024, C)),
        ("cc_a2in", (T, C)), ("cc_a2outA", (8, TH, C)),
        ("cc_a2outB", (8, TH, C)),
        ("cc_mkin", (TH, E)), ("cc_mkout", (8 * TH, E)),
    ]:
        sc[name] = nc.dram_tensor(name, list(shape), F32).ap()

    dbg_out = {}
    for name, shape in dbg:
        dbg_out[name] = nc.dram_tensor("dbg_" + name, list(shape), F32,
                                       kind="ExternalOutput").ap()

    stack = ExitStack()
    regions = []
    S = Sched(nc, stack)

    def close_all():
        for st in reversed(regions):
            st.close()
        stack.close()

    def sb(name, p, f, st=None):
        return (st or stack).enter_context(nc.sbuf_tensor(name, [p, f], F32))

    cb = {}
    for n, p, f in [
        ("ident", 128, 128), ("ones1_128", 1, 128), ("ones1_64", 1, 64),
        ("d0t", 128, HL * 128), ("u2", 128, 4 * 128), ("qs", 128, 4 * 128),
        ("c128", 128, 4 * 64), ("gnw128", 128, 4), ("gnb128", 128, 4),
        ("sel2", 128, 2), ("sel2T", 2, 128),
        ("triu_incl", 128, 128), ("triu_strict", 128, 128),
    ]:
        cb[n] = sb("cb_" + n, p, f)

    ps_big = stack.enter_context(nc.psum_tensor("ps_big", [128, 2048], F32))
    ps_mid = stack.enter_context(nc.psum_tensor("ps_mid", [128, 1024], F32))
    ps_a = stack.enter_context(nc.psum_tensor("ps_a", [128, 512], F32))
    ps_b = stack.enter_context(nc.psum_tensor("ps_b", [128, 512], F32))

    for n in cb:
        S.dma("sync", "c_ld", cb[n][:, :], di[n][:, :])

    lnrow = sb("lnrow", 1, C)

    def emit_bcast(dstbuf, srcname):
        S.wait("sync", "p0b_mm")
        S.dma("sync", "p0b_ld", lnrow[:, :], di[srcname][:, :])
        S.wait("tensor", "p0b_ld")
        S.wait("tensor", "c_ld")
        S.wait("tensor", "p0b_ev")
        for oh in range(2):
            S.inc("tensor", "p0b_mm", 1,
                  lambda e, oh=oh: e.matmul(
                      ps_mid[:, oh * 512:(oh + 1) * 512], cb["ones1_128"][:, :],
                      lnrow[:, oh * 512:(oh + 1) * 512], start=True, stop=True))
        S.wait("scalar", "p0b_mm")
        S.inc("scalar", "p0b_ev", 1,
              lambda e, d=dstbuf: e.copy(d[:, :], ps_mid[:, :]))

    st_p1 = ExitStack()
    regions.append(st_p1)
    xt = sb("p1_x", 128, 2 * C, st_p1)
    h1t = sb("p1_h1", 128, C, st_p1)
    sqd = sb("p1_sqd", 128, C, st_p1)
    lnb = sb("p1_lnb", 128, 8 * NT, st_p1)
    tpo = sb("p1_tpo", 128, 2 * 128, st_p1)

    def emit_ln_batched(pfx, n, load_fn, sums, sq, work):
        for i in range(n):
            par = i % 2
            ap = load_fn(i, par)
            S.wait("vector", pfx + "ld%d" % par)
            S.inc("vector", pfx + "rs", 1,
                  lambda e, ap=ap, i=i: e.reduce_sum(sums[:, i:i + 1], ap, axis=AX))
            S.wait("scalar", pfx + "ld%d" % par)
            S.inc("scalar", pfx + "sq", 1,
                  lambda e, ap=ap, i=i: e.activation(sqd[:, :], ap, AF.Square,
                                                     accum_out=sq[:, i:i + 1]))

    def emit_ln_finalize(pfx, n, sums, sq, negmean, rstd, work):
        S.wait("vector", pfx + "sq")
        S.inc("vector", pfx + "fz", 1,
              lambda e: e.tensor_scalar(negmean, sums, -1.0 / C, None, OP.mult))
        S.inc("vector", pfx + "fz", 1,
              lambda e: e.tensor_scalar(work[0], sq, 1.0 / C, None, OP.mult))
        S.inc("vector", pfx + "fz", 1,
              lambda e: e.tensor_tensor(work[1], negmean, negmean, OP.mult))
        S.inc("vector", pfx + "fz", 1,
              lambda e: e.tensor_tensor(work[1], work[0], work[1], OP.subtract))
        S.inc("vector", pfx + "fz", 1,
              lambda e: e.tensor_scalar(work[0], work[1], LN_EPS, None, OP.add))
        S.inc("vector", pfx + "fz", 1, lambda e: e.reciprocal(work[1], work[0]))
        S.wait("scalar", pfx + "fz")
        S.inc("scalar", pfx + "rst", 1,
              lambda e: e.activation(rstd, work[1], AF.Sqrt))

    def p1_load(i, par):
        S.wait_lag("sync", "p1rs", 1)
        S.wait_lag("sync", "p1sq", 1)
        S.dma("sync", "p1ld%d" % par, xt[:, par * C:(par + 1) * C],
              di["x"][i * 128:(i + 1) * 128, :])
        return xt[:, par * C:(par + 1) * C]

    S.wait("tensor", "c_ld")
    S.wait("vector", "c_ld")
    emit_ln_batched("p1", NT, p1_load,
                    lnb[:, 0:NT], lnb[:, NT:2 * NT], None)
    emit_ln_finalize("p1", NT, lnb[:, 0:NT], lnb[:, NT:2 * NT],
                     lnb[:, 2 * NT:3 * NT], lnb[:, 3 * NT:4 * NT],
                     (lnb[:, 4 * NT:5 * NT], lnb[:, 5 * NT:6 * NT]))
    S.wait("sync", "p1rs")
    S.wait("sync", "p1sq")
    for i in range(NT):
        par = i % 2
        S.wait_lag("sync", "p1ap", 1)
        S.dma("sync", "p1ld2%d" % par, xt[:, par * C:(par + 1) * C],
              di["x"][i * 128:(i + 1) * 128, :])
        S.wait("vector", "p1ld2%d" % par)
        S.wait("vector", "p1rst")
        S.wait("vector", "p1_mm")
        S.inc("vector", "p1ap", 1,
              lambda e, i=i, par=par: e.tensor_scalar(
                  h1t[:, :], xt[:, par * C:(par + 1) * C],
                  lnb[:, 2 * NT + i:2 * NT + i + 1],
                  lnb[:, 3 * NT + i:3 * NT + i + 1], OP.add, OP.mult))
        for ccj in range(8):
            tp = ccj % 2
            tps = ps_a if tp == 0 else ps_b
            S.wait("tensor", "p1ap")
            S.wait_lag("tensor", "p1_ev", 1)
            S.inc("tensor", "p1_mm", 1,
                  lambda e, ccj=ccj, tps=tps: e.transpose(
                      tps[:, 0:128],
                      h1t[:, ccj * 128:(ccj + 1) * 128], cb["ident"][:, :]))
            S.wait("scalar", "p1_mm")
            S.wait("scalar", "p1st%d" % tp)
            S.inc("scalar", "p1_ev", 1,
                  lambda e, tp=tp, tps=tps: e.copy(
                      tpo[:, tp * 128:(tp + 1) * 128], tps[:, 0:128]))
            S.wait("gpsimd", "p1_ev")
            S.dma("gpsimd", "p1st%d" % tp,
                  sc["h1C"][ccj * 128:(ccj + 1) * 128, i * 128:(i + 1) * 128],
                  tpo[:, tp * 128:(tp + 1) * 128])

    if "h1C" in dbg_out:
        S.wait("gpsimd", "p1st0")
        S.wait("gpsimd", "p1st1")
        for cc in range(8):
            S.dma("gpsimd", "dbg", dbg_out["h1C"][cc * 128:(cc + 1) * 128, :],
                  sc["h1C"][cc * 128:(cc + 1) * 128, :])

    if stop_after == "P1":
        with nc.Block() as blk:
            S.run(blk)
        close_all()
        return nc, dbg_out

    st_p1.close()
    S.join(("sync", "tensor", "vector", "scalar", "gpsimd"),
           ("p1_mm", "p1_ev", "p1st0", "p1st1", "ch_vector", "ch_scalar",
            "p1ld0", "p1ld1", "p1ld20", "p1ld21"))
    st_y = ExitStack()
    regions.append(st_y)
    yC = sb("yC", 128, 4 * T, st_y)
    state = sb("state", 128, 4 * 64, st_y)

    st_p23 = ExitStack()
    regions.append(st_p23)
    wq_rt = sb("wq_rt", 128, 8 * 128, st_p23)
    wk_rt = sb("wk_rt", 128, 8 * 128, st_p23)
    wkv_rt = sb("wkv_rt", 128, 8 * 256, st_p23)
    h1c_ch = sb("h1c_ch", 128, 2 * 8 * 512, st_p23)

    qd_p = sb("qd_p", 128, T, st_p23)
    kd_p = sb("kd_p", 128, T, st_p23)
    kt_p = sb("kt_p", 128, 32 * 128, st_p23)
    v_p = sb("v_p", 128, 32 * 128, st_p23)
    ploc = sb("ploc", 128, 2 * 128, st_p23)
    vs_t = sb("vs_t", 128, 128, st_p23)
    qs_t = sb("qs_t", 128, 128, st_p23)

    S.wait("gpsimd", "p1_ev")

    for rt in range(4):
        S.wait("sync", "p2mm")
        for cc in range(8):
            S.dma("sync", "p2w", wq_rt[:, cc * 128:(cc + 1) * 128],
                  di["wqC"][:, cc * 512 + rt * 128: cc * 512 + (rt + 1) * 128])
            S.dma("sync", "p2w", wk_rt[:, cc * 128:(cc + 1) * 128],
                  di["wkC"][:, cc * 512 + rt * 128: cc * 512 + (rt + 1) * 128])
            S.dma("sync", "p2w", wkv_rt[:, cc * 256:cc * 256 + 128],
                  di["wkC"][:, cc * 512 + rt * 128: cc * 512 + (rt + 1) * 128])
            S.dma("sync", "p2w", wkv_rt[:, cc * 256 + 128:(cc + 1) * 256],
                  di["wvC"][:, cc * 512 + rt * 128: cc * 512 + (rt + 1) * 128])

        S.wait("scalar", "p3vs")
        S.wait("scalar", "p3qs")

        for tc in range(NCH):
            tp = tc % 2
            S.wait("sync", "p2mm", target=max(0, S.cnt.get("p2mm", 0) - 48))
            if rt == 0 and tc == 0:
                S.wait("sync", "p1st0")
                S.wait("sync", "p1st1")
            for cc in range(8):
                S.dma("sync", "p2h%d" % tp,
                      h1c_ch[:, tp * 4096 + cc * 512:tp * 4096 + (cc + 1) * 512],
                      sc["h1C"][cc * 128:(cc + 1) * 128, tc * 512:(tc + 1) * 512])
            S.wait("tensor", "p2h%d" % tp)
            S.wait("tensor", "p2w")
            S.wait("tensor", "p2ev")
            S.wait("tensor", "p3pl")
            S.wait("tensor", "p3ev")
            S.wait("tensor", "p3st")
            for pj, wbuf in [(0, wq_rt), (1, wk_rt)]:
                for cc in range(8):
                    S.inc("tensor", "p2mm", 1,
                          lambda e, w=wbuf, cc=cc, pj=pj, tp=tp: e.matmul(
                              ps_big[:, pj * 512:(pj + 1) * 512],
                              w[:, cc * 128:(cc + 1) * 128],
                              h1c_ch[:, tp * 4096 + cc * 512:
                                     tp * 4096 + (cc + 1) * 512],
                              start=(cc == 0), stop=(cc == 7)))
            ktv_ps = [ps_mid[:, 0:256], ps_mid[:, 512:768],
                      ps_a[:, 0:256], ps_b[:, 0:256]]
            for ts in range(4):
                for cc in range(8):
                    S.inc("tensor", "p2mm", 1,
                          lambda e, ts=ts, cc=cc, tp=tp: e.matmul(
                              ktv_ps[ts],
                              h1c_ch[:, tp * 4096 + cc * 512 + ts * 128:
                                     tp * 4096 + cc * 512 + (ts + 1) * 128],
                              wkv_rt[:, cc * 256:(cc + 1) * 256],
                              start=(cc == 0), stop=(cc == 7)))
            S.wait("scalar", "p2mm")
            S.inc("scalar", "p2ev", 1,
                  lambda e, tc=tc: e.copy(qd_p[:, tc * 512:(tc + 1) * 512],
                                          ps_big[:, 0:512]))
            S.inc("scalar", "p2ev", 1,
                  lambda e, tc=tc: e.copy(kd_p[:, tc * 512:(tc + 1) * 512],
                                          ps_big[:, 512:1024]))
            for ts in range(4):
                j = tc * 4 + ts
                S.inc("scalar", "p2ev", 1,
                      lambda e, ts=ts, j=j: e.copy(
                          kt_p[:, j * 128:(j + 1) * 128],
                          ktv_ps[ts][:, 0:128]))
                S.inc("scalar", "p2ev", 1,
                      lambda e, ts=ts, j=j: e.copy(
                          v_p[:, j * 128:(j + 1) * 128],
                          ktv_ps[ts][:, 128:256]))
            S.wait("tensor", "p2ev")

        if rt == 0 and "q1d_rt0" in dbg_out:
            S.wait("gpsimd", "p2ev")
            S.dma("gpsimd", "dbg", dbg_out["q1d_rt0"][:, :], qd_p[:, :])
            S.dma("gpsimd", "dbg", dbg_out["k1d_rt0"][:, :], kd_p[:, :])
            S.dma("gpsimd", "dbg", dbg_out["v1_rt0"][:, :], v_p[:, :])
            S.dma("gpsimd", "dbg", dbg_out["kt_rt0"][:, :], kt_p[:, :])

        S.wait("vector", "p2ev")
        S.inc("vector", "p3st", 1,
              lambda e, rt=rt: e.memset(state[:, rt * 64:(rt + 1) * 64], 0.0))

        for j in range(NT):
            jb = (j % 2) * 1024
            S.wait_lag("tensor", "p3pl", 2)
            for hl in (0, 1):
                S.inc("tensor", "p3mm1", 1,
                      lambda e, hl=hl, j=j, jb=jb: e.matmul(
                          ps_big[:, jb + hl * 512:jb + hl * 512 + 128],
                          kd_p[hl * 64:(hl + 1) * 64, j * 128:(j + 1) * 128],
                          qd_p[hl * 64:(hl + 1) * 64, j * 128:(j + 1) * 128],
                          start=True, stop=True))
            S.wait("vector", "p3mm2")
            S.wait("vector", "p3mm1")
            for hl in (0, 1):
                h = 2 * rt + hl
                S.inc("vector", "p3pl", 1,
                      lambda e, hl=hl, h=h, jb=jb: e.tensor_tensor(
                          ploc[:, hl * 128:(hl + 1) * 128],
                          ps_big[:, jb + hl * 512:jb + hl * 512 + 128],
                          cb["d0t"][:, h * 128:(h + 1) * 128], OP.mult))
            S.inc("vector", "p3qs", 1,
                  lambda e, j=j, rt=rt: e.tensor_tensor(
                      qs_t[:, :], qd_p[:, j * 128:(j + 1) * 128],
                      cb["qs"][:, rt * 128:(rt + 1) * 128], OP.mult))
            S.wait("vector", "p3mma")
            S.inc("vector", "p3vs", 1,
                  lambda e, j=j, rt=rt: e.tensor_tensor(
                      vs_t[:, :], v_p[:, j * 128:(j + 1) * 128],
                      cb["u2"][:, rt * 128:(rt + 1) * 128], OP.mult))
            S.wait("tensor", "p3pl")
            S.wait("tensor", "p3qs")
            S.wait("tensor", "p3ev")
            S.wait("tensor", "p3st")
            for hl in range(2):
                S.inc("tensor", "p3mm2", 1,
                      lambda e, hl=hl, j=j: e.matmul(
                          ps_mid[hl * 64:(hl + 1) * 64, 0:128],
                          v_p[:, j * 128 + hl * 64: j * 128 + (hl + 1) * 64],
                          ploc[:, hl * 128:(hl + 1) * 128],
                          start=True, stop=False))
                S.inc("tensor", "p3mm2", 1,
                      lambda e, hl=hl, rt=rt: e.matmul(
                          ps_mid[hl * 64:(hl + 1) * 64, 0:128],
                          state[hl * 64:(hl + 1) * 64, rt * 64:(rt + 1) * 64],
                          qs_t[hl * 64:(hl + 1) * 64, :],
                          start=False, stop=True))
            S.wait("tensor", "p3vs")
            for hl in range(2):
                S.inc("tensor", "p3mma", 1,
                      lambda e, hl=hl, j=j: e.matmul(
                          ps_mid[hl * 64:(hl + 1) * 64, 512:576],
                          kt_p[:, j * 128 + hl * 64: j * 128 + (hl + 1) * 64],
                          vs_t[:, hl * 64:(hl + 1) * 64],
                          start=True, stop=True))
            S.wait("scalar", "p3mm2")
            S.inc("scalar", "p3ev", 1,
                  lambda e, rt=rt, j=j: e.copy(
                      yC[:, rt * T + j * 128:rt * T + (j + 1) * 128],
                      ps_mid[:, 0:128]))
            S.wait("vector", "p3mma")
            S.wait("vector", "p3mm2")
            S.inc("vector", "p3st", 1,
                  lambda e, rt=rt: e.tensor_tensor(
                      state[:, rt * 64:(rt + 1) * 64],
                      state[:, rt * 64:(rt + 1) * 64],
                      cb["c128"][:, rt * 64:(rt + 1) * 64], OP.mult))
            S.inc("vector", "p3st", 1,
                  lambda e, rt=rt: e.tensor_tensor(
                      state[:, rt * 64:(rt + 1) * 64],
                      state[:, rt * 64:(rt + 1) * 64],
                      ps_mid[:, 512:576], OP.add))

    if "yC" in dbg_out:
        S.wait("gpsimd", "p3ev")
        for rt in range(4):
            S.dma("gpsimd", "dbg", dbg_out["yC"][rt * 128:(rt + 1) * 128, :],
                  yC[:, rt * T:(rt + 1) * T])

    if stop_after == "P3":
        with nc.Block() as blk:
            S.run(blk)
        close_all()
        return nc, dbg_out

    st_p23.close()
    S.join(("sync", "tensor", "vector", "scalar", "gpsimd"),
           ("p3mm1", "p3mm2", "p3mma", "ch_vector", "ch_scalar",
            "p2w", "p2h", "p1_ld", "p1_stst"))
    st_p4 = ExitStack()
    regions.append(st_p4)
    ysq = sb("ysq", 128, T, st_p4)
    rowred = sb("rowred", 128, 2, st_p4)
    p4s = sb("p4s", 2, 32, st_p4)
    mean128 = sb("mean128", 128, 4, st_p4)
    rstd128 = sb("rstd128", 128, 4, st_p4)
    ab = sb("ab", 128, 8, st_p4)
    woC_sb = sb("woC_sb", 128, 4 * 1024, st_p4)
    a1tile = sb("a1tile", 128, 1024, st_p4)

    for cc in range(4):
        S.dma("sync", "p4w", woC_sb[:, cc * 1024:(cc + 1) * 1024],
              di["woC"][:, cc * 1024:(cc + 1) * 1024])

    S.wait("scalar", "p3ev")
    S.wait("vector", "p3ev")
    for rt in range(4):
        S.wait("scalar", "p4red")
        S.inc("scalar", "p4sq", 1,
              lambda e, rt=rt: e.activation(ysq[:, :], yC[:, rt * T:(rt + 1) * T],
                                            AF.Square))
        S.wait("vector", "p4smm")
        S.inc("vector", "p4red", 1,
              lambda e, rt=rt: e.reduce_sum(rowred[:, 0:1],
                                            yC[:, rt * T:(rt + 1) * T], axis=AX))
        S.wait("vector", "p4sq")
        S.inc("vector", "p4red", 1,
              lambda e: e.reduce_sum(rowred[:, 1:2], ysq[:, :], axis=AX))
        S.wait("tensor", "p4red")
        S.wait("tensor", "p4sev")
        S.inc("tensor", "p4smm", 1,
              lambda e, rt=rt: e.matmul(ps_a[0:2, 0:2], cb["sel2"][:, :],
                                        rowred[:, 0:2], start=True, stop=True))
        S.wait("scalar", "p4smm")
        S.inc("scalar", "p4sev", 1,
              lambda e, rt=rt: e.copy(p4s[:, rt:rt + 1], ps_a[0:2, 0:1]))
        S.inc("scalar", "p4sev", 1,
              lambda e, rt=rt: e.copy(p4s[:, 4 + rt:5 + rt], ps_a[0:2, 1:2]))
    nfac = 1.0 / (T * 64)
    S.wait("vector", "p4sev")
    S.inc("vector", "p4m", 1,
          lambda e: e.tensor_scalar(p4s[:, 8:12], p4s[:, 0:4], nfac, None, OP.mult))
    S.inc("vector", "p4m", 1,
          lambda e: e.tensor_scalar(p4s[:, 12:16], p4s[:, 4:8], nfac, None, OP.mult))
    S.inc("vector", "p4m", 1,
          lambda e: e.tensor_tensor(p4s[:, 16:20], p4s[:, 8:12], p4s[:, 8:12],
                                    OP.mult))
    S.inc("vector", "p4m", 1,
          lambda e: e.tensor_tensor(p4s[:, 16:20], p4s[:, 12:16], p4s[:, 16:20],
                                    OP.subtract))
    S.inc("vector", "p4m", 1,
          lambda e: e.tensor_scalar(p4s[:, 20:24], p4s[:, 16:20], LN_EPS, None,
                                    OP.add))
    S.inc("vector", "p4m", 1, lambda e: e.reciprocal(p4s[:, 24:28], p4s[:, 20:24]))
    S.wait("scalar", "p4m")
    S.inc("scalar", "p4rs", 1,
          lambda e: e.activation(p4s[:, 28:32], p4s[:, 24:28], AF.Sqrt))
    S.wait("tensor", "p4rs")
    S.wait("tensor", "p4sev")
    S.inc("tensor", "p4bmm", 1,
          lambda e: e.matmul(ps_b[:, 0:4], cb["sel2T"][:, :], p4s[:, 8:12],
                             start=True, stop=True))
    S.inc("tensor", "p4bmm", 1,
          lambda e: e.matmul(ps_mid[:, 0:4], cb["sel2T"][:, :], p4s[:, 28:32],
                             start=True, stop=True))
    S.wait("scalar", "p4bmm")
    S.inc("scalar", "p4bev", 1, lambda e: e.copy(mean128[:, :], ps_b[:, 0:4]))
    S.inc("scalar", "p4bev", 1, lambda e: e.copy(rstd128[:, :], ps_mid[:, 0:4]))
    S.wait("vector", "p4bev")
    S.inc("vector", "p4ab", 1,
          lambda e: e.tensor_tensor(ab[:, 0:4], rstd128[:, :], cb["gnw128"][:, :],
                                    OP.mult))
    S.inc("vector", "p4ab", 1,
          lambda e: e.tensor_tensor(ab[:, 4:8], mean128[:, :], ab[:, 0:4], OP.mult))
    S.inc("vector", "p4ab", 1,
          lambda e: e.tensor_tensor(ab[:, 4:8], cb["gnb128"][:, :], ab[:, 4:8],
                                    OP.subtract))
    p4st_q = [0, 0, 0, 0]
    ygt = sb("ygt", 128, 2 * 128, st_p4)
    for tt in range(NT):
        S.wait("tensor", "p4w")
        S.wait("tensor", "p4ev")
        for cc in range(4):
            par = cc % 2
            S.wait("vector", "p4mm")
            S.wait("vector", "p4ab")
            S.inc("vector", "p4ap", 1,
                  lambda e, tt=tt, cc=cc, par=par: e.tensor_scalar(
                      ygt[:, par * 128:(par + 1) * 128],
                      yC[:, cc * T + tt * 128:cc * T + (tt + 1) * 128],
                      ab[:, cc:cc + 1], ab[:, 4 + cc:5 + cc],
                      OP.mult, OP.add))
            S.wait("tensor", "p4ap")
            for oh in range(2):
                S.inc("tensor", "p4mm", 1,
                      lambda e, oh=oh, cc=cc, par=par: e.matmul(
                          ps_mid[:, oh * 512:(oh + 1) * 512],
                          ygt[:, par * 128:(par + 1) * 128],
                          woC_sb[:, cc * 1024 + oh * 512:cc * 1024 + (oh + 1) * 512],
                          start=(cc == 0), stop=(cc == 3)))
        S.wait("scalar", "p4mm")
        S.wait("scalar", "p4st")
        S.inc("scalar", "p4ev", 1,
              lambda e: e.copy(a1tile[:, :], ps_mid[:, :]))
        S.wait("gpsimd", "p4ev")
        S.dma("gpsimd", "p4st", sc["cc_a1in"][tt * 128:(tt + 1) * 128, :],
              a1tile[:, :])
        if tt % 8 == 7:
            p4st_q[tt // 8] = S.cnt["p4st"]

    if "attn1p" in dbg_out:
        S.wait("gpsimd", "p4st")
        S.dma("gpsimd", "dbg", dbg_out["attn1p"][:, :], sc["cc_a1in"][:, :])

    if stop_after == "P4":
        with nc.Block() as blk:
            S.run(blk)
        close_all()
        return nc, dbg_out

    st_p4.close()
    st_y.close()
    S.join(ENGS, ("p4mm", "p4red", "p4sq", "ch_vector", "ch_scalar"))
    gst = {}

    def pid_reg(e):
        if "pid" not in gst:
            gst["pid"] = e.to_reg(e.partition_id())
        return gst["pid"]

    def partner_copy(ccname, dsts_srcs_fn):
        pairs0 = dsts_srcs_fn(0)
        n = len(pairs0)
        S.sem(ccname)
        S.cnt[ccname] += 16 * n
        S.names[ccname] = (ccname, S.cnt[ccname])
        h = S.h[ccname]

        def go(e, fn=dsts_srcs_fn, h=h):
            reg = pid_reg(e)
            for j in range(8):
                with e.If_eq(reg, j):
                    for dst, srcap in fn(j):
                        e.dma_start(out=dst, in_=srcap).then_inc(h, 16)
        S.emit("gpsimd", go)

    for qq, onm in enumerate(("A", "B", "C", "D")):
        S.wait("gpsimd", "p4st", target=p4st_q[qq])
        S.inc("gpsimd", "p5cc" + onm, 1,
              lambda e, qq=qq, onm=onm: e.collective_compute(
                  "AllGather", OP.bypass, replica_groups=[list(range(8))],
                  ins=[sc["cc_a1in"][qq * 1024:(qq + 1) * 1024, :].opt()],
                  outs=[sc["cc_a1out" + onm][:, :, :].opt()]))
        S.wait("gpsimd", "p5cc" + onm)
        partner_copy("p5cp" + onm,
                     lambda j, qq=qq, onm=onm: [
                         (sc["a1p"][qq * 1024:(qq + 1) * 1024, :],
                          sc["cc_a1out" + onm][j ^ 1])])

    st_p5 = ExitStack()
    regions.append(st_p5)
    g1bc = sb("g1bc", 128, C, st_p5)
    b1bc = sb("b1bc", 128, C, st_p5)
    p5a = sb("p5_a", 128, 2 * C, st_p5)
    p5b = sb("p5_b", 128, 2 * C, st_p5)
    p5c = sb("p5_c", 128, 2 * C, st_p5)
    p5d = sb("p5_d", 128, 2 * C, st_p5)
    p5e = sb("p5_e", 128, 2 * C, st_p5)
    sq5 = sb("p5_sqd", 128, C, st_p5)
    ln5 = sb("p5_lnb", 128, 12 * NT, st_p5)
    tp5 = sb("p5_tp", 128, 2 * 128, st_p5)

    emit_bcast(g1bc, "ln1g")
    emit_bcast(b1bc, "ln1b")

    def p5_finalize(pfx, sums, sq, negmean, rstd, w0, w1):
        S.wait("vector", pfx + "sq")
        S.inc("vector", pfx + "fz", 1,
              lambda e: e.tensor_scalar(negmean, sums, -1.0 / C, None, OP.mult))
        S.inc("vector", pfx + "fz", 1,
              lambda e: e.tensor_scalar(w0, sq, 1.0 / C, None, OP.mult))
        S.inc("vector", pfx + "fz", 1,
              lambda e: e.tensor_tensor(w1, negmean, negmean, OP.mult))
        S.inc("vector", pfx + "fz", 1,
              lambda e: e.tensor_tensor(w1, w0, w1, OP.subtract))
        S.inc("vector", pfx + "fz", 1,
              lambda e: e.tensor_scalar(w0, w1, LN_EPS, None, OP.add))
        S.inc("vector", pfx + "fz", 1, lambda e: e.reciprocal(w1, w0))
        S.wait("scalar", pfx + "fz")
        S.inc("scalar", pfx + "rst", 1, lambda e: e.activation(rstd, w1, AF.Sqrt))

    for i in range(NT):
        par = i % 2
        S.wait("sync", "p5cp" + "ABCD"[i // 8])
        S.wait_lag("sync", "p5tt", 2)
        S.dma("sync", "p5ld%d" % par, p5a[:, par * C:(par + 1) * C],
              di["x"][i * 128:(i + 1) * 128, :])
        S.dma("sync", "p5ld%d" % par, p5b[:, par * C:(par + 1) * C],
              sc["cc_a1in"][i * 128:(i + 1) * 128, :])
        S.dma("sync", "p5ld%d" % par, p5c[:, par * C:(par + 1) * C],
              sc["a1p"][i * 128:(i + 1) * 128, :])
        S.wait("vector", "p5ld%d" % par)
        S.wait("vector", "p5x2st%d" % par)
        S.wait_lag("vector", "p5xsq", 1)
        S.inc("vector", "p5tt", 1,
              lambda e, par=par: e.tensor_tensor(
                  p5d[:, par * C:(par + 1) * C], p5b[:, par * C:(par + 1) * C],
                  p5c[:, par * C:(par + 1) * C], OP.add))
        S.inc("vector", "p5tt", 1,
              lambda e, par=par: e.tensor_tensor(
                  p5d[:, par * C:(par + 1) * C], p5a[:, par * C:(par + 1) * C],
                  p5d[:, par * C:(par + 1) * C], OP.add))
        S.inc("vector", "p5xrs", 1,
              lambda e, par=par, i=i: e.reduce_sum(
                  ln5[:, i:i + 1], p5d[:, par * C:(par + 1) * C], axis=AX))
        S.wait("scalar", "p5tt")
        S.inc("scalar", "p5xsq", 1,
              lambda e, par=par, i=i: e.activation(
                  sq5[:, :], p5d[:, par * C:(par + 1) * C], AF.Square,
                  accum_out=ln5[:, NT + i:NT + i + 1]))
        S.wait("gpsimd", "p5tt")
        S.dma("gpsimd", "p5x2st%d" % par, sc["x2"][i * 128:(i + 1) * 128, :],
              p5d[:, par * C:(par + 1) * C])
    p5_finalize("p5x", ln5[:, 0:NT], ln5[:, NT:2 * NT], ln5[:, 2 * NT:3 * NT],
                ln5[:, 3 * NT:4 * NT], ln5[:, 8 * NT:9 * NT],
                ln5[:, 9 * NT:10 * NT])
    S.wait("sync", "p5tt")
    S.wait("sync", "p5xrs")
    S.wait("sync", "p5xsq")
    for i in range(NT):
        par = i % 2
        S.wait_lag("sync", "p5ap", 1)
        S.wait("sync", "p5x2st%d" % par)
        S.dma("sync", "p5ld2%d" % par, p5d[:, par * C:(par + 1) * C],
              sc["x2"][i * 128:(i + 1) * 128, :])
        S.wait("vector", "p5ld2%d" % par)
        S.wait("vector", "p5xrst")
        S.wait("vector", "p5h2st%d" % par)
        S.wait_lag("vector", "p5ysq", 1)
        S.inc("vector", "p5ap", 1,
              lambda e, i=i, par=par: e.tensor_scalar(
                  p5e[:, par * C:(par + 1) * C], p5d[:, par * C:(par + 1) * C],
                  ln5[:, 2 * NT + i:2 * NT + i + 1],
                  ln5[:, 3 * NT + i:3 * NT + i + 1], OP.add, OP.mult))
        S.inc("vector", "p5yrs", 1,
              lambda e, par=par, i=i: e.reduce_sum(
                  ln5[:, 4 * NT + i:4 * NT + i + 1],
                  p5e[:, par * C:(par + 1) * C], axis=AX))
        S.wait("scalar", "p5ap")
        S.inc("scalar", "p5ysq", 1,
              lambda e, par=par, i=i: e.activation(
                  sq5[:, :], p5e[:, par * C:(par + 1) * C], AF.Square,
                  accum_out=ln5[:, 5 * NT + i:5 * NT + i + 1]))
        S.wait("gpsimd", "p5ap")
        S.dma("gpsimd", "p5h2st%d" % par, sc["h2"][i * 128:(i + 1) * 128, :],
              p5e[:, par * C:(par + 1) * C])
    p5_finalize("p5y", ln5[:, 4 * NT:5 * NT], ln5[:, 5 * NT:6 * NT],
                ln5[:, 6 * NT:7 * NT], ln5[:, 7 * NT:8 * NT],
                ln5[:, 8 * NT:9 * NT], ln5[:, 9 * NT:10 * NT])
    S.wait("sync", "p5ap")
    S.wait("sync", "p5yrs")
    S.wait("sync", "p5ysq")
    for i in range(NT):
        par = i % 2
        S.wait_lag("sync", "p5af", 1)
        S.wait("sync", "p5h2st%d" % par)
        S.dma("sync", "p5ld3%d" % par, p5d[:, par * C:(par + 1) * C],
              sc["h2"][i * 128:(i + 1) * 128, :])
        S.wait("vector", "p5ld3%d" % par)
        S.wait("vector", "p5yrst")
        S.wait("vector", "p5_mm")
        S.wait("vector", "p0b_ev")
        S.inc("vector", "p5af", 1,
              lambda e, i=i, par=par: e.tensor_scalar(
                  p5e[:, par * C:(par + 1) * C], p5d[:, par * C:(par + 1) * C],
                  ln5[:, 6 * NT + i:6 * NT + i + 1],
                  ln5[:, 7 * NT + i:7 * NT + i + 1], OP.add, OP.mult))
        S.inc("vector", "p5af", 1,
              lambda e, par=par: e.tensor_tensor(
                  p5e[:, par * C:(par + 1) * C], p5e[:, par * C:(par + 1) * C],
                  g1bc[:, :], OP.mult))
        S.inc("vector", "p5af", 1,
              lambda e, par=par: e.tensor_tensor(
                  p5e[:, par * C:(par + 1) * C], p5e[:, par * C:(par + 1) * C],
                  b1bc[:, :], OP.add))
        for ccj in range(8):
            tp = ccj % 2
            tps = ps_a if tp == 0 else ps_b
            S.wait("tensor", "p5af")
            S.wait_lag("tensor", "p5_ev", 1)
            S.inc("tensor", "p5_mm", 1,
                  lambda e, ccj=ccj, tps=tps, par=par: e.transpose(
                      tps[:, 0:128],
                      p5e[:, par * C + ccj * 128:par * C + (ccj + 1) * 128],
                      cb["ident"][:, :]))
            S.wait("scalar", "p5_mm")
            S.wait("scalar", "p5st%d" % tp)
            S.inc("scalar", "p5_ev", 1,
                  lambda e, tp=tp, tps=tps: e.copy(
                      tp5[:, tp * 128:(tp + 1) * 128], tps[:, 0:128]))
            S.wait("gpsimd", "p5_ev")
            S.dma("gpsimd", "p5st%d" % tp,
                  sc["h3C"][ccj * 128:(ccj + 1) * 128, i * 128:(i + 1) * 128],
                  tp5[:, tp * 128:(tp + 1) * 128])

    if "x2" in dbg_out:
        S.wait("gpsimd", "p5x2st0")
        S.wait("gpsimd", "p5x2st1")
        S.wait("gpsimd", "p5h2st0")
        S.wait("gpsimd", "p5h2st1")
        S.dma("gpsimd", "dbg", dbg_out["x2"][:, :], sc["x2"][:, :])
        S.dma("gpsimd", "dbg", dbg_out["h2"][:, :], sc["h2"][:, :])
    if "h3C" in dbg_out:
        S.wait("gpsimd", "p5st0")
        S.wait("gpsimd", "p5st1")
        for cc in range(8):
            S.dma("gpsimd", "dbg", dbg_out["h3C"][cc * 128:(cc + 1) * 128, :],
                  sc["h3C"][cc * 128:(cc + 1) * 128, :])

    if stop_after == "P5":
        with nc.Block() as blk:
            S.run(blk)
        close_all()
        return nc, dbg_out

    st_p5.close()
    S.join(ENGS, ("p5_mm", "p5_ev", "p5st0", "p5st1", "p5ld0", "p5ld1",
                  "p5ld20", "p5ld21", "p5ld30", "p5ld31",
                  "p5x2st0", "p5x2st1", "p5h2st0", "p5h2st1",
                  "ch_vector", "ch_scalar"))
    st_p6 = ExitStack()
    regions.append(st_p6)
    crep_sb = sb("crep_sb", 128, T, st_p6)
    srep_sb = sb("srep_sb", 128, T, st_p6)
    wqkv_sb = sb("wqkv_sb", 128, 8 * 1536, st_p6)
    h3c_ch6 = sb("h3c_ch6", 128, 2 * 8 * 512, st_p6)
    qraw = sb("qraw", 128, 4 * 512, st_p6)
    rt1 = sb("rt1", 128, 512, st_p6)
    rt2 = sb("rt2", 128, 512, st_p6)
    ro1 = sb("ro1", 128, 512, st_p6)
    v2at = sb("v2at6", 128, 8 * 65, st_p6)

    S.dma("sync", "p6c", crep_sb[:, :], di["crep"][:, :])
    S.dma("sync", "p6c", srep_sb[:, :], di["srep"][:, :])
    for cc in range(8):
        S.dma("sync", "p6c", wqkv_sb[:, cc * 1536:(cc + 1) * 1536],
              di["WqkvC"][:, cc * 1536:(cc + 1) * 1536])
    S.inc("vector", "p6one", 1,
          lambda e: e.memset(v2at[:, :].rearrange("a (h c) -> a h c", c=65)[:, :, 64:65], 1.0))

    def rope_phase(base, dst):
        for tc in range(NCH):
            pass

    for tc in range(NCH):
        t6 = tc % 2
        S.wait("sync", "p6mm", target=max(0, S.cnt.get("p6mm", 0) - 96))
        for cc in range(8):
            S.dma("sync", "p6h%d" % t6,
                  h3c_ch6[:, t6 * 4096 + cc * 512:t6 * 4096 + (cc + 1) * 512],
                  sc["h3C"][cc * 128:(cc + 1) * 128, tc * 512:(tc + 1) * 512])
        for base, dst in ((0, sc["q2r"]), (512, sc["k2r"])):
            S.wait("tensor", "p6h%d" % t6)
            S.wait("tensor", "p6c")
            S.wait("tensor", "p6ev")
            for rb in range(4):
                for cc in range(8):
                    S.inc("tensor", "p6mm", 1,
                          lambda e, rb=rb, cc=cc, base=base, t6=t6: e.matmul(
                              ps_big[:, rb * 512:(rb + 1) * 512],
                              wqkv_sb[:, cc * 1536 + base + rb * 128:
                                      cc * 1536 + base + (rb + 1) * 128],
                              h3c_ch6[:, t6 * 4096 + cc * 512:
                                      t6 * 4096 + (cc + 1) * 512],
                              start=(cc == 0), stop=(cc == 7)))
            S.wait("scalar", "p6mm")
            S.wait("scalar", "p6rp")
            for rb in range(4):
                S.inc("scalar", "p6ev", 1,
                      lambda e, rb=rb: e.copy(qraw[:, rb * 512:(rb + 1) * 512],
                                              ps_big[:, rb * 512:(rb + 1) * 512]))
            for p in range(2):
                cs = crep_sb[:, tc * 512:(tc + 1) * 512]
                sn = srep_sb[:, tc * 512:(tc + 1) * 512]
                S.wait("vector", "p6ev")
                S.wait("vector", "p6vst")
                S.inc("vector", "p6rp", 1,
                      lambda e, p=p, cs=cs: e.tensor_tensor(
                          rt1[:, :], qraw[:, p * 512:(p + 1) * 512], cs, OP.mult))
                S.inc("vector", "p6rp", 1,
                      lambda e, p=p, sn=sn: e.tensor_tensor(
                          rt2[:, :], qraw[:, (p + 2) * 512:(p + 3) * 512], sn,
                          OP.mult))
                S.wait("vector", "p6vst")
                S.inc("vector", "p6rp", 1,
                      lambda e: e.tensor_tensor(ro1[:, :], rt1[:, :], rt2[:, :],
                                                OP.subtract))
                S.wait("gpsimd", "p6rp")
                for hi in range(4):
                    S.dma("gpsimd", "p6vst",
                          dst[(p * 4 + hi) * 64 + 0:(p * 4 + hi) * 64 + 32,
                              tc * 512:(tc + 1) * 512],
                          ro1[hi * 32:(hi + 1) * 32, :])
                S.inc("vector", "p6rp", 1,
                      lambda e, p=p, sn=sn: e.tensor_tensor(
                          rt1[:, :], qraw[:, p * 512:(p + 1) * 512], sn, OP.mult))
                S.inc("vector", "p6rp", 1,
                      lambda e, p=p, cs=cs: e.tensor_tensor(
                          rt2[:, :], qraw[:, (p + 2) * 512:(p + 3) * 512], cs,
                          OP.mult))
                S.wait("vector", "p6vst")
                S.inc("vector", "p6rp", 1,
                      lambda e: e.tensor_tensor(ro1[:, :], rt1[:, :], rt2[:, :],
                                                OP.add))
                S.wait("gpsimd", "p6rp")
                for hi in range(4):
                    S.dma("gpsimd", "p6vst",
                          dst[(p * 4 + hi) * 64 + 32:(p * 4 + hi) * 64 + 64,
                              tc * 512:(tc + 1) * 512],
                          ro1[hi * 32:(hi + 1) * 32, :])
        S.wait("tensor", "p6ev")
        for ts in range(4):
            for cc in range(8):
                S.inc("tensor", "p6mm", 1,
                      lambda e, ts=ts, cc=cc, t6=t6: e.matmul(
                          ps_big[:, ts * 512:(ts + 1) * 512],
                          h3c_ch6[:, t6 * 4096 + cc * 512 + ts * 128:
                                  t6 * 4096 + cc * 512 + (ts + 1) * 128],
                          wqkv_sb[:, cc * 1536 + 1024:cc * 1536 + 1536],
                          start=(cc == 0), stop=(cc == 7)))
        for ts in range(4):
            S.wait("scalar", "p6mm")
            S.wait("scalar", "p6vst")
            S.wait("scalar", "p6one")
            S.inc("scalar", "p6ev", 1,
                  lambda e, ts=ts: e.copy(
                      v2at[:, :].rearrange("a (h c) -> a h c", c=65)[:, :, 0:64],
                      ps_big[:, ts * 512:(ts + 1) * 512].rearrange(
                          "a (h c) -> a h c", c=64)))
            S.wait("gpsimd", "p6ev")
            S.dma("gpsimd", "p6vst",
                  sc["v2a"][tc * 512 + ts * 128:tc * 512 + (ts + 1) * 128, :, :],
                  v2at[:, :].rearrange("a (h c) -> a h c", c=65))

    if "q2r" in dbg_out:
        S.wait("gpsimd", "p6vst")
        for rr in range(4):
            S.dma("gpsimd", "dbg", dbg_out["q2r"][rr * 128:(rr + 1) * 128, :],
                  sc["q2r"][rr * 128:(rr + 1) * 128, :])
            S.dma("gpsimd", "dbg", dbg_out["k2r"][rr * 128:(rr + 1) * 128, :],
                  sc["k2r"][rr * 128:(rr + 1) * 128, :])
        S.dma("gpsimd", "dbg", dbg_out["v2a"][:, :],
              sc["v2a"][:, :, :].rearrange("a h c -> a (h c)"))

    if stop_after == "P6":
        with nc.Block() as blk:
            S.run(blk)
        close_all()
        return nc, dbg_out

    st_p6.close()
    S.join(ENGS, ("p6mm", "p6ev", "p6rp", "p6vst", "p6h", "p6c",
                  "ch_vector", "ch_scalar"))
    st_p7 = ExitStack()
    regions.append(st_p7)
    dmask_sb = sb("dmask_sb", 128, 4 * 512, st_p7)
    ebias = sb("ebias", 128, 1, st_p7)
    wproj_sb = sb("wproj_sb", 128, 4 * 1024, st_p7)
    q2h = sb("q2h", 128, T, st_p7)
    k2h = sb("k2h", 128, T, st_p7)
    v2h = sb("v2h", 128, 2 * 32 * 65, st_p7)
    pexp = sb("pexp", 128, 2048, st_p7)
    rrow = sb("rrow", 1, 512, st_p7)
    rbc = sb("rbc", 64, 512, st_p7)
    rbc2 = sb("rbc2", 64, 512, st_p7)
    attd = sb("attd", 64, 512, st_p7)
    atile = sb("atile", 128, 2 * 128, st_p7)
    a2tile = sb("a2tile", 128, 1024, st_p7)

    S.dma("sync", "p7c", dmask_sb[:, :], di["dmask"][:, :])
    S.inc("vector", "p7eb", 1, lambda e: e.memset(ebias[:, :], -EXPC))
    for cc in range(4):
        S.dma("sync", "p7c", wproj_sb[:, cc * 1024:(cc + 1) * 1024],
              di["WprojR"][:, cc * 1024:(cc + 1) * 1024])

    for h in range(HL):
        hp = h % 2
        nmm1 = 2 * sum(range(2, 2 * NCH + 2, 2))
        S.wait("sync", "p7mm1",
               target=max(0, S.cnt.get("p7mm1", 0) - nmm1))
        S.wait("sync", "p7mm2",
               target=max(0, S.cnt.get("p7mm2", 0) - nmm1))
        S.dma("sync", "p7ldh%d" % hp, q2h[hp * 64:(hp + 1) * 64, :],
              sc["q2r"][h * 64:(h + 1) * 64, :])
        S.dma("sync", "p7ldh%d" % hp, k2h[hp * 64:(hp + 1) * 64, :],
              sc["k2r"][h * 64:(h + 1) * 64, :])
        S.dma("sync", "p7ldh%d" % hp,
              v2h[:, hp * 2080:(hp + 1) * 2080].rearrange(
                  "a (j c) -> a j c", c=65),
              sc["v2a"][:, h, :].rearrange("(j p) c -> p j c", p=128))
        for qc in range(NCH):
            ngr = 2 * (qc + 1)

            def mm1_group(g, h=h, qc=qc, hp=hp):
                st = g % 2
                S.wait("tensor", "p7ldh%d" % hp)
                S.wait_lag("tensor", "p7ex", 1)
                for kj2 in range(2):
                    f = 2 * g + kj2
                    S.inc("tensor", "p7mm1", 1,
                          lambda e, qc=qc, f=f, st=st, kj2=kj2, hp=hp: e.matmul(
                              ps_big[:, st * 1024 + kj2 * 512:
                                     st * 1024 + (kj2 + 1) * 512],
                              k2h[hp * 64:(hp + 1) * 64, f * 128:(f + 1) * 128],
                              q2h[hp * 64:(hp + 1) * 64,
                                  qc * 512:(qc + 1) * 512],
                              start=True, stop=True))

            yb = qc % 2
            psY = ps_mid[0:65, yb * 512:(yb + 1) * 512]
            mm1_group(0)
            for g in range(ngr):
                st = g % 2
                S.wait("scalar", "p7mm1")
                S.wait("scalar", "p7mm2",
                       target=max(0, S.cnt.get("p7mm2", 0) - 2))
                S.wait("scalar", "p7msk")
                S.wait("scalar", "p7eb")
                S.inc("scalar", "p7ex", 1,
                      lambda e, st=st: e.activation(
                          pexp[:, st * 1024:(st + 1) * 1024],
                          ps_big[:, st * 1024:(st + 1) * 1024], AF.Exp,
                          bias=ebias[:, 0:1], scale=0.125))
                diag = (2 * g >= 4 * qc)
                if diag:
                    mo = (2 * g - 4 * qc) * 512
                    S.wait("vector", "p7ex")
                    S.wait("vector", "p7c")
                    S.inc("vector", "p7msk", 1,
                          lambda e, st=st, mo=mo: e.tensor_tensor(
                              pexp[:, st * 1024:(st + 1) * 1024],
                              pexp[:, st * 1024:(st + 1) * 1024],
                              dmask_sb[:, mo:mo + 1024], OP.mult))
                if g + 1 < ngr:
                    mm1_group(g + 1)
                if diag:
                    S.wait("tensor", "p7msk")
                S.wait("tensor", "p7ex")
                if g == 0:
                    S.wait_lag("tensor", "p7bcv", 1)
                for kj2 in range(2):
                    f = 2 * g + kj2
                    S.inc("tensor", "p7mm2", 1,
                          lambda e, f=f, st=st, kj2=kj2, g=g, ngr=ngr, psY=psY,
                          hp=hp: e.matmul(
                              psY,
                              v2h[:, hp * 2080 + f * 65:
                                  hp * 2080 + (f + 1) * 65],
                              pexp[:, st * 1024 + kj2 * 512:
                                   st * 1024 + (kj2 + 1) * 512],
                              start=(g == 0 and kj2 == 0),
                              stop=(g == ngr - 1 and kj2 == 1)))
            S.wait("scalar", "p7mm2")
            S.wait("scalar", "p7bc")
            S.inc("scalar", "p7rr", 1,
                  lambda e, psY=psY: e.copy(rrow[:, :], psY[64:65, 0:512]))
            S.wait("tensor", "p7rr")
            S.wait("tensor", "p7bce")
            S.inc("tensor", "p7bc", 1,
                  lambda e: e.matmul(ps_b[0:64, 0:512], cb["ones1_64"][:, :],
                                     rrow[:, :], start=True, stop=True))
            S.wait("scalar", "p7bc")
            S.wait("scalar", "p7rcp")
            S.inc("scalar", "p7bce", 1,
                  lambda e: e.copy(rbc[:, :], ps_b[0:64, 0:512]))
            S.wait("vector", "p7bce")
            S.wait("vector", "p7rcp")
            S.inc("vector", "p7rcp", 1,
                  lambda e: e.reciprocal(rbc2[:, :], rbc[:, :]))
            S.wait("vector", "p7mm2")
            S.wait("vector", "p7ast")
            S.inc("vector", "p7bcv", 1,
                  lambda e, psY=psY: e.tensor_tensor(
                      attd[:, :], psY[0:64, 0:512], rbc2[:, :], OP.mult))
            S.wait("gpsimd", "p7bcv")
            S.dma("gpsimd", "p7ast",
                  sc["att"][h * 64:(h + 1) * 64, qc * 512:(qc + 1) * 512],
                  attd[:, :])

    if "att" in dbg_out:
        S.wait("gpsimd", "p7ast")
        for rr in range(4):
            S.dma("gpsimd", "dbg", dbg_out["att"][rr * 128:(rr + 1) * 128, :],
                  sc["att"][rr * 128:(rr + 1) * 128, :])

    S.wait("sync", "p7ast")
    for tt in range(NT):
        for cc in range(4):
            par = cc % 2
            S.wait("sync", "p7pmm")
            S.dma("sync", "p7pld", atile[:, par * 128:(par + 1) * 128],
                  sc["att"][cc * 128:(cc + 1) * 128, tt * 128:(tt + 1) * 128])
            S.wait("tensor", "p7pld")
            S.wait("tensor", "p7pev")
            for oh in range(2):
                S.inc("tensor", "p7pmm", 1,
                      lambda e, cc=cc, oh=oh, par=par: e.matmul(
                          ps_mid[:, oh * 512:(oh + 1) * 512],
                          atile[:, par * 128:(par + 1) * 128],
                          wproj_sb[:, cc * 1024 + oh * 512:
                                   cc * 1024 + (oh + 1) * 512],
                          start=(cc == 0), stop=(cc == 3)))
        S.wait("scalar", "p7pmm")
        S.wait("scalar", "p7pst")
        S.inc("scalar", "p7pev", 1,
              lambda e: e.copy(a2tile[:, :], ps_mid[:, :]))
        S.wait("gpsimd", "p7pev")
        S.dma("gpsimd", "p7pst", sc["cc_a2in"][tt * 128:(tt + 1) * 128, :],
              a2tile[:, :])
        if tt == 15:
            p7pst_half = S.cnt["p7pst"]

    if "attn2p" in dbg_out:
        S.wait("gpsimd", "p7pst")
        S.dma("gpsimd", "dbg", dbg_out["attn2p"][:, :], sc["cc_a2in"][:, :])

    if stop_after == "P7":
        with nc.Block() as blk:
            S.run(blk)
        close_all()
        return nc, dbg_out

    st_p7.close()
    S.join(ENGS, ("p7mm1", "p7mm2", "p7pmm", "p7pld", "p7ldh0", "p7ldh1", "p7pst",
                  "p7ast", "p7c", "ch_vector", "ch_scalar"))

    S.wait("gpsimd", "p7pst", target=p7pst_half)
    S.inc("gpsimd", "p8ccA", 1,
          lambda e: e.collective_compute(
              "AllGather", OP.bypass, replica_groups=[list(range(8))],
              ins=[sc["cc_a2in"][0:TH, :].opt()],
              outs=[sc["cc_a2outA"][:, :, :].opt()]))
    S.wait("gpsimd", "p8ccA")

    def p8pairsA(j):
        hg = j % 2
        sl = slice(hg * TH, (hg + 1) * TH)
        pairs = [(sc["h2own"][:, :], sc["h2"][sl, :]),
                 (sc["x2own"][:, :], sc["x2"][sl, :])]
        if hg == 0:
            pairs.append((sc["a2p"][0:TH, :], sc["cc_a2outA"][j ^ 1]))
        else:
            pairs.append((sc["a1p"][0:TH, :], sc["cc_a2in"][sl, :]))
        return pairs
    partner_copy("p8cpA", p8pairsA)
    S.wait("gpsimd", "p7pst")
    S.inc("gpsimd", "p8ccB", 1,
          lambda e: e.collective_compute(
              "AllGather", OP.bypass, replica_groups=[list(range(8))],
              ins=[sc["cc_a2in"][TH:T, :].opt()],
              outs=[sc["cc_a2outB"][:, :, :].opt()]))
    S.wait("gpsimd", "p8ccB")

    def p8pairsB(j):
        hg = j % 2
        sl = slice(hg * TH, (hg + 1) * TH)
        if hg == 0:
            return [(sc["a1p"][0:TH, :], sc["cc_a2in"][sl, :])]
        return [(sc["a2p"][0:TH, :], sc["cc_a2outB"][j ^ 1])]
    partner_copy("p8cpB", p8pairsB)

    gates_all = sb("gates_all", 128, 16 * 8)
    gk_all = sb("gk_all", 128, 16 * 8)

    st_p8 = ExitStack()
    regions.append(st_p8)
    h4C = sb("h4C", 128, 8 * TH, st_p8)
    g2bc = sb("g2bc", 128, C, st_p8)
    b2bc = sb("b2bc", 128, C, st_p8)
    p8a = sb("p8_a", 128, 2 * C, st_p8)
    p8b = sb("p8_b", 128, 2 * C, st_p8)
    p8c = sb("p8_c", 128, 2 * C, st_p8)
    p8d = sb("p8_d", 128, 2 * C, st_p8)
    p8e = sb("p8_e", 128, 2 * C, st_p8)
    sq8 = sb("p8_sqd", 128, C, st_p8)
    ln8 = sb("p8_lnb", 128, 8 * 16, st_p8)
    tp8 = sb("p8_tp", 128, 2 * 128, st_p8)
    wg_sb = sb("wg_sb", 128, 8 * 8, st_p8)
    gl8 = sb("p8_gl", 128, 8, st_p8)
    ge8 = sb("p8_ge", 128, 8, st_p8)
    gs8 = sb("p8_gs", 128, 4, st_p8)
    gm8 = sb("p8_gm", 128, 8, st_p8)
    mk8 = sb("p8_mk", 128, 8, st_p8)
    w8a = sb("p8_w8a", 128, 16, st_p8)
    w8b = sb("p8_w8b", 128, 16, st_p8)

    S.dma("sync", "p8c", wg_sb[:, :], di["WgateC"][:, :])
    emit_bcast(g2bc, "ln2g")
    emit_bcast(b2bc, "ln2b")
    NTH = TH // 128
    for i in range(NTH):
        par = i % 2
        S.wait("sync", "p8cpA")
        S.wait("sync", "p8cpB")
        S.wait_lag("sync", "p8tt", 2)
        S.dma("sync", "p8ld%d" % par, p8a[:, par * C:(par + 1) * C],
              sc["h2own"][i * 128:(i + 1) * 128, :])
        S.dma("sync", "p8ld%d" % par, p8b[:, par * C:(par + 1) * C],
              sc["a1p"][i * 128:(i + 1) * 128, :])
        S.dma("sync", "p8ld%d" % par, p8c[:, par * C:(par + 1) * C],
              sc["a2p"][i * 128:(i + 1) * 128, :])
        S.wait("vector", "p8ld%d" % par)
        S.wait("vector", "p8x3st%d" % par)
        S.wait_lag("vector", "p8xsq", 1)
        S.inc("vector", "p8tt", 1,
              lambda e, par=par: e.tensor_tensor(
                  p8d[:, par * C:(par + 1) * C], p8b[:, par * C:(par + 1) * C],
                  p8c[:, par * C:(par + 1) * C], OP.add))
        S.inc("vector", "p8tt", 1,
              lambda e, par=par: e.tensor_tensor(
                  p8d[:, par * C:(par + 1) * C], p8a[:, par * C:(par + 1) * C],
                  p8d[:, par * C:(par + 1) * C], OP.add))
        S.inc("vector", "p8xrs", 1,
              lambda e, par=par, i=i: e.reduce_sum(
                  ln8[:, i:i + 1], p8d[:, par * C:(par + 1) * C], axis=AX))
        S.wait("scalar", "p8tt")
        S.inc("scalar", "p8xsq", 1,
              lambda e, par=par, i=i: e.activation(
                  sq8[:, :], p8d[:, par * C:(par + 1) * C], AF.Square,
                  accum_out=ln8[:, 16 + i:16 + i + 1]))
        S.wait("gpsimd", "p8tt")
        S.dma("gpsimd", "p8x3st%d" % par, sc["x3"][i * 128:(i + 1) * 128, :],
              p8d[:, par * C:(par + 1) * C])
    p5_finalize("p8x", ln8[:, 0:16], ln8[:, 16:32], ln8[:, 32:48],
                ln8[:, 48:64], w8a[:, :], w8b[:, :])
    S.wait("sync", "p8tt")
    S.wait("sync", "p8xrs")
    S.wait("sync", "p8xsq")
    for i in range(NTH):
        par = i % 2
        S.wait_lag("sync", "p8af", 4)
        S.wait("sync", "p8x3st%d" % par)
        S.dma("sync", "p8ld2%d" % par, p8d[:, par * C:(par + 1) * C],
              sc["x3"][i * 128:(i + 1) * 128, :])
        S.wait("vector", "p8ld2%d" % par)
        S.wait("vector", "p8xrst")
        S.wait("vector", "p8_mm")
        S.wait("vector", "p8h4st%d" % par)
        S.wait("vector", "p0b_ev")
        S.inc("vector", "p8af", 1,
              lambda e, i=i, par=par: e.tensor_scalar(
                  p8e[:, par * C:(par + 1) * C], p8d[:, par * C:(par + 1) * C],
                  ln8[:, 32 + i:32 + i + 1], ln8[:, 48 + i:48 + i + 1],
                  OP.add, OP.mult))
        S.inc("vector", "p8af", 1,
              lambda e, par=par: e.tensor_tensor(
                  p8e[:, par * C:(par + 1) * C], p8e[:, par * C:(par + 1) * C],
                  g2bc[:, :], OP.mult))
        S.inc("vector", "p8af", 1,
              lambda e, par=par: e.tensor_tensor(
                  p8e[:, par * C:(par + 1) * C], p8e[:, par * C:(par + 1) * C],
                  b2bc[:, :], OP.add))
        S.wait("gpsimd", "p8af")
        S.dma("gpsimd", "p8h4st%d" % par,
              sc["h4tok"][i * 128:(i + 1) * 128, :], p8e[:, par * C:(par + 1) * C])
        for ccj in range(8):
            tp = ccj % 2
            tps = ps_a if tp == 0 else ps_b
            S.wait("tensor", "p8af")
            S.wait_lag("tensor", "p8_ev", 1)
            S.inc("tensor", "p8_mm", 1,
                  lambda e, ccj=ccj, tps=tps, par=par: e.transpose(
                      tps[:, 0:128],
                      p8e[:, par * C + ccj * 128:par * C + (ccj + 1) * 128],
                      cb["ident"][:, :]))
            S.wait("scalar", "p8_mm")
            S.inc("scalar", "p8_ev", 1,
                  lambda e, ccj=ccj, i=i, tps=tps: e.copy(
                      h4C[:, ccj * TH + i * 128:ccj * TH + (i + 1) * 128],
                      tps[:, 0:128]))
        S.wait("tensor", "p8_ev")
        S.wait("tensor", "p8c")
        S.wait("tensor", "p8gev")
        for cc in range(8):
            S.inc("tensor", "p8gmm", 1,
                  lambda e, cc=cc, i=i: e.matmul(
                      ps_mid[:, 0:8],
                      h4C[:, cc * TH + i * 128:cc * TH + (i + 1) * 128],
                      wg_sb[:, cc * 8:(cc + 1) * 8],
                      start=(cc == 0), stop=(cc == 7)))
        S.wait("scalar", "p8gmm")
        S.inc("scalar", "p8gev", 1, lambda e: e.copy(gl8[:, :], ps_mid[:, 0:8]))
        S.wait("scalar", "p8gv")
        S.inc("scalar", "p8ge", 1,
              lambda e: e.activation(ge8[:, :], gl8[:, :], AF.Exp,
                                     accum_out=gs8[:, 0:1]))
        S.wait("vector", "p8ge")
        S.inc("vector", "p8gv", 1, lambda e: e.reciprocal(gs8[:, 1:2], gs8[:, 0:1]))
        S.inc("vector", "p8gv", 1,
              lambda e, i=i: e.tensor_scalar(gates_all[:, i * 8:(i + 1) * 8],
                                             ge8[:, :], gs8[:, 1:2], None,
                                             OP.mult))
        S.inc("vector", "p8gv", 1,
              lambda e, i=i: e.reduce_max(gs8[:, 2:3],
                                          gates_all[:, i * 8:(i + 1) * 8], axis=AX))
        S.inc("vector", "p8gv", 1,
              lambda e, i=i: e.tensor_scalar(gm8[:, :],
                                             gates_all[:, i * 8:(i + 1) * 8],
                                             gs8[:, 2:3], None, OP.is_ge))
        S.inc("vector", "p8gv", 1,
              lambda e, i=i: e.scalar_tensor_tensor(
                  gm8[:, :], gm8[:, :], -1e9,
                  gates_all[:, i * 8:(i + 1) * 8], OP.mult, OP.add))
        S.inc("vector", "p8gv", 1,
              lambda e: e.reduce_max(gs8[:, 3:4], gm8[:, :], axis=AX))
        S.wait("vector", "p8mst")
        S.inc("vector", "p8gv", 1,
              lambda e, i=i: e.tensor_scalar(mk8[:, :],
                                             gates_all[:, i * 8:(i + 1) * 8],
                                             gs8[:, 3:4], None, OP.is_ge))
        S.wait("gpsimd", "p8gv")
        S.dma("gpsimd", "p8mst", sc["cc_mkin"][i * 128:(i + 1) * 128, :],
              mk8[:, :])

    S.wait("gpsimd", "p8mst")
    S.inc("gpsimd", "p9cc", 1,
          lambda e: e.collective_compute(
              "AllGather", OP.bypass, replica_groups=[list(range(8))],
              ins=[sc["cc_mkin"][:, :].opt()], outs=[sc["cc_mkout"][:, :].opt()]))
    S.wait("gpsimd", "p9cc")

    if stop_after == "P8":
        with nc.Block() as blk:
            S.run(blk)
        close_all()
        return nc, dbg_out

    st_p8.close()
    S.join(ENGS, ("p8_mm", "p8gmm", "p8ld0", "p8ld1", "p8ld20", "p8ld21", "p8c",
                  "p8x3st0", "p8x3st1", "p8h4st0", "p8h4st1", "p8mst",
                  "ch_vector", "ch_scalar"))
    st_p9 = ExitStack()
    regions.append(st_p9)
    mA = sb("p9_mA", 128, 1024, st_p9)
    mB = sb("p9_mB", 128, 1024, st_p9)
    tot = sb("p9_tot", 128, 8, st_p9)
    offr = sb("p9_off", 128, 8, st_p9)
    kb9 = sb("p9_kb", 128, 1024, st_p9)
    neg1 = sb("p9_neg1", 1, 8, st_p9)

    S.wait("sync", "p9cc")
    S.dma("sync", "p9ld", mA[:, :],
          sc["cc_mkout"][:, :].rearrange("(g p) e -> g p e", p=128))
    S.inc("vector", "p9v", 1, lambda e: e.memset(neg1[:, :], -1.0))
    cur, nxt = mA, mB
    S.wait("vector", "p9ld")
    for k in [1, 2, 4, 8, 16, 32, 64]:
        S.inc("vector", "p9v", 1,
              lambda e, cur=cur, nxt=nxt, k=k: e.tensor_copy(
                  nxt[:, 0:k * 8], cur[:, 0:k * 8]))
        S.inc("vector", "p9v", 1,
              lambda e, cur=cur, nxt=nxt, k=k: e.tensor_tensor(
                  nxt[:, k * 8:1024], cur[:, k * 8:1024],
                  cur[:, 0:1024 - k * 8], OP.add))
        cur, nxt = nxt, cur
    cum = cur
    S.wait("tensor", "p9v")
    S.wait("tensor", "p9ld")
    S.inc("tensor", "p9mm", 1,
          lambda e: e.matmul(ps_a[:, 0:8], cb["triu_strict"][:, :],
                             cum[:, 127 * 8:128 * 8], start=True, stop=False))
    S.inc("tensor", "p9mm", 1,
          lambda e: e.matmul(ps_a[:, 0:8], cb["ones1_128"][:, :], neg1[:, :],
                             start=False, stop=True))
    S.wait("scalar", "p9mm")
    S.inc("scalar", "p9ev", 1, lambda e: e.copy(offr[:, :], ps_a[:, 0:8]))
    S.wait("vector", "p9ev")
    S.inc("vector", "p9v", 1,
          lambda e, cum=cum: e.tensor_tensor(
              cum[:, :].rearrange("a (p c) -> a p c", c=8), cum[:, :].rearrange(
                  "a (p c) -> a p c", c=8),
              offr[:, :].unsqueeze(1).broadcast_to([128, 128, 8]), OP.add))
    S.inc("vector", "p9v", 1,
          lambda e, cum=cum: e.tensor_scalar(kb9[:, :], cum[:, :], float(CAP),
                                             None, OP.is_lt))
    S.wait("sync", "p9v")
    S.dma("sync", "p9ld2", mA[:, :],
          sc["cc_mkout"][:, :].rearrange("(g p) e -> g p e", p=128))
    S.wait("vector", "p9ld2")
    S.inc("vector", "p9v", 1,
          lambda e: e.tensor_tensor(kb9[:, :], kb9[:, :], mA[:, :], OP.mult))
    sbase = sb("p9_sbase", 128, 8, st_p9)
    shmask = [(i // 16) * 16 for i in range(32)]
    S.inc("vector", "p9v", 1,
          lambda e: e.stream_shuffle(sbase[:, :], offr[:, :], shmask))
    S.inc("vector", "p9v", 1,
          lambda e, cum=cum: e.tensor_tensor(
              mA[:, :].rearrange("a (p c) -> a p c", c=8),
              cum[:, :].rearrange("a (p c) -> a p c", c=8),
              sbase[:, :].unsqueeze(1).broadcast_to([128, 128, 8]), OP.subtract))
    S.inc("vector", "p9v", 1,
          lambda e: e.tensor_scalar(mB[:, :], mA[:, :], 1.0, None, OP.subtract))
    def p9pairs_sb(e):
        reg = pid_reg(e)
        for j in range(8):
            with e.If_eq(reg, j):
                e.dma_start(
                    out=sc["keepown"][:, :].rearrange("(g p) e -> g (p e)", g=16),
                    in_=kb9[j * 16:(j + 1) * 16, :]).then_inc(
                    S.h["p9kst"], 16)
                e.dma_start(
                    out=sc["lposown"][:, :].rearrange("(g p) e -> g (p e)", g=16),
                    in_=mB[j * 16:(j + 1) * 16, :]).then_inc(
                    S.h["p9kst"], 16)
    S.sem("p9kst")
    S.wait("gpsimd", "p9v")
    S.cnt["p9kst"] += 32
    S.names["p9kst"] = ("p9kst", S.cnt["p9kst"])
    S.emit("gpsimd", p9pairs_sb)

    if "keep" in dbg_out:
        S.wait("gpsimd", "p9kst")
        S.dma("gpsimd", "dbg", dbg_out["keep"][:, :], sc["keepown"][:, :])
        S.dma("gpsimd", "dbg", dbg_out["gatesall"][:, :], gates_all[:, :])

    if stop_after == "P9":
        with nc.Block() as blk:
            S.run(blk)
        close_all()
        return nc, dbg_out

    BCP = 768
    DUMP = E * BCP
    st_p9.close()
    S.join(ENGS, ("p9mm", "p9ld", "p9ld2", "p9kst", "ch_vector", "ch_scalar"))
    st_pA = ExitStack()
    regions.append(st_pA)
    hidb = sb("hidb", 128, 2 * 32 * 384, st_pA)
    moeC = sb("moeC", 128, 2 * 1024, st_pA)
    w1b = sb("w1b", 128, 2 * 1024, st_pA)
    w2b = sb("w2b", 128, 2 * 1024, st_pA)
    b1t_sb = sb("b1t_sb", 128, E * 32, st_pA)
    b2_sb = sb("b2_sb", 8, C, st_pA)
    kt8 = sb("p10_kt", 128, 8, st_pA)
    lp8 = sb("p10_lp", 128, 8, st_pA)
    ev8 = sb("p10_ev", 128, 8 * 4, st_pA)
    sc8 = sb("p10_sc", 128, 16, st_pA)
    idxf = sb("p10_idxf", 128, 2, st_pA)
    idxi = st_pA.enter_context(nc.sbuf_tensor("p10_idxi", [128, 2 * 16],
                                               mybir.dt.int32))
    g12 = sb("p10_g12", 128, 2 * 16, st_pA)
    gkT = sb("p10_gkT", 8, 128, st_pA)
    evec_sb = sb("p10_evec", 128, 8, st_pA)
    bktt = sb("p10_bktt", 128, 2 * 1024, st_pA)
    bktC = sb("p10_bktC", 128, 8 * 384, st_pA)
    eo_sb = sb("p10_eo", 128, 2 * 1024, st_pA)
    ga1 = sb("p10_ga1", 128, C, st_pA)
    ga2 = sb("p10_ga2", 128, C, st_pA)
    xf1 = sb("p10_x1", 128, C, st_pA)
    xf2 = sb("p10_x2", 128, C, st_pA)

    S.dma("sync", "pAc", b1t_sb[:, :], di["b1t"][:, :])
    S.dma("sync", "pAc", b2_sb[:, :], di["b2sb"][:, :])
    S.dma("sync", "pAc", evec_sb[:, :], di["evec128"][:, :])

    S.inc("vector", "pAz", 1, lambda e: e.memset(ga1[:, :], 0.0))
    S.wait("gpsimd", "pAz")
    NBK = (E * BCP + 40 + 127) // 128
    for j in range(NBK):
        r0 = j * 128
        r1 = min(r0 + 128, E * BCP + 40)
        S.dma("gpsimd", "pAzst", sc["bkt"][r0:r1, :], ga1[0:r1 - r0, :])
    S.dma("gpsimd", "pAzst", sc["eout"][E * BCP:E * BCP + 40, :], ga1[0:40, :])

    for i in range(NTH):
        S.wait("sync", "pAdsp")
        S.dma("sync", "pAkld", kt8[:, :], sc["keepown"][i * 128:(i + 1) * 128, :])
        S.dma("sync", "pAkld", lp8[:, :], sc["lposown"][i * 128:(i + 1) * 128, :])
        S.wait("vector", "pAkld")
        S.wait("vector", "pAc")
        gsl = gates_all[:, i * 8:(i + 1) * 8]
        S.inc("vector", "pAdsp", 1,
              lambda e, i=i, gsl=gsl: e.tensor_tensor(
                  gk_all[:, i * 8:(i + 1) * 8], gsl, kt8[:, :], OP.mult))
        S.inc("vector", "pAdsp", 1,
              lambda e, gsl=gsl: e.reduce_max(sc8[:, 0:1], gsl, axis=AX))
        S.inc("vector", "pAdsp", 1,
              lambda e, gsl=gsl: e.tensor_scalar(ev8[:, 0:8], gsl, sc8[:, 0:1],
                                                 None, OP.is_ge))
        S.inc("vector", "pAdsp", 1,
              lambda e, gsl=gsl: e.scalar_tensor_tensor(
                  ev8[:, 16:24], ev8[:, 0:8], -1e9, gsl, OP.mult, OP.add))
        S.inc("vector", "pAdsp", 1,
              lambda e: e.reduce_max(sc8[:, 1:2], ev8[:, 16:24], axis=AX))
        S.inc("vector", "pAdsp", 1,
              lambda e, gsl=gsl: e.tensor_scalar(ev8[:, 8:16], gsl, sc8[:, 1:2],
                                                 None, OP.is_ge))
        S.inc("vector", "pAdsp", 1,
              lambda e: e.tensor_tensor(ev8[:, 8:16], ev8[:, 8:16], ev8[:, 0:8],
                                        OP.subtract))
        for k in range(2):
            eq = lambda k=k: ev8[:, k * 8:(k + 1) * 8]
            col = 2 + k * 4
            S.inc("vector", "pAdsp", 1,
                  lambda e, k=k, col=col: e.tensor_tensor(
                      ev8[:, 16:24], ev8[:, k * 8:(k + 1) * 8], evec_sb[:, :],
                      OP.mult))
            S.inc("vector", "pAdsp", 1,
                  lambda e, col=col: e.reduce_sum(sc8[:, col:col + 1],
                                                  ev8[:, 16:24], axis=AX))
            S.inc("vector", "pAdsp", 1,
                  lambda e, k=k: e.tensor_tensor(
                      ev8[:, 16:24], ev8[:, k * 8:(k + 1) * 8], lp8[:, :],
                      OP.mult))
            S.inc("vector", "pAdsp", 1,
                  lambda e, col=col: e.reduce_sum(sc8[:, col + 1:col + 2],
                                                  ev8[:, 16:24], axis=AX))
            S.inc("vector", "pAdsp", 1,
                  lambda e, k=k: e.tensor_tensor(
                      ev8[:, 16:24], ev8[:, k * 8:(k + 1) * 8], kt8[:, :],
                      OP.mult))
            S.inc("vector", "pAdsp", 1,
                  lambda e, col=col: e.reduce_sum(sc8[:, col + 2:col + 3],
                                                  ev8[:, 16:24], axis=AX))
            S.inc("vector", "pAdsp", 1,
                  lambda e, i=i, k=k: e.tensor_tensor(
                      ev8[:, 24:32], ev8[:, k * 8:(k + 1) * 8],
                      gk_all[:, i * 8:(i + 1) * 8], OP.mult))
            S.inc("vector", "pAdsp", 1,
                  lambda e, i=i, k=k: e.reduce_sum(g12[:, i * 2 + k:i * 2 + k + 1],
                                                   ev8[:, 24:32], axis=AX))
            dumpk = float(DUMP + i * 2 + k)
            S.inc("vector", "pAdsp", 1,
                  lambda e, col=col: e.tensor_scalar(
                      idxf[:, 0:1], sc8[:, col:col + 1], float(BCP), None,
                      OP.mult))
            S.inc("vector", "pAdsp", 1,
                  lambda e, col=col: e.tensor_tensor(
                      idxf[:, 0:1], idxf[:, 0:1], sc8[:, col + 1:col + 2],
                      OP.add))
            S.inc("vector", "pAdsp", 1,
                  lambda e, dumpk=dumpk: e.tensor_scalar(
                      idxf[:, 1:2], idxf[:, 0:1], -dumpk, None, OP.add))
            S.inc("vector", "pAdsp", 1,
                  lambda e, col=col: e.tensor_tensor(
                      idxf[:, 1:2], idxf[:, 1:2], sc8[:, col + 2:col + 3],
                      OP.mult))
            S.inc("vector", "pAdsp", 1,
                  lambda e, dumpk=dumpk: e.tensor_scalar(
                      idxf[:, 0:1], idxf[:, 1:2], dumpk, None, OP.add))
            S.inc("vector", "pAdsp", 1,
                  lambda e, i=i, k=k: e.tensor_copy(
                      idxi[:, i * 2 + k:i * 2 + k + 1], idxf[:, 0:1]))
    h4sb = bktt
    S.wait("gpsimd", "pAdsp")
    S.wait("gpsimd", "pAzst")
    S.wait("sync", "p8h4st0")
    S.wait("sync", "p8h4st1")
    for i in range(NTH):
        par = i % 2
        S.wait("sync", "pAsc")
        S.dma("sync", "pAh4ld", h4sb[:, par * 1024:(par + 1) * 1024],
              sc["h4tok"][i * 128:(i + 1) * 128, :])
        S.wait("gpsimd", "pAh4ld")
        for k in range(2):
            S.inc("gpsimd", "pAsc", 16,
                  lambda e, i=i, k=k, par=par: e.indirect_dma_start(
                      out=sc["bkt"][:, :],
                      out_offset=bass.IndirectOffsetOnAxis(
                          ap=idxi[:, i * 2 + k:i * 2 + k + 1], axis=0),
                      in_=h4sb[:, par * 1024:(par + 1) * 1024],
                      in_offset=None))
    nphase = 0
    for ee in range(E):
        for hf in range(2):
            hpar = nphase % 2
            hb0 = hpar * 32 * 384
            nphase += 1
            for jt in range(3):
                par = jt % 2
                S.wait("sync", "pAtr")
                S.wait("sync", "pAsc")
                S.dma("sync", "pAbld", bktt[:, par * 1024:(par + 1) * 1024],
                      sc["bkt"][ee * BCP + hf * 384 + jt * 128:
                                ee * BCP + hf * 384 + (jt + 1) * 128, :])
                S.wait("tensor", "pAbld")
                S.wait("tensor", "pAge")
                for ccj in range(8):
                    S.wait("tensor", "pAbev")
                    S.inc("tensor", "pAtr", 1,
                          lambda e, ccj=ccj, par=par: e.transpose(
                              ps_b[:, 0:128],
                              bktt[:, par * 1024 + ccj * 128:
                                   par * 1024 + (ccj + 1) * 128],
                              cb["ident"][:, :]))
                    S.wait("scalar", "pAtr")
                    S.inc("scalar", "pAbev", 1,
                          lambda e, ccj=ccj, jt=jt: e.copy(
                              bktC[:, ccj * 384 + jt * 128:
                                   ccj * 384 + (jt + 1) * 128],
                              ps_b[:, 0:128]))
            for ft in range(32):
                par = ft % 2
                S.wait("sync", "pAh1", target=max(0, S.cnt.get("pAh1", 0) - 8))
                S.dma("sync", "pAw1%d" % par,
                      w1b[:, par * 1024:(par + 1) * 1024], di["W1t"][ee, ft])
                S.wait("tensor", "pAw1%d" % par)
                S.wait("tensor", "pAbev")
                hps = ps_a if ft % 2 == 0 else ps_b
                S.wait_lag("tensor", "pAge", 1)
                for cc in range(8):
                    S.inc("tensor", "pAh1", 1,
                          lambda e, cc=cc, par=par, hps=hps: e.matmul(
                              hps[:, 0:384],
                              w1b[:, par * 1024 + cc * 128:
                                  par * 1024 + (cc + 1) * 128],
                              bktC[:, cc * 384:(cc + 1) * 384],
                              start=(cc == 0), stop=(cc == 7)))
                S.wait("scalar", "pAh1")
                S.wait("scalar", "pAw2mm",
                       target=max(0, S.cnt.get("pAw2mm", 0) - 192))
                S.inc("scalar", "pAge", 1,
                      lambda e, ft=ft, ee=ee, hb0=hb0, hps=hps: e.activation(
                          hidb[:, hb0 + ft * 384:hb0 + (ft + 1) * 384],
                          hps[:, 0:384], AF.Gelu,
                          bias=b1t_sb[:, ee * 32 + ft:ee * 32 + ft + 1],
                          scale=1.0))
            w2ps = [ps_big[:, 0:512], ps_big[:, 512:1024],
                    ps_big[:, 1024:1536], ps_big[:, 1536:2048],
                    ps_mid[:, 0:512], ps_mid[:, 512:1024]]
            for ft in range(32):
                par = ft % 2
                S.wait("sync", "pAw2mm",
                       target=max(0, S.cnt.get("pAw2mm", 0) - 6))
                S.dma("sync", "pAw2%d" % par,
                      w2b[:, par * 1024:(par + 1) * 1024],
                      di["W2"][ee, ft * 128:(ft + 1) * 128, :])
                S.wait("tensor", "pAw2%d" % par)
                S.wait("tensor", "pAge")
                if ft == 0:
                    S.wait("tensor", "pAeev")
                for jt in range(3):
                    for oh in range(2):
                        S.inc("tensor", "pAw2mm", 1,
                              lambda e, jt=jt, oh=oh, ft=ft, par=par, hb0=hb0:
                              e.matmul(
                                  w2ps[jt * 2 + oh],
                                  hidb[:, hb0 + ft * 384 + jt * 128:
                                       hb0 + ft * 384 + (jt + 1) * 128],
                                  w2b[:, par * 1024 + oh * 512:
                                      par * 1024 + (oh + 1) * 512],
                                  start=(ft == 0), stop=(ft == 31)))
            S.wait("scalar", "pAw2mm")
            for jt in range(3):
                jp = jt % 2
                S.wait("scalar", "pAest%d" % jp)
                S.inc("scalar", "pAeev", 1,
                      lambda e, jt=jt, jp=jp: e.copy(
                          eo_sb[:, jp * 1024:(jp + 1) * 1024],
                          ps_mid[:, :] if jt == 2 else
                          ps_big[:, jt * 1024:(jt + 1) * 1024]))
                S.wait("gpsimd", "pAeev")
                S.dma("gpsimd", "pAest%d" % jp,
                      sc["eout"][ee * BCP + hf * 384 + jt * 128:
                                 ee * BCP + hf * 384 + (jt + 1) * 128, :],
                      eo_sb[:, jp * 1024:(jp + 1) * 1024])
    S.wait("gpsimd", "pAest0")
    S.wait("gpsimd", "pAest1")
    for i in range(NTH):
        S.wait("tensor", "pAgkT")
        S.wait("tensor", "pAfin")
        S.inc("tensor", "pAtr2", 1,
              lambda e, i=i: e.transpose(ps_b[0:8, 0:128],
                                         gk_all[:, i * 8:(i + 1) * 8],
                                         cb["ident"][:, :]))
        S.wait("scalar", "pAtr2")
        S.inc("scalar", "pAgkT", 1, lambda e: e.copy(gkT[:, :], ps_b[0:8, 0:128]))
        S.wait("tensor", "pAgkT")
        S.wait("tensor", "pAc")
        S.wait("tensor", "pAinev")
        for oh in range(2):
            S.inc("tensor", "pAinmm", 1,
                  lambda e, oh=oh: e.matmul(
                      ps_mid[:, oh * 512:(oh + 1) * 512], gkT[:, :],
                      b2_sb[:, oh * 512:(oh + 1) * 512], start=True, stop=True))
        S.wait("scalar", "pAinmm")
        S.wait("scalar", "pAfin")
        S.inc("scalar", "pAinev", 1,
              lambda e, i=i: e.copy(moeC[:, (i % 2) * 1024:(i % 2 + 1) * 1024],
                                    ps_mid[:, :]))
        S.wait("gpsimd", "pAfin")
        for k, gbuf in ((0, ga1), (1, ga2)):
            S.inc("gpsimd", "pAga", 16,
                  lambda e, i=i, k=k, gbuf=gbuf: e.indirect_dma_start(
                      out=gbuf[:, :], out_offset=None,
                      in_=sc["eout"][:, :],
                      in_offset=bass.IndirectOffsetOnAxis(
                          ap=idxi[:, i * 2 + k:i * 2 + k + 1], axis=0)))
        S.wait("sync", "pAfin")
        S.wait("sync", "pAost")
        S.dma("sync", "pAfld", xf1[:, :], sc["x2own"][i * 128:(i + 1) * 128, :])
        S.dma("sync", "pAfld", xf2[:, :], sc["x3"][i * 128:(i + 1) * 128, :])
        S.wait("vector", "pAfld")
        S.wait("vector", "pAga")
        S.wait("vector", "pAinev")
        S.inc("vector", "pAfin", 1,
              lambda e, i=i: e.scalar_tensor_tensor(
                  moeC[:, (i % 2) * 1024:(i % 2 + 1) * 1024], ga1[:, :],
                  g12[:, i * 2:i * 2 + 1],
                  moeC[:, (i % 2) * 1024:(i % 2 + 1) * 1024], OP.mult, OP.add))
        S.inc("vector", "pAfin", 1,
              lambda e, i=i: e.scalar_tensor_tensor(
                  moeC[:, (i % 2) * 1024:(i % 2 + 1) * 1024], ga2[:, :],
                  g12[:, i * 2 + 1:i * 2 + 2],
                  moeC[:, (i % 2) * 1024:(i % 2 + 1) * 1024], OP.mult, OP.add))
        S.inc("vector", "pAfin", 1,
              lambda e, i=i: e.tensor_tensor(
                  xf2[:, :], moeC[:, (i % 2) * 1024:(i % 2 + 1) * 1024],
                  xf2[:, :], OP.add))
        S.inc("vector", "pAfin", 1,
              lambda e: e.tensor_tensor(xf2[:, :], xf1[:, :], xf2[:, :], OP.add))
        S.wait("gpsimd", "pAfin")
        S.dma("gpsimd", "pAost", out[i * 128:(i + 1) * 128, :], xf2[:, :])

    S.wait("gpsimd", "pAost")
    with nc.Block() as blk:
        S.run(blk)
    close_all()
    return nc, dbg_out


def _final_inputs(ins):
    return [
        _core_inputs(ins, core) for core in range(8)
    ]


def kernel(**inputs):
    nc, _ = build_program()
    in_maps = _final_inputs(inputs)
    res = run_bass_kernel_spmd(nc, in_maps, core_ids=list(range(8)))
    o = np.zeros((B, T, C), np.float32)
    for core in range(8):
        b = core // 2
        hg = core % 2
        o[b, hg * TH:(hg + 1) * TH, :] = np.asarray(res.results[core]["out"])
    return o


# version 11 (final)
# speedup vs baseline: 1.0332x; 1.0262x over previous
import math
from contextlib import ExitStack

import numpy as np

import concourse.bass as bass
import concourse.mybir as mybir
from concourse.bass_utils import run_bass_kernel_spmd

F32 = mybir.dt.float32
AF = mybir.ActivationFunctionType
OP = mybir.AluOpType
AX = mybir.AxisListType.X

B, T, C = 4, 4096, 1024
H, HD = 16, 64
HL = 8
CL = HL * HD
NT = T // 128
NCH = T // 512
E = 8
CAP = 4096
TH = T // 2
FF = 4096
EXPC = 12.0
LN_EPS = 1e-5



def _gammas():
    lo = math.log(1.0 - 0.999)
    hi = math.log(1.0 - 0.9)
    return (1.0 - np.exp(np.linspace(lo, hi, H))).astype(np.float32)


def _tile_kmaj(a):
    K, M = a.shape
    return np.ascontiguousarray(
        a.reshape(K // 128, 128, M).transpose(1, 0, 2).reshape(128, (K // 128) * M)
    ).astype(np.float32)


def _host_consts(core):
    hg = core % 2
    g = _gammas()[hg * HL:(hg + 1) * HL].astype(np.float64)
    iloc = np.arange(128)

    dist = iloc[None, :] - iloc[:, None]
    d0t = np.zeros((128, HL * 128), np.float32)
    for h in range(HL):
        d0t[:, h * 128:(h + 1) * 128] = np.where(
            dist >= 0, g[h] ** np.maximum(dist, 0), 0.0).astype(np.float32)

    u2 = np.zeros((128, 4 * 128), np.float32)
    qs = np.zeros((128, 4 * 128), np.float32)
    c128 = np.zeros((128, 4 * 64), np.float32)
    for rt in range(4):
        for hl in range(2):
            gf = g[2 * rt + hl]
            u2[:, rt * 128 + hl * 64:rt * 128 + (hl + 1) * 64] = \
                (gf ** (127 - iloc))[:, None].astype(np.float32)
            qs[hl * 64:(hl + 1) * 64, rt * 128:(rt + 1) * 128] = \
                (gf ** (iloc + 1))[None, :].astype(np.float32)
            c128[hl * 64:(hl + 1) * 64, rt * 64:(rt + 1) * 64] = \
                np.float32(gf ** 128)

    ident = np.eye(128, dtype=np.float32)

    invf = 1.0 / (10000.0 ** (np.arange(0, HD, 2, dtype=np.float64) / HD))
    ang = np.arange(T, dtype=np.float64)[:, None] * invf[None, :]
    crep = np.tile(np.cos(ang).astype(np.float32).T, (4, 1))
    srep = np.tile(np.sin(ang).astype(np.float32).T, (4, 1))

    dmask = np.zeros((128, 4 * 512), np.float32)
    for kj in range(4):
        kg = kj * 128 + iloc[:, None]
        qg = np.arange(512)[None, :]
        dmask[:, kj * 512:(kj + 1) * 512] = (qg >= kg).astype(np.float32)

    triu_incl = np.triu(np.ones((128, 128), np.float32), 0)
    triu_strict = np.triu(np.ones((128, 128), np.float32), 1)

    sel2 = np.zeros((128, 2), np.float32)
    sel2[:64, 0] = 1.0
    sel2[64:, 1] = 1.0
    sel2T = np.ascontiguousarray(sel2.T)

    ones1_128 = np.ones((1, 128), np.float32)
    ones1_64 = np.ones((1, 64), np.float32)

    return {
        "d0t": d0t, "u2": u2, "qs": qs, "c128": c128, "ident": ident,
        "crep": crep, "srep": srep, "dmask": dmask,
        "triu_incl": triu_incl, "triu_strict": triu_strict,
        "sel2": sel2, "sel2T": sel2T,
        "ones1_128": ones1_128, "ones1_64": ones1_64,
        "evec128": np.broadcast_to(np.arange(8, dtype=np.float32)[None, :],
                                   (128, 8)).copy(),
    }


def _core_inputs(ins, core):
    b = core // 2
    hg = core % 2
    hsl = slice(hg * CL, (hg + 1) * CL)

    x = np.ascontiguousarray(np.asarray(ins["L0_attn_x"])[b]).astype(np.float32)

    wq = np.asarray(ins["L0_attn_wq"]); wk = np.asarray(ins["L0_attn_wk"])
    wv = np.asarray(ins["L0_attn_wv"]); wo = np.asarray(ins["L0_attn_wo"])
    wqC = _tile_kmaj(np.ascontiguousarray(wq[hsl].T))
    wkC = _tile_kmaj(np.ascontiguousarray(wk[hsl].T))
    wvC = _tile_kmaj(np.ascontiguousarray(wv[hsl].T))
    woC = _tile_kmaj(np.ascontiguousarray(wo[:, hsl].T))

    gw = np.asarray(ins["L0_attn_gn_w"])[hsl].reshape(HL, HD)
    gb = np.asarray(ins["L0_attn_gn_b"])[hsl].reshape(HL, HD)
    gnw128 = np.zeros((128, 4), np.float32)
    gnb128 = np.zeros((128, 4), np.float32)
    for rt in range(4):
        for hl in range(2):
            gnw128[hl * 64:(hl + 1) * 64, rt] = gw[2 * rt + hl]
            gnb128[hl * 64:(hl + 1) * 64, rt] = gb[2 * rt + hl]

    Wqkv = np.asarray(ins["L0_ffn_Wqkv"])
    heads = np.arange(hg * HL, (hg + 1) * HL)
    colq = heads[:, None] * 192 + np.arange(64)[None, :]
    colk = colq + 64
    colv = colq + 128
    cols = np.concatenate([colq[:, :32].ravel(), colq[:, 32:].ravel(),
                           colk[:, :32].ravel(), colk[:, 32:].ravel(),
                           colv.ravel()])
    WqkvC = _tile_kmaj(np.ascontiguousarray(Wqkv[:, cols]))

    WprojR = _tile_kmaj(np.ascontiguousarray(
        np.asarray(ins["L0_ffn_Wproj"])[hsl, :]))

    WgateC = _tile_kmaj(np.asarray(ins["L0_ffn_Wgate"]).astype(np.float32))

    W1 = np.asarray(ins["L0_ffn_W1"]).astype(np.float32)
    W1t = np.ascontiguousarray(
        W1.reshape(E, 8, 128, 32, 128).transpose(0, 3, 2, 1, 4)).reshape(
        E, 32, 128, 8 * 128)
    W2 = np.ascontiguousarray(np.asarray(ins["L0_ffn_W2"])).astype(np.float32)
    b1t = np.ascontiguousarray(
        np.asarray(ins["L0_ffn_b1"]).reshape(E, 32, 128).transpose(2, 0, 1)
    ).reshape(128, E * 32).astype(np.float32)
    b2sb = np.asarray(ins["L0_ffn_b2"]).astype(np.float32)

    m = {
        "x": x, "wqC": wqC, "wkC": wkC, "wvC": wvC, "woC": woC,
        "gnw128": gnw128, "gnb128": gnb128,
        "WqkvC": WqkvC, "WprojR": WprojR,
        "ln1g": np.asarray(ins["L0_ffn_ln1_g"]).reshape(1, C).astype(np.float32),
        "ln1b": np.asarray(ins["L0_ffn_ln1_b"]).reshape(1, C).astype(np.float32),
        "ln2g": np.asarray(ins["L0_ffn_ln2_g"]).reshape(1, C).astype(np.float32),
        "ln2b": np.asarray(ins["L0_ffn_ln2_b"]).reshape(1, C).astype(np.float32),
        "WgateC": WgateC, "W1t": W1t, "W2": W2, "b1t": b1t, "b2sb": b2sb,
    }
    m.update(_host_consts(core))
    return m



ENGS = ("sync", "tensor", "vector", "scalar", "gpsimd")


CHAINED = ("vector", "scalar")


class Sched:

    def __init__(self, nc, stack):
        self.nc = nc
        self.q = {e: [] for e in ENGS}
        self.cnt = {}
        self.h = {}
        self.stack = stack
        self.names = {}
        self.hist = {}

    def sem(self, name):
        if name not in self.h:
            self.h[name] = self.stack.enter_context(self.nc.semaphore(name))
            self.cnt[name] = 0
        return self.h[name]

    def emit(self, eng, fn):
        self.q[eng].append(fn)

    def inc(self, eng, sname, k, build):
        if eng in CHAINED:
            ch = "ch_" + eng
            self.sem(ch)
            t = self.cnt[ch]
            h = self.h[ch]
            if t > 0:
                self.q[eng].append(lambda e, h=h, t=t: e.wait_ge(h, t))
            self.cnt[ch] += 1
            self.names[sname] = (ch, self.cnt[ch])
            self.hist.setdefault(sname, []).append(self.cnt[ch])
            self.q[eng].append(lambda e, b=build, h=h: b(e).then_inc(h, 1))
        else:
            self.sem(sname)
            self.cnt[sname] += k
            self.names[sname] = (sname, self.cnt[sname])
            h = self.h[sname]
            self.q[eng].append(lambda e, b=build, h=h, k=k: b(e).then_inc(h, k))

    def wait(self, eng, sname, target=None):
        if sname.startswith("ch_"):
            if sname not in self.cnt or self.cnt[sname] <= 0:
                return
            h = self.h[sname]
            t = self.cnt[sname]
            self.q[eng].append(lambda e, h=h, t=t: e.wait_ge(h, t))
            return
        if sname not in self.names:
            if target is None or target <= 0:
                return
            self.sem(sname)
            self.names[sname] = (sname, 0)
        s, v = self.names[sname]
        if target is not None:
            assert s == sname, f"explicit target on chained name {sname}"
            v = target
        if v <= 0:
            return
        h = self.h[s]
        self.q[eng].append(lambda e, h=h, t=v: e.wait_ge(h, t))

    def wait_lag(self, eng, sname, lag):
        hist = self.hist.get(sname, [])
        if len(hist) <= lag:
            return
        s, _ = self.names[sname]
        v = hist[-1 - lag]
        h = self.h[s]
        self.q[eng].append(lambda e, h=h, t=v: e.wait_ge(h, t))

    def join(self, engines, names):
        for e in engines:
            for n in names:
                self.wait(e, n)

    def dma(self, eng, sname, dst, src):
        self.inc(eng, sname, 16, lambda e, d=dst, s=src: e.dma_start(out=d, in_=s))

    def run(self, block):
        for eng in ENGS:
            ops = self.q[eng]

            def make(ops):
                def body(e):
                    for f in ops:
                        f(e)
                return body
            getattr(block, eng)(make(ops))



IN_SPECS = [
    ("x", (T, C)), ("wqC", (128, 8 * 512)), ("wkC", (128, 8 * 512)),
    ("wvC", (128, 8 * 512)), ("woC", (128, 4 * 1024)),
    ("gnw128", (128, 4)), ("gnb128", (128, 4)),
    ("WqkvC", (128, 8 * 1536)), ("WprojR", (128, 4 * 1024)),
    ("ln1g", (1, C)), ("ln1b", (1, C)), ("ln2g", (1, C)), ("ln2b", (1, C)),
    ("WgateC", (128, 8 * 8)), ("W1t", (E, 32, 128, 8 * 128)),
    ("W2", (E, FF, C)), ("b1t", (128, E * 32)), ("b2sb", (8, C)),
    ("d0t", (128, HL * 128)), ("u2", (128, 4 * 128)), ("qs", (128, 4 * 128)),
    ("c128", (128, 4 * 64)), ("ident", (128, 128)),
    ("crep", (128, T)), ("srep", (128, T)), ("dmask", (128, 4 * 512)),
    ("triu_incl", (128, 128)), ("triu_strict", (128, 128)),
    ("sel2", (128, 2)), ("sel2T", (2, 128)), ("evec128", (128, 8)),
    ("ones1_128", (1, 128)), ("ones1_64", (1, 64)),
]


def build_program(dbg=(), stop_after=None):
    nc = bass.Bass(trn_type="TRN2", target_bir_lowering=False)

    di = {}
    for n, s in IN_SPECS:
        di[n] = nc.dram_tensor(n, list(s), F32, kind="ExternalInput").ap()

    out = nc.dram_tensor("out", [TH, C], F32, kind="ExternalOutput").ap()

    sc = {}
    for name, shape in [
        ("h1C", (C, T)), ("x2", (T, C)), ("h2", (T, C)), ("h3C", (C, T)),
        ("q2r", (CL, T)), ("k2r", (CL, T)), ("v2a", (T, HL, 65)),
        ("att", (CL, T)), ("x3", (TH, C)),
        ("a1p", (T, C)), ("a2p", (T, C)),
        ("h2own", (TH, C)), ("x2own", (TH, C)),
        ("keepown", (TH, E)), ("lposown", (TH, E)), ("h4tok", (TH, C)),
        ("bkt", (E * 768 + 40, C)), ("eout", (E * 768 + 40, C)),
        ("cc_a1in", (T, C)), ("cc_a1outA", (8, 1024, C)),
        ("cc_a1outB", (8, 1024, C)), ("cc_a1outC", (8, 1024, C)),
        ("cc_a1outD", (8, 1024, C)),
        ("cc_a2in", (T, C)), ("cc_a2outA", (8, TH, C)),
        ("cc_a2outB", (8, TH, C)),
        ("cc_mkin", (TH, E)), ("cc_mkout", (8 * TH, E)),
    ]:
        sc[name] = nc.dram_tensor(name, list(shape), F32).ap()

    dbg_out = {}
    for name, shape in dbg:
        dbg_out[name] = nc.dram_tensor("dbg_" + name, list(shape), F32,
                                       kind="ExternalOutput").ap()

    stack = ExitStack()
    regions = []
    S = Sched(nc, stack)

    def close_all():
        for st in reversed(regions):
            st.close()
        stack.close()

    def sb(name, p, f, st=None):
        return (st or stack).enter_context(nc.sbuf_tensor(name, [p, f], F32))

    cb = {}
    for n, p, f in [
        ("ident", 128, 128), ("ones1_128", 1, 128), ("ones1_64", 1, 64),
        ("d0t", 128, HL * 128), ("u2", 128, 4 * 128), ("qs", 128, 4 * 128),
        ("c128", 128, 4 * 64), ("gnw128", 128, 4), ("gnb128", 128, 4),
        ("sel2", 128, 2), ("sel2T", 2, 128),
        ("triu_incl", 128, 128), ("triu_strict", 128, 128),
    ]:
        cb[n] = sb("cb_" + n, p, f)

    ps_big = stack.enter_context(nc.psum_tensor("ps_big", [128, 2048], F32))
    ps_mid = stack.enter_context(nc.psum_tensor("ps_mid", [128, 1024], F32))
    ps_a = stack.enter_context(nc.psum_tensor("ps_a", [128, 512], F32))
    ps_b = stack.enter_context(nc.psum_tensor("ps_b", [128, 512], F32))

    for n in cb:
        S.dma("sync", "c_ld", cb[n][:, :], di[n][:, :])

    lnrow = sb("lnrow", 1, C)

    def emit_bcast(dstbuf, srcname):
        S.wait("sync", "p0b_mm")
        S.dma("sync", "p0b_ld", lnrow[:, :], di[srcname][:, :])
        S.wait("tensor", "p0b_ld")
        S.wait("tensor", "c_ld")
        S.wait("tensor", "p0b_ev")
        for oh in range(2):
            S.inc("tensor", "p0b_mm", 1,
                  lambda e, oh=oh: e.matmul(
                      ps_mid[:, oh * 512:(oh + 1) * 512], cb["ones1_128"][:, :],
                      lnrow[:, oh * 512:(oh + 1) * 512], start=True, stop=True))
        S.wait("scalar", "p0b_mm")
        S.inc("scalar", "p0b_ev", 1,
              lambda e, d=dstbuf: e.copy(d[:, :], ps_mid[:, :]))

    st_p1 = ExitStack()
    regions.append(st_p1)
    xt = sb("p1_x", 128, 2 * C, st_p1)
    h1t = sb("p1_h1", 128, C, st_p1)
    sqd = sb("p1_sqd", 128, C, st_p1)
    lnb = sb("p1_lnb", 128, 8 * NT, st_p1)
    tpo = sb("p1_tpo", 128, 2 * 128, st_p1)

    def emit_ln_batched(pfx, n, load_fn, sums, sq, work):
        for i in range(n):
            par = i % 2
            ap = load_fn(i, par)
            S.wait("vector", pfx + "ld%d" % par)
            S.inc("vector", pfx + "rs", 1,
                  lambda e, ap=ap, i=i: e.reduce_sum(sums[:, i:i + 1], ap, axis=AX))
            S.wait("scalar", pfx + "ld%d" % par)
            S.inc("scalar", pfx + "sq", 1,
                  lambda e, ap=ap, i=i: e.activation(sqd[:, :], ap, AF.Square,
                                                     accum_out=sq[:, i:i + 1]))

    def emit_ln_finalize(pfx, n, sums, sq, negmean, rstd, work):
        S.wait("vector", pfx + "sq")
        S.inc("vector", pfx + "fz", 1,
              lambda e: e.tensor_scalar(negmean, sums, -1.0 / C, None, OP.mult))
        S.inc("vector", pfx + "fz", 1,
              lambda e: e.tensor_scalar(work[0], sq, 1.0 / C, None, OP.mult))
        S.inc("vector", pfx + "fz", 1,
              lambda e: e.tensor_tensor(work[1], negmean, negmean, OP.mult))
        S.inc("vector", pfx + "fz", 1,
              lambda e: e.tensor_tensor(work[1], work[0], work[1], OP.subtract))
        S.inc("vector", pfx + "fz", 1,
              lambda e: e.tensor_scalar(work[0], work[1], LN_EPS, None, OP.add))
        S.inc("vector", pfx + "fz", 1, lambda e: e.reciprocal(work[1], work[0]))
        S.wait("scalar", pfx + "fz")
        S.inc("scalar", pfx + "rst", 1,
              lambda e: e.activation(rstd, work[1], AF.Sqrt))

    def p1_load(i, par):
        S.wait_lag("sync", "p1rs", 1)
        S.wait_lag("sync", "p1sq", 1)
        S.dma("sync", "p1ld%d" % par, xt[:, par * C:(par + 1) * C],
              di["x"][i * 128:(i + 1) * 128, :])
        return xt[:, par * C:(par + 1) * C]

    S.wait("tensor", "c_ld")
    S.wait("vector", "c_ld")
    emit_ln_batched("p1", NT, p1_load,
                    lnb[:, 0:NT], lnb[:, NT:2 * NT], None)
    emit_ln_finalize("p1", NT, lnb[:, 0:NT], lnb[:, NT:2 * NT],
                     lnb[:, 2 * NT:3 * NT], lnb[:, 3 * NT:4 * NT],
                     (lnb[:, 4 * NT:5 * NT], lnb[:, 5 * NT:6 * NT]))
    S.wait("sync", "p1rs")
    S.wait("sync", "p1sq")
    for i in range(NT):
        par = i % 2
        S.wait_lag("sync", "p1ap", 1)
        S.dma("sync", "p1ld2%d" % par, xt[:, par * C:(par + 1) * C],
              di["x"][i * 128:(i + 1) * 128, :])
        S.wait("vector", "p1ld2%d" % par)
        S.wait("vector", "p1rst")
        S.wait("vector", "p1_mm")
        S.inc("vector", "p1ap", 1,
              lambda e, i=i, par=par: e.tensor_scalar(
                  h1t[:, :], xt[:, par * C:(par + 1) * C],
                  lnb[:, 2 * NT + i:2 * NT + i + 1],
                  lnb[:, 3 * NT + i:3 * NT + i + 1], OP.add, OP.mult))
        for ccj in range(8):
            tp = ccj % 2
            tps = ps_a if tp == 0 else ps_b
            S.wait("tensor", "p1ap")
            S.wait_lag("tensor", "p1_ev", 1)
            S.inc("tensor", "p1_mm", 1,
                  lambda e, ccj=ccj, tps=tps: e.transpose(
                      tps[:, 0:128],
                      h1t[:, ccj * 128:(ccj + 1) * 128], cb["ident"][:, :]))
            S.wait("scalar", "p1_mm")
            S.wait("scalar", "p1st%d" % tp)
            S.inc("scalar", "p1_ev", 1,
                  lambda e, tp=tp, tps=tps: e.copy(
                      tpo[:, tp * 128:(tp + 1) * 128], tps[:, 0:128]))
            S.wait("gpsimd", "p1_ev")
            S.dma("gpsimd", "p1st%d" % tp,
                  sc["h1C"][ccj * 128:(ccj + 1) * 128, i * 128:(i + 1) * 128],
                  tpo[:, tp * 128:(tp + 1) * 128])

    if "h1C" in dbg_out:
        S.wait("gpsimd", "p1st0")
        S.wait("gpsimd", "p1st1")
        for cc in range(8):
            S.dma("gpsimd", "dbg", dbg_out["h1C"][cc * 128:(cc + 1) * 128, :],
                  sc["h1C"][cc * 128:(cc + 1) * 128, :])

    if stop_after == "P1":
        with nc.Block() as blk:
            S.run(blk)
        close_all()
        return nc, dbg_out

    st_p1.close()
    S.join(("sync", "tensor", "vector", "scalar", "gpsimd"),
           ("p1_mm", "p1_ev", "p1st0", "p1st1", "ch_vector", "ch_scalar",
            "p1ld0", "p1ld1", "p1ld20", "p1ld21"))
    st_y = ExitStack()
    regions.append(st_y)
    yC = sb("yC", 128, 4 * T, st_y)
    state = sb("state", 128, 4 * 64, st_y)

    st_p23 = ExitStack()
    regions.append(st_p23)
    wq_rt = sb("wq_rt", 128, 8 * 128, st_p23)
    wk_rt = sb("wk_rt", 128, 8 * 128, st_p23)
    wkv_rt = sb("wkv_rt", 128, 8 * 256, st_p23)
    h1c_ch = sb("h1c_ch", 128, 2 * 8 * 512, st_p23)

    qd_p = sb("qd_p", 128, T, st_p23)
    kd_p = sb("kd_p", 128, T, st_p23)
    kt_p = sb("kt_p", 128, 32 * 128, st_p23)
    v_p = sb("v_p", 128, 32 * 128, st_p23)
    ploc = sb("ploc", 128, 2 * 128, st_p23)
    vs_t = sb("vs_t", 128, 128, st_p23)
    qs_t = sb("qs_t", 128, 128, st_p23)

    S.wait("gpsimd", "p1_ev")

    for rt in range(4):
        S.wait("sync", "p2mm")
        for cc in range(8):
            S.dma("sync", "p2w", wq_rt[:, cc * 128:(cc + 1) * 128],
                  di["wqC"][:, cc * 512 + rt * 128: cc * 512 + (rt + 1) * 128])
            S.dma("sync", "p2w", wk_rt[:, cc * 128:(cc + 1) * 128],
                  di["wkC"][:, cc * 512 + rt * 128: cc * 512 + (rt + 1) * 128])
            S.dma("sync", "p2w", wkv_rt[:, cc * 256:cc * 256 + 128],
                  di["wkC"][:, cc * 512 + rt * 128: cc * 512 + (rt + 1) * 128])
            S.dma("sync", "p2w", wkv_rt[:, cc * 256 + 128:(cc + 1) * 256],
                  di["wvC"][:, cc * 512 + rt * 128: cc * 512 + (rt + 1) * 128])

        S.wait("scalar", "p3vs")
        S.wait("scalar", "p3qs")

        for tc in range(NCH):
            tp = tc % 2
            S.wait("sync", "p2mm", target=max(0, S.cnt.get("p2mm", 0) - 48))
            if rt == 0 and tc == 0:
                S.wait("sync", "p1st0")
                S.wait("sync", "p1st1")
            for cc in range(8):
                S.dma("sync", "p2h%d" % tp,
                      h1c_ch[:, tp * 4096 + cc * 512:tp * 4096 + (cc + 1) * 512],
                      sc["h1C"][cc * 128:(cc + 1) * 128, tc * 512:(tc + 1) * 512])
            S.wait("tensor", "p2h%d" % tp)
            S.wait("tensor", "p2w")
            S.wait("tensor", "p2ev")
            S.wait("tensor", "p3pl")
            S.wait("tensor", "p3ev")
            S.wait("tensor", "p3st")
            for pj, wbuf in [(0, wq_rt), (1, wk_rt)]:
                for cc in range(8):
                    S.inc("tensor", "p2mm", 1,
                          lambda e, w=wbuf, cc=cc, pj=pj, tp=tp: e.matmul(
                              ps_big[:, pj * 512:(pj + 1) * 512],
                              w[:, cc * 128:(cc + 1) * 128],
                              h1c_ch[:, tp * 4096 + cc * 512:
                                     tp * 4096 + (cc + 1) * 512],
                              start=(cc == 0), stop=(cc == 7)))
            ktv_ps = [ps_mid[:, 0:256], ps_mid[:, 512:768],
                      ps_a[:, 0:256], ps_b[:, 0:256]]
            for ts in range(4):
                for cc in range(8):
                    S.inc("tensor", "p2mm", 1,
                          lambda e, ts=ts, cc=cc, tp=tp: e.matmul(
                              ktv_ps[ts],
                              h1c_ch[:, tp * 4096 + cc * 512 + ts * 128:
                                     tp * 4096 + cc * 512 + (ts + 1) * 128],
                              wkv_rt[:, cc * 256:(cc + 1) * 256],
                              start=(cc == 0), stop=(cc == 7)))
            S.wait("scalar", "p2mm")
            S.inc("scalar", "p2ev", 1,
                  lambda e, tc=tc: e.copy(qd_p[:, tc * 512:(tc + 1) * 512],
                                          ps_big[:, 0:512]))
            S.inc("scalar", "p2ev", 1,
                  lambda e, tc=tc: e.copy(kd_p[:, tc * 512:(tc + 1) * 512],
                                          ps_big[:, 512:1024]))
            for ts in range(4):
                j = tc * 4 + ts
                S.inc("scalar", "p2ev", 1,
                      lambda e, ts=ts, j=j: e.copy(
                          kt_p[:, j * 128:(j + 1) * 128],
                          ktv_ps[ts][:, 0:128]))
                S.inc("scalar", "p2ev", 1,
                      lambda e, ts=ts, j=j: e.copy(
                          v_p[:, j * 128:(j + 1) * 128],
                          ktv_ps[ts][:, 128:256]))
            S.wait("tensor", "p2ev")

        if rt == 0 and "q1d_rt0" in dbg_out:
            S.wait("gpsimd", "p2ev")
            S.dma("gpsimd", "dbg", dbg_out["q1d_rt0"][:, :], qd_p[:, :])
            S.dma("gpsimd", "dbg", dbg_out["k1d_rt0"][:, :], kd_p[:, :])
            S.dma("gpsimd", "dbg", dbg_out["v1_rt0"][:, :], v_p[:, :])
            S.dma("gpsimd", "dbg", dbg_out["kt_rt0"][:, :], kt_p[:, :])

        S.wait("vector", "p2ev")
        S.inc("vector", "p3st", 1,
              lambda e, rt=rt: e.memset(state[:, rt * 64:(rt + 1) * 64], 0.0))

        for j in range(NT):
            jb = (j % 2) * 1024
            S.wait_lag("tensor", "p3pl", 2)
            for hl in (0, 1):
                S.inc("tensor", "p3mm1", 1,
                      lambda e, hl=hl, j=j, jb=jb: e.matmul(
                          ps_big[:, jb + hl * 512:jb + hl * 512 + 128],
                          kd_p[hl * 64:(hl + 1) * 64, j * 128:(j + 1) * 128],
                          qd_p[hl * 64:(hl + 1) * 64, j * 128:(j + 1) * 128],
                          start=True, stop=True))
            S.wait("vector", "p3mm2")
            S.wait("vector", "p3mm1")
            for hl in (0, 1):
                h = 2 * rt + hl
                S.inc("vector", "p3pl", 1,
                      lambda e, hl=hl, h=h, jb=jb: e.tensor_tensor(
                          ploc[:, hl * 128:(hl + 1) * 128],
                          ps_big[:, jb + hl * 512:jb + hl * 512 + 128],
                          cb["d0t"][:, h * 128:(h + 1) * 128], OP.mult))
            S.inc("vector", "p3qs", 1,
                  lambda e, j=j, rt=rt: e.tensor_tensor(
                      qs_t[:, :], qd_p[:, j * 128:(j + 1) * 128],
                      cb["qs"][:, rt * 128:(rt + 1) * 128], OP.mult))
            S.wait("vector", "p3mma")
            S.inc("vector", "p3vs", 1,
                  lambda e, j=j, rt=rt: e.tensor_tensor(
                      vs_t[:, :], v_p[:, j * 128:(j + 1) * 128],
                      cb["u2"][:, rt * 128:(rt + 1) * 128], OP.mult))
            S.wait("tensor", "p3pl")
            S.wait("tensor", "p3qs")
            S.wait("tensor", "p3ev")
            S.wait("tensor", "p3st")
            for hl in range(2):
                S.inc("tensor", "p3mm2", 1,
                      lambda e, hl=hl, j=j: e.matmul(
                          ps_mid[hl * 64:(hl + 1) * 64, 0:128],
                          v_p[:, j * 128 + hl * 64: j * 128 + (hl + 1) * 64],
                          ploc[:, hl * 128:(hl + 1) * 128],
                          start=True, stop=False))
                S.inc("tensor", "p3mm2", 1,
                      lambda e, hl=hl, rt=rt: e.matmul(
                          ps_mid[hl * 64:(hl + 1) * 64, 0:128],
                          state[hl * 64:(hl + 1) * 64, rt * 64:(rt + 1) * 64],
                          qs_t[hl * 64:(hl + 1) * 64, :],
                          start=False, stop=True))
            S.wait("tensor", "p3vs")
            for hl in range(2):
                S.inc("tensor", "p3mma", 1,
                      lambda e, hl=hl, j=j: e.matmul(
                          ps_mid[hl * 64:(hl + 1) * 64, 512:576],
                          kt_p[:, j * 128 + hl * 64: j * 128 + (hl + 1) * 64],
                          vs_t[:, hl * 64:(hl + 1) * 64],
                          start=True, stop=True))
            S.wait("scalar", "p3mm2")
            S.inc("scalar", "p3ev", 1,
                  lambda e, rt=rt, j=j: e.copy(
                      yC[:, rt * T + j * 128:rt * T + (j + 1) * 128],
                      ps_mid[:, 0:128]))
            S.wait("vector", "p3mma")
            S.wait("vector", "p3mm2")
            S.inc("vector", "p3st", 1,
                  lambda e, rt=rt: e.tensor_tensor(
                      state[:, rt * 64:(rt + 1) * 64],
                      state[:, rt * 64:(rt + 1) * 64],
                      cb["c128"][:, rt * 64:(rt + 1) * 64], OP.mult))
            S.inc("vector", "p3st", 1,
                  lambda e, rt=rt: e.tensor_tensor(
                      state[:, rt * 64:(rt + 1) * 64],
                      state[:, rt * 64:(rt + 1) * 64],
                      ps_mid[:, 512:576], OP.add))

    if "yC" in dbg_out:
        S.wait("gpsimd", "p3ev")
        for rt in range(4):
            S.dma("gpsimd", "dbg", dbg_out["yC"][rt * 128:(rt + 1) * 128, :],
                  yC[:, rt * T:(rt + 1) * T])

    if stop_after == "P3":
        with nc.Block() as blk:
            S.run(blk)
        close_all()
        return nc, dbg_out

    st_p23.close()
    S.join(("sync", "tensor", "vector", "scalar", "gpsimd"),
           ("p3mm1", "p3mm2", "p3mma", "ch_vector", "ch_scalar",
            "p2w", "p2h", "p1_ld", "p1_stst"))
    st_p4 = ExitStack()
    regions.append(st_p4)
    ysq = sb("ysq", 128, T, st_p4)
    rowred = sb("rowred", 128, 2, st_p4)
    p4s = sb("p4s", 2, 32, st_p4)
    mean128 = sb("mean128", 128, 4, st_p4)
    rstd128 = sb("rstd128", 128, 4, st_p4)
    ab = sb("ab", 128, 8, st_p4)
    woC_sb = sb("woC_sb", 128, 4 * 1024, st_p4)
    a1tile = sb("a1tile", 128, 1024, st_p4)

    for cc in range(4):
        S.dma("sync", "p4w", woC_sb[:, cc * 1024:(cc + 1) * 1024],
              di["woC"][:, cc * 1024:(cc + 1) * 1024])

    S.wait("scalar", "p3ev")
    S.wait("vector", "p3ev")
    for rt in range(4):
        S.wait("scalar", "p4red")
        S.inc("scalar", "p4sq", 1,
              lambda e, rt=rt: e.activation(ysq[:, :], yC[:, rt * T:(rt + 1) * T],
                                            AF.Square))
        S.wait("vector", "p4smm")
        S.inc("vector", "p4red", 1,
              lambda e, rt=rt: e.reduce_sum(rowred[:, 0:1],
                                            yC[:, rt * T:(rt + 1) * T], axis=AX))
        S.wait("vector", "p4sq")
        S.inc("vector", "p4red", 1,
              lambda e: e.reduce_sum(rowred[:, 1:2], ysq[:, :], axis=AX))
        S.wait("tensor", "p4red")
        S.wait("tensor", "p4sev")
        S.inc("tensor", "p4smm", 1,
              lambda e, rt=rt: e.matmul(ps_a[0:2, 0:2], cb["sel2"][:, :],
                                        rowred[:, 0:2], start=True, stop=True))
        S.wait("scalar", "p4smm")
        S.inc("scalar", "p4sev", 1,
              lambda e, rt=rt: e.copy(p4s[:, rt:rt + 1], ps_a[0:2, 0:1]))
        S.inc("scalar", "p4sev", 1,
              lambda e, rt=rt: e.copy(p4s[:, 4 + rt:5 + rt], ps_a[0:2, 1:2]))
    nfac = 1.0 / (T * 64)
    S.wait("vector", "p4sev")
    S.inc("vector", "p4m", 1,
          lambda e: e.tensor_scalar(p4s[:, 8:12], p4s[:, 0:4], nfac, None, OP.mult))
    S.inc("vector", "p4m", 1,
          lambda e: e.tensor_scalar(p4s[:, 12:16], p4s[:, 4:8], nfac, None, OP.mult))
    S.inc("vector", "p4m", 1,
          lambda e: e.tensor_tensor(p4s[:, 16:20], p4s[:, 8:12], p4s[:, 8:12],
                                    OP.mult))
    S.inc("vector", "p4m", 1,
          lambda e: e.tensor_tensor(p4s[:, 16:20], p4s[:, 12:16], p4s[:, 16:20],
                                    OP.subtract))
    S.inc("vector", "p4m", 1,
          lambda e: e.tensor_scalar(p4s[:, 20:24], p4s[:, 16:20], LN_EPS, None,
                                    OP.add))
    S.inc("vector", "p4m", 1, lambda e: e.reciprocal(p4s[:, 24:28], p4s[:, 20:24]))
    S.wait("scalar", "p4m")
    S.inc("scalar", "p4rs", 1,
          lambda e: e.activation(p4s[:, 28:32], p4s[:, 24:28], AF.Sqrt))
    S.wait("tensor", "p4rs")
    S.wait("tensor", "p4sev")
    S.inc("tensor", "p4bmm", 1,
          lambda e: e.matmul(ps_b[:, 0:4], cb["sel2T"][:, :], p4s[:, 8:12],
                             start=True, stop=True))
    S.inc("tensor", "p4bmm", 1,
          lambda e: e.matmul(ps_mid[:, 0:4], cb["sel2T"][:, :], p4s[:, 28:32],
                             start=True, stop=True))
    S.wait("scalar", "p4bmm")
    S.inc("scalar", "p4bev", 1, lambda e: e.copy(mean128[:, :], ps_b[:, 0:4]))
    S.inc("scalar", "p4bev", 1, lambda e: e.copy(rstd128[:, :], ps_mid[:, 0:4]))
    S.wait("vector", "p4bev")
    S.inc("vector", "p4ab", 1,
          lambda e: e.tensor_tensor(ab[:, 0:4], rstd128[:, :], cb["gnw128"][:, :],
                                    OP.mult))
    S.inc("vector", "p4ab", 1,
          lambda e: e.tensor_tensor(ab[:, 4:8], mean128[:, :], ab[:, 0:4], OP.mult))
    S.inc("vector", "p4ab", 1,
          lambda e: e.tensor_tensor(ab[:, 4:8], cb["gnb128"][:, :], ab[:, 4:8],
                                    OP.subtract))
    p4st_q = [0, 0, 0, 0]
    ygt = sb("ygt", 128, 2 * 128, st_p4)
    for tt in range(NT):
        S.wait("tensor", "p4w")
        S.wait("tensor", "p4ev")
        for cc in range(4):
            par = cc % 2
            S.wait("vector", "p4mm")
            S.wait("vector", "p4ab")
            S.inc("vector", "p4ap", 1,
                  lambda e, tt=tt, cc=cc, par=par: e.tensor_scalar(
                      ygt[:, par * 128:(par + 1) * 128],
                      yC[:, cc * T + tt * 128:cc * T + (tt + 1) * 128],
                      ab[:, cc:cc + 1], ab[:, 4 + cc:5 + cc],
                      OP.mult, OP.add))
            S.wait("tensor", "p4ap")
            for oh in range(2):
                S.inc("tensor", "p4mm", 1,
                      lambda e, oh=oh, cc=cc, par=par: e.matmul(
                          ps_mid[:, oh * 512:(oh + 1) * 512],
                          ygt[:, par * 128:(par + 1) * 128],
                          woC_sb[:, cc * 1024 + oh * 512:cc * 1024 + (oh + 1) * 512],
                          start=(cc == 0), stop=(cc == 3)))
        S.wait("scalar", "p4mm")
        S.wait("scalar", "p4st")
        S.inc("scalar", "p4ev", 1,
              lambda e: e.copy(a1tile[:, :], ps_mid[:, :]))
        S.wait("gpsimd", "p4ev")
        S.dma("gpsimd", "p4st", sc["cc_a1in"][tt * 128:(tt + 1) * 128, :],
              a1tile[:, :])
        if tt % 8 == 7:
            p4st_q[tt // 8] = S.cnt["p4st"]

    if "attn1p" in dbg_out:
        S.wait("gpsimd", "p4st")
        S.dma("gpsimd", "dbg", dbg_out["attn1p"][:, :], sc["cc_a1in"][:, :])

    if stop_after == "P4":
        with nc.Block() as blk:
            S.run(blk)
        close_all()
        return nc, dbg_out

    st_p4.close()
    st_y.close()
    S.join(ENGS, ("p4mm", "p4red", "p4sq", "ch_vector", "ch_scalar"))
    gst = {}

    def pid_reg(e):
        if "pid" not in gst:
            gst["pid"] = e.to_reg(e.partition_id())
        return gst["pid"]

    def partner_copy(ccname, dsts_srcs_fn):
        pairs0 = dsts_srcs_fn(0)
        n = len(pairs0)
        S.sem(ccname)
        S.cnt[ccname] += 16 * n
        S.names[ccname] = (ccname, S.cnt[ccname])
        h = S.h[ccname]

        def go(e, fn=dsts_srcs_fn, h=h):
            reg = pid_reg(e)
            for j in range(8):
                with e.If_eq(reg, j):
                    for dst, srcap in fn(j):
                        e.dma_start(out=dst, in_=srcap).then_inc(h, 16)
        S.emit("gpsimd", go)

    for qq, onm in enumerate(("A", "B", "C", "D")):
        S.wait("gpsimd", "p4st", target=p4st_q[qq])
        S.inc("gpsimd", "p5cc" + onm, 1,
              lambda e, qq=qq, onm=onm: e.collective_compute(
                  "AllGather", OP.bypass, replica_groups=[list(range(8))],
                  ins=[sc["cc_a1in"][qq * 1024:(qq + 1) * 1024, :].opt()],
                  outs=[sc["cc_a1out" + onm][:, :, :].opt()]))
        S.wait("gpsimd", "p5cc" + onm)
        partner_copy("p5cp" + onm,
                     lambda j, qq=qq, onm=onm: [
                         (sc["a1p"][qq * 1024:(qq + 1) * 1024, :],
                          sc["cc_a1out" + onm][j ^ 1])])

    st_p5 = ExitStack()
    regions.append(st_p5)
    g1bc = sb("g1bc", 128, C, st_p5)
    b1bc = sb("b1bc", 128, C, st_p5)
    p5a = sb("p5_a", 128, 2 * C, st_p5)
    p5b = sb("p5_b", 128, 2 * C, st_p5)
    p5c = sb("p5_c", 128, 2 * C, st_p5)
    p5d = sb("p5_d", 128, 2 * C, st_p5)
    p5e = sb("p5_e", 128, 2 * C, st_p5)
    sq5 = sb("p5_sqd", 128, C, st_p5)
    ln5 = sb("p5_lnb", 128, 12 * NT, st_p5)
    tp5 = sb("p5_tp", 128, 2 * 128, st_p5)

    emit_bcast(g1bc, "ln1g")
    emit_bcast(b1bc, "ln1b")

    def p5_finalize(pfx, sums, sq, negmean, rstd, w0, w1):
        S.wait("vector", pfx + "sq")
        S.inc("vector", pfx + "fz", 1,
              lambda e: e.tensor_scalar(negmean, sums, -1.0 / C, None, OP.mult))
        S.inc("vector", pfx + "fz", 1,
              lambda e: e.tensor_scalar(w0, sq, 1.0 / C, None, OP.mult))
        S.inc("vector", pfx + "fz", 1,
              lambda e: e.tensor_tensor(w1, negmean, negmean, OP.mult))
        S.inc("vector", pfx + "fz", 1,
              lambda e: e.tensor_tensor(w1, w0, w1, OP.subtract))
        S.inc("vector", pfx + "fz", 1,
              lambda e: e.tensor_scalar(w0, w1, LN_EPS, None, OP.add))
        S.inc("vector", pfx + "fz", 1, lambda e: e.reciprocal(w1, w0))
        S.wait("scalar", pfx + "fz")
        S.inc("scalar", pfx + "rst", 1, lambda e: e.activation(rstd, w1, AF.Sqrt))

    for i in range(NT):
        par = i % 2
        S.wait("sync", "p5cp" + "ABCD"[i // 8])
        S.wait_lag("sync", "p5tt", 2)
        S.dma("sync", "p5ld%d" % par, p5a[:, par * C:(par + 1) * C],
              di["x"][i * 128:(i + 1) * 128, :])
        S.dma("sync", "p5ld%d" % par, p5b[:, par * C:(par + 1) * C],
              sc["cc_a1in"][i * 128:(i + 1) * 128, :])
        S.dma("sync", "p5ld%d" % par, p5c[:, par * C:(par + 1) * C],
              sc["a1p"][i * 128:(i + 1) * 128, :])
        S.wait("vector", "p5ld%d" % par)
        S.wait("vector", "p5x2st%d" % par)
        S.wait_lag("vector", "p5xsq", 1)
        S.inc("vector", "p5tt", 1,
              lambda e, par=par: e.tensor_tensor(
                  p5d[:, par * C:(par + 1) * C], p5b[:, par * C:(par + 1) * C],
                  p5c[:, par * C:(par + 1) * C], OP.add))
        S.inc("vector", "p5tt", 1,
              lambda e, par=par: e.tensor_tensor(
                  p5d[:, par * C:(par + 1) * C], p5a[:, par * C:(par + 1) * C],
                  p5d[:, par * C:(par + 1) * C], OP.add))
        S.inc("vector", "p5xrs", 1,
              lambda e, par=par, i=i: e.reduce_sum(
                  ln5[:, i:i + 1], p5d[:, par * C:(par + 1) * C], axis=AX))
        S.wait("scalar", "p5tt")
        S.inc("scalar", "p5xsq", 1,
              lambda e, par=par, i=i: e.activation(
                  sq5[:, :], p5d[:, par * C:(par + 1) * C], AF.Square,
                  accum_out=ln5[:, NT + i:NT + i + 1]))
        S.wait("gpsimd", "p5tt")
        S.dma("gpsimd", "p5x2st%d" % par, sc["x2"][i * 128:(i + 1) * 128, :],
              p5d[:, par * C:(par + 1) * C])
    p5_finalize("p5x", ln5[:, 0:NT], ln5[:, NT:2 * NT], ln5[:, 2 * NT:3 * NT],
                ln5[:, 3 * NT:4 * NT], ln5[:, 8 * NT:9 * NT],
                ln5[:, 9 * NT:10 * NT])
    S.wait("sync", "p5tt")
    S.wait("sync", "p5xrs")
    S.wait("sync", "p5xsq")
    for i in range(NT):
        par = i % 2
        S.wait_lag("sync", "p5ap", 1)
        S.wait("sync", "p5x2st%d" % par)
        S.dma("sync", "p5ld2%d" % par, p5d[:, par * C:(par + 1) * C],
              sc["x2"][i * 128:(i + 1) * 128, :])
        S.wait("vector", "p5ld2%d" % par)
        S.wait("vector", "p5xrst")
        S.wait("vector", "p5h2st%d" % par)
        S.wait_lag("vector", "p5ysq", 1)
        S.inc("vector", "p5ap", 1,
              lambda e, i=i, par=par: e.tensor_scalar(
                  p5e[:, par * C:(par + 1) * C], p5d[:, par * C:(par + 1) * C],
                  ln5[:, 2 * NT + i:2 * NT + i + 1],
                  ln5[:, 3 * NT + i:3 * NT + i + 1], OP.add, OP.mult))
        S.inc("vector", "p5yrs", 1,
              lambda e, par=par, i=i: e.reduce_sum(
                  ln5[:, 4 * NT + i:4 * NT + i + 1],
                  p5e[:, par * C:(par + 1) * C], axis=AX))
        S.wait("scalar", "p5ap")
        S.inc("scalar", "p5ysq", 1,
              lambda e, par=par, i=i: e.activation(
                  sq5[:, :], p5e[:, par * C:(par + 1) * C], AF.Square,
                  accum_out=ln5[:, 5 * NT + i:5 * NT + i + 1]))
        S.wait("gpsimd", "p5ap")
        S.dma("gpsimd", "p5h2st%d" % par, sc["h2"][i * 128:(i + 1) * 128, :],
              p5e[:, par * C:(par + 1) * C])
    p5_finalize("p5y", ln5[:, 4 * NT:5 * NT], ln5[:, 5 * NT:6 * NT],
                ln5[:, 6 * NT:7 * NT], ln5[:, 7 * NT:8 * NT],
                ln5[:, 8 * NT:9 * NT], ln5[:, 9 * NT:10 * NT])
    S.wait("sync", "p5ap")
    S.wait("sync", "p5yrs")
    S.wait("sync", "p5ysq")
    for i in range(NT):
        par = i % 2
        S.wait_lag("sync", "p5af", 1)
        S.wait("sync", "p5h2st%d" % par)
        S.dma("sync", "p5ld3%d" % par, p5d[:, par * C:(par + 1) * C],
              sc["h2"][i * 128:(i + 1) * 128, :])
        S.wait("vector", "p5ld3%d" % par)
        S.wait("vector", "p5yrst")
        S.wait("vector", "p5_mm")
        S.wait("vector", "p0b_ev")
        S.inc("vector", "p5af", 1,
              lambda e, i=i, par=par: e.tensor_scalar(
                  p5e[:, par * C:(par + 1) * C], p5d[:, par * C:(par + 1) * C],
                  ln5[:, 6 * NT + i:6 * NT + i + 1],
                  ln5[:, 7 * NT + i:7 * NT + i + 1], OP.add, OP.mult))
        S.inc("vector", "p5af", 1,
              lambda e, par=par: e.tensor_tensor(
                  p5e[:, par * C:(par + 1) * C], p5e[:, par * C:(par + 1) * C],
                  g1bc[:, :], OP.mult))
        S.inc("vector", "p5af", 1,
              lambda e, par=par: e.tensor_tensor(
                  p5e[:, par * C:(par + 1) * C], p5e[:, par * C:(par + 1) * C],
                  b1bc[:, :], OP.add))
        for ccj in range(8):
            tp = ccj % 2
            tps = ps_a if tp == 0 else ps_b
            S.wait("tensor", "p5af")
            S.wait_lag("tensor", "p5_ev", 1)
            S.inc("tensor", "p5_mm", 1,
                  lambda e, ccj=ccj, tps=tps, par=par: e.transpose(
                      tps[:, 0:128],
                      p5e[:, par * C + ccj * 128:par * C + (ccj + 1) * 128],
                      cb["ident"][:, :]))
            S.wait("scalar", "p5_mm")
            S.wait("scalar", "p5st%d" % tp)
            S.inc("scalar", "p5_ev", 1,
                  lambda e, tp=tp, tps=tps: e.copy(
                      tp5[:, tp * 128:(tp + 1) * 128], tps[:, 0:128]))
            S.wait("gpsimd", "p5_ev")
            S.dma("gpsimd", "p5st%d" % tp,
                  sc["h3C"][ccj * 128:(ccj + 1) * 128, i * 128:(i + 1) * 128],
                  tp5[:, tp * 128:(tp + 1) * 128])

    if "x2" in dbg_out:
        S.wait("gpsimd", "p5x2st0")
        S.wait("gpsimd", "p5x2st1")
        S.wait("gpsimd", "p5h2st0")
        S.wait("gpsimd", "p5h2st1")
        S.dma("gpsimd", "dbg", dbg_out["x2"][:, :], sc["x2"][:, :])
        S.dma("gpsimd", "dbg", dbg_out["h2"][:, :], sc["h2"][:, :])
    if "h3C" in dbg_out:
        S.wait("gpsimd", "p5st0")
        S.wait("gpsimd", "p5st1")
        for cc in range(8):
            S.dma("gpsimd", "dbg", dbg_out["h3C"][cc * 128:(cc + 1) * 128, :],
                  sc["h3C"][cc * 128:(cc + 1) * 128, :])

    if stop_after == "P5":
        with nc.Block() as blk:
            S.run(blk)
        close_all()
        return nc, dbg_out

    st_p5.close()
    S.join(ENGS, ("p5_mm", "p5_ev", "p5st0", "p5st1", "p5ld0", "p5ld1",
                  "p5ld20", "p5ld21", "p5ld30", "p5ld31",
                  "p5x2st0", "p5x2st1", "p5h2st0", "p5h2st1",
                  "ch_vector", "ch_scalar"))
    st_p6 = ExitStack()
    regions.append(st_p6)
    crep_sb = sb("crep_sb", 128, T, st_p6)
    srep_sb = sb("srep_sb", 128, T, st_p6)
    wqkv_sb = sb("wqkv_sb", 128, 8 * 1536, st_p6)
    h3c_ch6 = sb("h3c_ch6", 128, 2 * 8 * 512, st_p6)
    qraw = sb("qraw", 128, 4 * 512, st_p6)
    rt1 = sb("rt1", 128, 512, st_p6)
    rt2 = sb("rt2", 128, 512, st_p6)
    ro1 = sb("ro1", 128, 512, st_p6)
    v2at = sb("v2at6", 128, 8 * 65, st_p6)

    S.dma("sync", "p6c", crep_sb[:, :], di["crep"][:, :])
    S.dma("sync", "p6c", srep_sb[:, :], di["srep"][:, :])
    for cc in range(8):
        S.dma("sync", "p6c", wqkv_sb[:, cc * 1536:(cc + 1) * 1536],
              di["WqkvC"][:, cc * 1536:(cc + 1) * 1536])
    S.inc("vector", "p6one", 1,
          lambda e: e.memset(v2at[:, :].rearrange("a (h c) -> a h c", c=65)[:, :, 64:65], 1.0))

    def rope_phase(base, dst):
        for tc in range(NCH):
            pass

    for tc in range(NCH):
        t6 = tc % 2
        S.wait("sync", "p6mm", target=max(0, S.cnt.get("p6mm", 0) - 96))
        for cc in range(8):
            S.dma("sync", "p6h%d" % t6,
                  h3c_ch6[:, t6 * 4096 + cc * 512:t6 * 4096 + (cc + 1) * 512],
                  sc["h3C"][cc * 128:(cc + 1) * 128, tc * 512:(tc + 1) * 512])
        for base, dst in ((0, sc["q2r"]), (512, sc["k2r"])):
            S.wait("tensor", "p6h%d" % t6)
            S.wait("tensor", "p6c")
            S.wait("tensor", "p6ev")
            for rb in range(4):
                for cc in range(8):
                    S.inc("tensor", "p6mm", 1,
                          lambda e, rb=rb, cc=cc, base=base, t6=t6: e.matmul(
                              ps_big[:, rb * 512:(rb + 1) * 512],
                              wqkv_sb[:, cc * 1536 + base + rb * 128:
                                      cc * 1536 + base + (rb + 1) * 128],
                              h3c_ch6[:, t6 * 4096 + cc * 512:
                                      t6 * 4096 + (cc + 1) * 512],
                              start=(cc == 0), stop=(cc == 7)))
            S.wait("scalar", "p6mm")
            S.wait("scalar", "p6rp")
            for rb in range(4):
                S.inc("scalar", "p6ev", 1,
                      lambda e, rb=rb: e.copy(qraw[:, rb * 512:(rb + 1) * 512],
                                              ps_big[:, rb * 512:(rb + 1) * 512]))
            for p in range(2):
                cs = crep_sb[:, tc * 512:(tc + 1) * 512]
                sn = srep_sb[:, tc * 512:(tc + 1) * 512]
                S.wait("vector", "p6ev")
                S.wait("vector", "p6vst")
                S.inc("vector", "p6rp", 1,
                      lambda e, p=p, cs=cs: e.tensor_tensor(
                          rt1[:, :], qraw[:, p * 512:(p + 1) * 512], cs, OP.mult))
                S.inc("vector", "p6rp", 1,
                      lambda e, p=p, sn=sn: e.tensor_tensor(
                          rt2[:, :], qraw[:, (p + 2) * 512:(p + 3) * 512], sn,
                          OP.mult))
                S.wait("vector", "p6vst")
                S.inc("vector", "p6rp", 1,
                      lambda e: e.tensor_tensor(ro1[:, :], rt1[:, :], rt2[:, :],
                                                OP.subtract))
                S.wait("gpsimd", "p6rp")
                for hi in range(4):
                    S.dma("gpsimd", "p6vst",
                          dst[(p * 4 + hi) * 64 + 0:(p * 4 + hi) * 64 + 32,
                              tc * 512:(tc + 1) * 512],
                          ro1[hi * 32:(hi + 1) * 32, :])
                S.inc("vector", "p6rp", 1,
                      lambda e, p=p, sn=sn: e.tensor_tensor(
                          rt1[:, :], qraw[:, p * 512:(p + 1) * 512], sn, OP.mult))
                S.inc("vector", "p6rp", 1,
                      lambda e, p=p, cs=cs: e.tensor_tensor(
                          rt2[:, :], qraw[:, (p + 2) * 512:(p + 3) * 512], cs,
                          OP.mult))
                S.wait("vector", "p6vst")
                S.inc("vector", "p6rp", 1,
                      lambda e: e.tensor_tensor(ro1[:, :], rt1[:, :], rt2[:, :],
                                                OP.add))
                S.wait("gpsimd", "p6rp")
                for hi in range(4):
                    S.dma("gpsimd", "p6vst",
                          dst[(p * 4 + hi) * 64 + 32:(p * 4 + hi) * 64 + 64,
                              tc * 512:(tc + 1) * 512],
                          ro1[hi * 32:(hi + 1) * 32, :])
        S.wait("tensor", "p6ev")
        for ts in range(4):
            for cc in range(8):
                S.inc("tensor", "p6mm", 1,
                      lambda e, ts=ts, cc=cc, t6=t6: e.matmul(
                          ps_big[:, ts * 512:(ts + 1) * 512],
                          h3c_ch6[:, t6 * 4096 + cc * 512 + ts * 128:
                                  t6 * 4096 + cc * 512 + (ts + 1) * 128],
                          wqkv_sb[:, cc * 1536 + 1024:cc * 1536 + 1536],
                          start=(cc == 0), stop=(cc == 7)))
        for ts in range(4):
            S.wait("scalar", "p6mm")
            S.wait("scalar", "p6vst")
            S.wait("scalar", "p6one")
            S.inc("scalar", "p6ev", 1,
                  lambda e, ts=ts: e.copy(
                      v2at[:, :].rearrange("a (h c) -> a h c", c=65)[:, :, 0:64],
                      ps_big[:, ts * 512:(ts + 1) * 512].rearrange(
                          "a (h c) -> a h c", c=64)))
            S.wait("gpsimd", "p6ev")
            S.dma("gpsimd", "p6vst",
                  sc["v2a"][tc * 512 + ts * 128:tc * 512 + (ts + 1) * 128, :, :],
                  v2at[:, :].rearrange("a (h c) -> a h c", c=65))

    if "q2r" in dbg_out:
        S.wait("gpsimd", "p6vst")
        for rr in range(4):
            S.dma("gpsimd", "dbg", dbg_out["q2r"][rr * 128:(rr + 1) * 128, :],
                  sc["q2r"][rr * 128:(rr + 1) * 128, :])
            S.dma("gpsimd", "dbg", dbg_out["k2r"][rr * 128:(rr + 1) * 128, :],
                  sc["k2r"][rr * 128:(rr + 1) * 128, :])
        S.dma("gpsimd", "dbg", dbg_out["v2a"][:, :],
              sc["v2a"][:, :, :].rearrange("a h c -> a (h c)"))

    if stop_after == "P6":
        with nc.Block() as blk:
            S.run(blk)
        close_all()
        return nc, dbg_out

    st_p6.close()
    S.join(ENGS, ("p6mm", "p6ev", "p6rp", "p6vst", "p6h", "p6c",
                  "ch_vector", "ch_scalar"))
    st_p7 = ExitStack()
    regions.append(st_p7)
    dmask_sb = sb("dmask_sb", 128, 4 * 512, st_p7)
    ebias = sb("ebias", 128, 1, st_p7)
    wproj_sb = sb("wproj_sb", 128, 4 * 1024, st_p7)
    q2h = sb("q2h", 128, T, st_p7)
    k2h = sb("k2h", 128, T, st_p7)
    v2h = sb("v2h", 128, 2 * 32 * 65, st_p7)
    pexp = sb("pexp", 128, 2048, st_p7)
    rrow = sb("rrow", 1, 512, st_p7)
    rbc = sb("rbc", 64, 512, st_p7)
    rbc2 = sb("rbc2", 64, 512, st_p7)
    attd = sb("attd", 64, 512, st_p7)
    atile = sb("atile", 128, 2 * 128, st_p7)
    a2tile = sb("a2tile", 128, 1024, st_p7)

    S.dma("sync", "p7c", dmask_sb[:, :], di["dmask"][:, :])
    S.inc("vector", "p7eb", 1, lambda e: e.memset(ebias[:, :], -EXPC))
    for cc in range(4):
        S.dma("sync", "p7c", wproj_sb[:, cc * 1024:(cc + 1) * 1024],
              di["WprojR"][:, cc * 1024:(cc + 1) * 1024])

    for h in range(HL):
        hp = h % 2
        nmm1 = 2 * sum(range(2, 2 * NCH + 2, 2))
        S.wait("sync", "p7mm1",
               target=max(0, S.cnt.get("p7mm1", 0) - nmm1))
        S.wait("sync", "p7mm2",
               target=max(0, S.cnt.get("p7mm2", 0) - nmm1))
        S.dma("sync", "p7ldh%d" % hp, q2h[hp * 64:(hp + 1) * 64, :],
              sc["q2r"][h * 64:(h + 1) * 64, :])
        S.dma("sync", "p7ldh%d" % hp, k2h[hp * 64:(hp + 1) * 64, :],
              sc["k2r"][h * 64:(h + 1) * 64, :])
        S.dma("sync", "p7ldh%d" % hp,
              v2h[:, hp * 2080:(hp + 1) * 2080].rearrange(
                  "a (j c) -> a j c", c=65),
              sc["v2a"][:, h, :].rearrange("(j p) c -> p j c", p=128))
        for qc in range(NCH):
            ngr = 2 * (qc + 1)

            def mm1_group(g, h=h, qc=qc, hp=hp):
                st = g % 2
                S.wait("tensor", "p7ldh%d" % hp)
                S.wait_lag("tensor", "p7ex", 1)
                for kj2 in range(2):
                    f = 2 * g + kj2
                    S.inc("tensor", "p7mm1", 1,
                          lambda e, qc=qc, f=f, st=st, kj2=kj2, hp=hp: e.matmul(
                              ps_big[:, st * 1024 + kj2 * 512:
                                     st * 1024 + (kj2 + 1) * 512],
                              k2h[hp * 64:(hp + 1) * 64, f * 128:(f + 1) * 128],
                              q2h[hp * 64:(hp + 1) * 64,
                                  qc * 512:(qc + 1) * 512],
                              start=True, stop=True))

            yb = qc % 2
            psY = ps_mid[0:65, yb * 512:(yb + 1) * 512]
            mm1_group(0)
            for g in range(ngr):
                st = g % 2
                S.wait("scalar", "p7mm1")
                S.wait("scalar", "p7mm2",
                       target=max(0, S.cnt.get("p7mm2", 0) - 2))
                S.wait("scalar", "p7msk")
                S.wait("scalar", "p7eb")
                S.inc("scalar", "p7ex", 1,
                      lambda e, st=st: e.activation(
                          pexp[:, st * 1024:(st + 1) * 1024],
                          ps_big[:, st * 1024:(st + 1) * 1024], AF.Exp,
                          bias=ebias[:, 0:1], scale=0.125))
                diag = (2 * g >= 4 * qc)
                if diag:
                    mo = (2 * g - 4 * qc) * 512
                    S.wait("vector", "p7ex")
                    S.wait("vector", "p7c")
                    S.inc("vector", "p7msk", 1,
                          lambda e, st=st, mo=mo: e.tensor_tensor(
                              pexp[:, st * 1024:(st + 1) * 1024],
                              pexp[:, st * 1024:(st + 1) * 1024],
                              dmask_sb[:, mo:mo + 1024], OP.mult))
                if g + 1 < ngr:
                    mm1_group(g + 1)
                if diag:
                    S.wait("tensor", "p7msk")
                S.wait("tensor", "p7ex")
                if g == 0:
                    S.wait_lag("tensor", "p7bcv", 1)
                for kj2 in range(2):
                    f = 2 * g + kj2
                    S.inc("tensor", "p7mm2", 1,
                          lambda e, f=f, st=st, kj2=kj2, g=g, ngr=ngr, psY=psY,
                          hp=hp: e.matmul(
                              psY,
                              v2h[:, hp * 2080 + f * 65:
                                  hp * 2080 + (f + 1) * 65],
                              pexp[:, st * 1024 + kj2 * 512:
                                   st * 1024 + (kj2 + 1) * 512],
                              start=(g == 0 and kj2 == 0),
                              stop=(g == ngr - 1 and kj2 == 1)))
            S.wait("scalar", "p7mm2")
            S.wait("scalar", "p7bc")
            S.inc("scalar", "p7rr", 1,
                  lambda e, psY=psY: e.copy(rrow[:, :], psY[64:65, 0:512]))
            S.wait("tensor", "p7rr")
            S.wait("tensor", "p7bce")
            S.inc("tensor", "p7bc", 1,
                  lambda e: e.matmul(ps_b[0:64, 0:512], cb["ones1_64"][:, :],
                                     rrow[:, :], start=True, stop=True))
            S.wait("scalar", "p7bc")
            S.wait("scalar", "p7rcp")
            S.inc("scalar", "p7bce", 1,
                  lambda e: e.copy(rbc[:, :], ps_b[0:64, 0:512]))
            S.wait("vector", "p7bce")
            S.wait("vector", "p7rcp")
            S.inc("vector", "p7rcp", 1,
                  lambda e: e.reciprocal(rbc2[:, :], rbc[:, :]))
            S.wait("vector", "p7mm2")
            S.wait("vector", "p7ast")
            S.inc("vector", "p7bcv", 1,
                  lambda e, psY=psY: e.tensor_tensor(
                      attd[:, :], psY[0:64, 0:512], rbc2[:, :], OP.mult))
            S.wait("gpsimd", "p7bcv")
            S.dma("gpsimd", "p7ast",
                  sc["att"][h * 64:(h + 1) * 64, qc * 512:(qc + 1) * 512],
                  attd[:, :])

    if "att" in dbg_out:
        S.wait("gpsimd", "p7ast")
        for rr in range(4):
            S.dma("gpsimd", "dbg", dbg_out["att"][rr * 128:(rr + 1) * 128, :],
                  sc["att"][rr * 128:(rr + 1) * 128, :])

    attr = sb("p7_attr", 128, 4 * 4096, st_p7)
    S.wait("sync", "p7ast")
    for cc in range(4):
        S.dma("sync", "p7attld", attr[:, cc * 4096:(cc + 1) * 4096],
              sc["att"][cc * 128:(cc + 1) * 128, :])
    for tt in range(NT):
        S.wait("tensor", "p7attld")
        S.wait("tensor", "p7pev")
        for cc in range(4):
            for oh in range(2):
                S.inc("tensor", "p7pmm", 1,
                      lambda e, cc=cc, oh=oh, tt=tt: e.matmul(
                          ps_mid[:, oh * 512:(oh + 1) * 512],
                          attr[:, cc * 4096 + tt * 128:cc * 4096 + (tt + 1) * 128],
                          wproj_sb[:, cc * 1024 + oh * 512:
                                   cc * 1024 + (oh + 1) * 512],
                          start=(cc == 0), stop=(cc == 3)))
        S.wait("scalar", "p7pmm")
        S.wait("scalar", "p7pst")
        S.inc("scalar", "p7pev", 1,
              lambda e: e.copy(a2tile[:, :], ps_mid[:, :]))
        S.wait("gpsimd", "p7pev")
        S.dma("gpsimd", "p7pst", sc["cc_a2in"][tt * 128:(tt + 1) * 128, :],
              a2tile[:, :])
        if tt == 15:
            p7pst_half = S.cnt["p7pst"]

    if "attn2p" in dbg_out:
        S.wait("gpsimd", "p7pst")
        S.dma("gpsimd", "dbg", dbg_out["attn2p"][:, :], sc["cc_a2in"][:, :])

    if stop_after == "P7":
        with nc.Block() as blk:
            S.run(blk)
        close_all()
        return nc, dbg_out

    st_p7.close()
    S.join(ENGS, ("p7mm1", "p7mm2", "p7pmm", "p7attld", "p7ldh0", "p7ldh1", "p7pst",
                  "p7ast", "p7c", "ch_vector", "ch_scalar"))

    S.wait("gpsimd", "p7pst", target=p7pst_half)
    S.inc("gpsimd", "p8ccA", 1,
          lambda e: e.collective_compute(
              "AllGather", OP.bypass, replica_groups=[list(range(8))],
              ins=[sc["cc_a2in"][0:TH, :].opt()],
              outs=[sc["cc_a2outA"][:, :, :].opt()]))
    S.wait("gpsimd", "p8ccA")

    def p8pairsA(j):
        hg = j % 2
        sl = slice(hg * TH, (hg + 1) * TH)
        pairs = [(sc["h2own"][:, :], sc["h2"][sl, :]),
                 (sc["x2own"][:, :], sc["x2"][sl, :])]
        if hg == 0:
            pairs.append((sc["a2p"][0:TH, :], sc["cc_a2outA"][j ^ 1]))
        else:
            pairs.append((sc["a1p"][0:TH, :], sc["cc_a2in"][sl, :]))
        return pairs
    partner_copy("p8cpA", p8pairsA)
    S.wait("gpsimd", "p7pst")
    S.inc("gpsimd", "p8ccB", 1,
          lambda e: e.collective_compute(
              "AllGather", OP.bypass, replica_groups=[list(range(8))],
              ins=[sc["cc_a2in"][TH:T, :].opt()],
              outs=[sc["cc_a2outB"][:, :, :].opt()]))
    S.wait("gpsimd", "p8ccB")

    def p8pairsB(j):
        hg = j % 2
        sl = slice(hg * TH, (hg + 1) * TH)
        if hg == 0:
            return [(sc["a1p"][0:TH, :], sc["cc_a2in"][sl, :])]
        return [(sc["a2p"][0:TH, :], sc["cc_a2outB"][j ^ 1])]
    partner_copy("p8cpB", p8pairsB)

    gates_all = sb("gates_all", 128, 16 * 8)
    gk_all = sb("gk_all", 128, 16 * 8)

    st_p8 = ExitStack()
    regions.append(st_p8)
    h4C = sb("h4C", 128, 8 * TH, st_p8)
    g2bc = sb("g2bc", 128, C, st_p8)
    b2bc = sb("b2bc", 128, C, st_p8)
    p8a = sb("p8_a", 128, 2 * C, st_p8)
    p8b = sb("p8_b", 128, 2 * C, st_p8)
    p8c = sb("p8_c", 128, 2 * C, st_p8)
    p8d = sb("p8_d", 128, 2 * C, st_p8)
    p8e = sb("p8_e", 128, 2 * C, st_p8)
    sq8 = sb("p8_sqd", 128, C, st_p8)
    ln8 = sb("p8_lnb", 128, 8 * 16, st_p8)
    tp8 = sb("p8_tp", 128, 2 * 128, st_p8)
    wg_sb = sb("wg_sb", 128, 8 * 8, st_p8)
    gl8 = sb("p8_gl", 128, 8, st_p8)
    ge8 = sb("p8_ge", 128, 8, st_p8)
    gs8 = sb("p8_gs", 128, 4, st_p8)
    gm8 = sb("p8_gm", 128, 8, st_p8)
    mk8 = sb("p8_mk", 128, 8, st_p8)
    w8a = sb("p8_w8a", 128, 16, st_p8)
    w8b = sb("p8_w8b", 128, 16, st_p8)

    S.dma("sync", "p8c", wg_sb[:, :], di["WgateC"][:, :])
    emit_bcast(g2bc, "ln2g")
    emit_bcast(b2bc, "ln2b")
    NTH = TH // 128
    for i in range(NTH):
        par = i % 2
        S.wait("sync", "p8cpA")
        S.wait("sync", "p8cpB")
        S.wait_lag("sync", "p8tt", 2)
        S.dma("sync", "p8ld%d" % par, p8a[:, par * C:(par + 1) * C],
              sc["h2own"][i * 128:(i + 1) * 128, :])
        S.dma("sync", "p8ld%d" % par, p8b[:, par * C:(par + 1) * C],
              sc["a1p"][i * 128:(i + 1) * 128, :])
        S.dma("sync", "p8ld%d" % par, p8c[:, par * C:(par + 1) * C],
              sc["a2p"][i * 128:(i + 1) * 128, :])
        S.wait("vector", "p8ld%d" % par)
        S.wait("vector", "p8x3st%d" % par)
        S.wait_lag("vector", "p8xsq", 1)
        S.inc("vector", "p8tt", 1,
              lambda e, par=par: e.tensor_tensor(
                  p8d[:, par * C:(par + 1) * C], p8b[:, par * C:(par + 1) * C],
                  p8c[:, par * C:(par + 1) * C], OP.add))
        S.inc("vector", "p8tt", 1,
              lambda e, par=par: e.tensor_tensor(
                  p8d[:, par * C:(par + 1) * C], p8a[:, par * C:(par + 1) * C],
                  p8d[:, par * C:(par + 1) * C], OP.add))
        S.inc("vector", "p8xrs", 1,
              lambda e, par=par, i=i: e.reduce_sum(
                  ln8[:, i:i + 1], p8d[:, par * C:(par + 1) * C], axis=AX))
        S.wait("scalar", "p8tt")
        S.inc("scalar", "p8xsq", 1,
              lambda e, par=par, i=i: e.activation(
                  sq8[:, :], p8d[:, par * C:(par + 1) * C], AF.Square,
                  accum_out=ln8[:, 16 + i:16 + i + 1]))
        S.wait("gpsimd", "p8tt")
        S.dma("gpsimd", "p8x3st%d" % par, sc["x3"][i * 128:(i + 1) * 128, :],
              p8d[:, par * C:(par + 1) * C])
    p5_finalize("p8x", ln8[:, 0:16], ln8[:, 16:32], ln8[:, 32:48],
                ln8[:, 48:64], w8a[:, :], w8b[:, :])
    S.wait("sync", "p8tt")
    S.wait("sync", "p8xrs")
    S.wait("sync", "p8xsq")
    for i in range(NTH):
        par = i % 2
        S.wait_lag("sync", "p8af", 4)
        S.wait("sync", "p8x3st%d" % par)
        S.dma("sync", "p8ld2%d" % par, p8d[:, par * C:(par + 1) * C],
              sc["x3"][i * 128:(i + 1) * 128, :])
        S.wait("vector", "p8ld2%d" % par)
        S.wait("vector", "p8xrst")
        S.wait("vector", "p8_mm")
        S.wait("vector", "p8h4st%d" % par)
        S.wait("vector", "p0b_ev")
        S.inc("vector", "p8af", 1,
              lambda e, i=i, par=par: e.tensor_scalar(
                  p8e[:, par * C:(par + 1) * C], p8d[:, par * C:(par + 1) * C],
                  ln8[:, 32 + i:32 + i + 1], ln8[:, 48 + i:48 + i + 1],
                  OP.add, OP.mult))
        S.inc("vector", "p8af", 1,
              lambda e, par=par: e.tensor_tensor(
                  p8e[:, par * C:(par + 1) * C], p8e[:, par * C:(par + 1) * C],
                  g2bc[:, :], OP.mult))
        S.inc("vector", "p8af", 1,
              lambda e, par=par: e.tensor_tensor(
                  p8e[:, par * C:(par + 1) * C], p8e[:, par * C:(par + 1) * C],
                  b2bc[:, :], OP.add))
        S.wait("gpsimd", "p8af")
        S.dma("gpsimd", "p8h4st%d" % par,
              sc["h4tok"][i * 128:(i + 1) * 128, :], p8e[:, par * C:(par + 1) * C])
        for ccj in range(8):
            tp = ccj % 2
            tps = ps_a if tp == 0 else ps_b
            S.wait("tensor", "p8af")
            S.wait_lag("tensor", "p8_ev", 1)
            S.inc("tensor", "p8_mm", 1,
                  lambda e, ccj=ccj, tps=tps, par=par: e.transpose(
                      tps[:, 0:128],
                      p8e[:, par * C + ccj * 128:par * C + (ccj + 1) * 128],
                      cb["ident"][:, :]))
            S.wait("scalar", "p8_mm")
            S.inc("scalar", "p8_ev", 1,
                  lambda e, ccj=ccj, i=i, tps=tps: e.copy(
                      h4C[:, ccj * TH + i * 128:ccj * TH + (i + 1) * 128],
                      tps[:, 0:128]))
        S.wait("tensor", "p8_ev")
        S.wait("tensor", "p8c")
        S.wait("tensor", "p8gev")
        for cc in range(8):
            S.inc("tensor", "p8gmm", 1,
                  lambda e, cc=cc, i=i: e.matmul(
                      ps_mid[:, 0:8],
                      h4C[:, cc * TH + i * 128:cc * TH + (i + 1) * 128],
                      wg_sb[:, cc * 8:(cc + 1) * 8],
                      start=(cc == 0), stop=(cc == 7)))
        S.wait("scalar", "p8gmm")
        S.inc("scalar", "p8gev", 1, lambda e: e.copy(gl8[:, :], ps_mid[:, 0:8]))
        S.wait("scalar", "p8gv")
        S.inc("scalar", "p8ge", 1,
              lambda e: e.activation(ge8[:, :], gl8[:, :], AF.Exp,
                                     accum_out=gs8[:, 0:1]))
        S.wait("vector", "p8ge")
        S.inc("vector", "p8gv", 1, lambda e: e.reciprocal(gs8[:, 1:2], gs8[:, 0:1]))
        S.inc("vector", "p8gv", 1,
              lambda e, i=i: e.tensor_scalar(gates_all[:, i * 8:(i + 1) * 8],
                                             ge8[:, :], gs8[:, 1:2], None,
                                             OP.mult))
        S.inc("vector", "p8gv", 1,
              lambda e, i=i: e.reduce_max(gs8[:, 2:3],
                                          gates_all[:, i * 8:(i + 1) * 8], axis=AX))
        S.inc("vector", "p8gv", 1,
              lambda e, i=i: e.tensor_scalar(gm8[:, :],
                                             gates_all[:, i * 8:(i + 1) * 8],
                                             gs8[:, 2:3], None, OP.is_ge))
        S.inc("vector", "p8gv", 1,
              lambda e, i=i: e.scalar_tensor_tensor(
                  gm8[:, :], gm8[:, :], -1e9,
                  gates_all[:, i * 8:(i + 1) * 8], OP.mult, OP.add))
        S.inc("vector", "p8gv", 1,
              lambda e: e.reduce_max(gs8[:, 3:4], gm8[:, :], axis=AX))
        S.wait("vector", "p8mst")
        S.inc("vector", "p8gv", 1,
              lambda e, i=i: e.tensor_scalar(mk8[:, :],
                                             gates_all[:, i * 8:(i + 1) * 8],
                                             gs8[:, 3:4], None, OP.is_ge))
        S.wait("gpsimd", "p8gv")
        S.dma("gpsimd", "p8mst", sc["cc_mkin"][i * 128:(i + 1) * 128, :],
              mk8[:, :])

    S.wait("gpsimd", "p8mst")
    S.inc("gpsimd", "p9cc", 1,
          lambda e: e.collective_compute(
              "AllGather", OP.bypass, replica_groups=[list(range(8))],
              ins=[sc["cc_mkin"][:, :].opt()], outs=[sc["cc_mkout"][:, :].opt()]))
    S.wait("gpsimd", "p9cc")

    if stop_after == "P8":
        with nc.Block() as blk:
            S.run(blk)
        close_all()
        return nc, dbg_out

    st_p8.close()
    S.join(ENGS, ("p8_mm", "p8gmm", "p8ld0", "p8ld1", "p8ld20", "p8ld21", "p8c",
                  "p8x3st0", "p8x3st1", "p8h4st0", "p8h4st1", "p8mst",
                  "ch_vector", "ch_scalar"))
    st_p9 = ExitStack()
    regions.append(st_p9)
    mA = sb("p9_mA", 128, 1024, st_p9)
    mB = sb("p9_mB", 128, 1024, st_p9)
    tot = sb("p9_tot", 128, 8, st_p9)
    offr = sb("p9_off", 128, 8, st_p9)
    kb9 = sb("p9_kb", 128, 1024, st_p9)
    neg1 = sb("p9_neg1", 1, 8, st_p9)

    S.wait("sync", "p9cc")
    S.dma("sync", "p9ld", mA[:, :],
          sc["cc_mkout"][:, :].rearrange("(g p) e -> g p e", p=128))
    S.inc("vector", "p9v", 1, lambda e: e.memset(neg1[:, :], -1.0))
    cur, nxt = mA, mB
    S.wait("vector", "p9ld")
    for k in [1, 2, 4, 8, 16, 32, 64]:
        S.inc("vector", "p9v", 1,
              lambda e, cur=cur, nxt=nxt, k=k: e.tensor_copy(
                  nxt[:, 0:k * 8], cur[:, 0:k * 8]))
        S.inc("vector", "p9v", 1,
              lambda e, cur=cur, nxt=nxt, k=k: e.tensor_tensor(
                  nxt[:, k * 8:1024], cur[:, k * 8:1024],
                  cur[:, 0:1024 - k * 8], OP.add))
        cur, nxt = nxt, cur
    cum = cur
    S.wait("tensor", "p9v")
    S.wait("tensor", "p9ld")
    S.inc("tensor", "p9mm", 1,
          lambda e: e.matmul(ps_a[:, 0:8], cb["triu_strict"][:, :],
                             cum[:, 127 * 8:128 * 8], start=True, stop=False))
    S.inc("tensor", "p9mm", 1,
          lambda e: e.matmul(ps_a[:, 0:8], cb["ones1_128"][:, :], neg1[:, :],
                             start=False, stop=True))
    S.wait("scalar", "p9mm")
    S.inc("scalar", "p9ev", 1, lambda e: e.copy(offr[:, :], ps_a[:, 0:8]))
    S.wait("vector", "p9ev")
    S.inc("vector", "p9v", 1,
          lambda e, cum=cum: e.tensor_tensor(
              cum[:, :].rearrange("a (p c) -> a p c", c=8), cum[:, :].rearrange(
                  "a (p c) -> a p c", c=8),
              offr[:, :].unsqueeze(1).broadcast_to([128, 128, 8]), OP.add))
    S.inc("vector", "p9v", 1,
          lambda e, cum=cum: e.tensor_scalar(kb9[:, :], cum[:, :], float(CAP),
                                             None, OP.is_lt))
    S.wait("sync", "p9v")
    S.dma("sync", "p9ld2", mA[:, :],
          sc["cc_mkout"][:, :].rearrange("(g p) e -> g p e", p=128))
    S.wait("vector", "p9ld2")
    S.inc("vector", "p9v", 1,
          lambda e: e.tensor_tensor(kb9[:, :], kb9[:, :], mA[:, :], OP.mult))
    sbase = sb("p9_sbase", 128, 8, st_p9)
    shmask = [(i // 16) * 16 for i in range(32)]
    S.inc("vector", "p9v", 1,
          lambda e: e.stream_shuffle(sbase[:, :], offr[:, :], shmask))
    S.inc("vector", "p9v", 1,
          lambda e, cum=cum: e.tensor_tensor(
              mA[:, :].rearrange("a (p c) -> a p c", c=8),
              cum[:, :].rearrange("a (p c) -> a p c", c=8),
              sbase[:, :].unsqueeze(1).broadcast_to([128, 128, 8]), OP.subtract))
    S.inc("vector", "p9v", 1,
          lambda e: e.tensor_scalar(mB[:, :], mA[:, :], 1.0, None, OP.subtract))
    def p9pairs_sb(e):
        reg = pid_reg(e)
        for j in range(8):
            with e.If_eq(reg, j):
                e.dma_start(
                    out=sc["keepown"][:, :].rearrange("(g p) e -> g (p e)", g=16),
                    in_=kb9[j * 16:(j + 1) * 16, :]).then_inc(
                    S.h["p9kst"], 16)
                e.dma_start(
                    out=sc["lposown"][:, :].rearrange("(g p) e -> g (p e)", g=16),
                    in_=mB[j * 16:(j + 1) * 16, :]).then_inc(
                    S.h["p9kst"], 16)
    S.sem("p9kst")
    S.wait("gpsimd", "p9v")
    S.cnt["p9kst"] += 32
    S.names["p9kst"] = ("p9kst", S.cnt["p9kst"])
    S.emit("gpsimd", p9pairs_sb)

    if "keep" in dbg_out:
        S.wait("gpsimd", "p9kst")
        S.dma("gpsimd", "dbg", dbg_out["keep"][:, :], sc["keepown"][:, :])
        S.dma("gpsimd", "dbg", dbg_out["gatesall"][:, :], gates_all[:, :])

    if stop_after == "P9":
        with nc.Block() as blk:
            S.run(blk)
        close_all()
        return nc, dbg_out

    BCP = 768
    DUMP = E * BCP
    st_p9.close()
    S.join(ENGS, ("p9mm", "p9ld", "p9ld2", "p9kst", "ch_vector", "ch_scalar"))
    st_pA = ExitStack()
    regions.append(st_pA)
    hidb = sb("hidb", 128, 2 * 32 * 384, st_pA)
    moeC = sb("moeC", 128, 2 * 1024, st_pA)
    w1b = sb("w1b", 128, 2 * 1024, st_pA)
    w2b = sb("w2b", 128, 2 * 1024, st_pA)
    b1t_sb = sb("b1t_sb", 128, E * 32, st_pA)
    b2_sb = sb("b2_sb", 8, C, st_pA)
    kt8 = sb("p10_kt", 128, 8, st_pA)
    lp8 = sb("p10_lp", 128, 8, st_pA)
    ev8 = sb("p10_ev", 128, 8 * 4, st_pA)
    sc8 = sb("p10_sc", 128, 16, st_pA)
    idxf = sb("p10_idxf", 128, 2, st_pA)
    idxi = st_pA.enter_context(nc.sbuf_tensor("p10_idxi", [128, 2 * 16],
                                               mybir.dt.int32))
    g12 = sb("p10_g12", 128, 2 * 16, st_pA)
    gkT = sb("p10_gkT", 8, 128, st_pA)
    evec_sb = sb("p10_evec", 128, 8, st_pA)
    bktt = sb("p10_bktt", 128, 2 * 1024, st_pA)
    bktC = sb("p10_bktC", 128, 8 * 384, st_pA)
    eo_sb = sb("p10_eo", 128, 2 * 1024, st_pA)
    ga1 = sb("p10_ga1", 128, C, st_pA)
    ga2 = sb("p10_ga2", 128, C, st_pA)
    xf1 = sb("p10_x1", 128, C, st_pA)
    xf2 = sb("p10_x2", 128, C, st_pA)

    S.dma("sync", "pAc", b1t_sb[:, :], di["b1t"][:, :])
    S.dma("sync", "pAc", b2_sb[:, :], di["b2sb"][:, :])
    S.dma("sync", "pAc", evec_sb[:, :], di["evec128"][:, :])

    S.inc("vector", "pAz", 1, lambda e: e.memset(ga1[:, :], 0.0))
    S.wait("gpsimd", "pAz")
    NBK = (E * BCP + 40 + 127) // 128
    for j in range(NBK):
        r0 = j * 128
        r1 = min(r0 + 128, E * BCP + 40)
        S.dma("gpsimd", "pAzst", sc["bkt"][r0:r1, :], ga1[0:r1 - r0, :])
    S.dma("gpsimd", "pAzst", sc["eout"][E * BCP:E * BCP + 40, :], ga1[0:40, :])

    for i in range(NTH):
        S.wait("sync", "pAdsp")
        S.dma("sync", "pAkld", kt8[:, :], sc["keepown"][i * 128:(i + 1) * 128, :])
        S.dma("sync", "pAkld", lp8[:, :], sc["lposown"][i * 128:(i + 1) * 128, :])
        S.wait("vector", "pAkld")
        S.wait("vector", "pAc")
        gsl = gates_all[:, i * 8:(i + 1) * 8]
        S.inc("vector", "pAdsp", 1,
              lambda e, i=i, gsl=gsl: e.tensor_tensor(
                  gk_all[:, i * 8:(i + 1) * 8], gsl, kt8[:, :], OP.mult))
        S.inc("vector", "pAdsp", 1,
              lambda e, gsl=gsl: e.reduce_max(sc8[:, 0:1], gsl, axis=AX))
        S.inc("vector", "pAdsp", 1,
              lambda e, gsl=gsl: e.tensor_scalar(ev8[:, 0:8], gsl, sc8[:, 0:1],
                                                 None, OP.is_ge))
        S.inc("vector", "pAdsp", 1,
              lambda e, gsl=gsl: e.scalar_tensor_tensor(
                  ev8[:, 16:24], ev8[:, 0:8], -1e9, gsl, OP.mult, OP.add))
        S.inc("vector", "pAdsp", 1,
              lambda e: e.reduce_max(sc8[:, 1:2], ev8[:, 16:24], axis=AX))
        S.inc("vector", "pAdsp", 1,
              lambda e, gsl=gsl: e.tensor_scalar(ev8[:, 8:16], gsl, sc8[:, 1:2],
                                                 None, OP.is_ge))
        S.inc("vector", "pAdsp", 1,
              lambda e: e.tensor_tensor(ev8[:, 8:16], ev8[:, 8:16], ev8[:, 0:8],
                                        OP.subtract))
        for k in range(2):
            eq = lambda k=k: ev8[:, k * 8:(k + 1) * 8]
            col = 2 + k * 4
            S.inc("vector", "pAdsp", 1,
                  lambda e, k=k, col=col: e.tensor_tensor(
                      ev8[:, 16:24], ev8[:, k * 8:(k + 1) * 8], evec_sb[:, :],
                      OP.mult))
            S.inc("vector", "pAdsp", 1,
                  lambda e, col=col: e.reduce_sum(sc8[:, col:col + 1],
                                                  ev8[:, 16:24], axis=AX))
            S.inc("vector", "pAdsp", 1,
                  lambda e, k=k: e.tensor_tensor(
                      ev8[:, 16:24], ev8[:, k * 8:(k + 1) * 8], lp8[:, :],
                      OP.mult))
            S.inc("vector", "pAdsp", 1,
                  lambda e, col=col: e.reduce_sum(sc8[:, col + 1:col + 2],
                                                  ev8[:, 16:24], axis=AX))
            S.inc("vector", "pAdsp", 1,
                  lambda e, k=k: e.tensor_tensor(
                      ev8[:, 16:24], ev8[:, k * 8:(k + 1) * 8], kt8[:, :],
                      OP.mult))
            S.inc("vector", "pAdsp", 1,
                  lambda e, col=col: e.reduce_sum(sc8[:, col + 2:col + 3],
                                                  ev8[:, 16:24], axis=AX))
            S.inc("vector", "pAdsp", 1,
                  lambda e, i=i, k=k: e.tensor_tensor(
                      ev8[:, 24:32], ev8[:, k * 8:(k + 1) * 8],
                      gk_all[:, i * 8:(i + 1) * 8], OP.mult))
            S.inc("vector", "pAdsp", 1,
                  lambda e, i=i, k=k: e.reduce_sum(g12[:, i * 2 + k:i * 2 + k + 1],
                                                   ev8[:, 24:32], axis=AX))
            dumpk = float(DUMP + i * 2 + k)
            S.inc("vector", "pAdsp", 1,
                  lambda e, col=col: e.tensor_scalar(
                      idxf[:, 0:1], sc8[:, col:col + 1], float(BCP), None,
                      OP.mult))
            S.inc("vector", "pAdsp", 1,
                  lambda e, col=col: e.tensor_tensor(
                      idxf[:, 0:1], idxf[:, 0:1], sc8[:, col + 1:col + 2],
                      OP.add))
            S.inc("vector", "pAdsp", 1,
                  lambda e, dumpk=dumpk: e.tensor_scalar(
                      idxf[:, 1:2], idxf[:, 0:1], -dumpk, None, OP.add))
            S.inc("vector", "pAdsp", 1,
                  lambda e, col=col: e.tensor_tensor(
                      idxf[:, 1:2], idxf[:, 1:2], sc8[:, col + 2:col + 3],
                      OP.mult))
            S.inc("vector", "pAdsp", 1,
                  lambda e, dumpk=dumpk: e.tensor_scalar(
                      idxf[:, 0:1], idxf[:, 1:2], dumpk, None, OP.add))
            S.inc("vector", "pAdsp", 1,
                  lambda e, i=i, k=k: e.tensor_copy(
                      idxi[:, i * 2 + k:i * 2 + k + 1], idxf[:, 0:1]))
    h4sb = bktt
    S.wait("gpsimd", "pAdsp")
    S.wait("gpsimd", "pAzst")
    S.wait("sync", "p8h4st0")
    S.wait("sync", "p8h4st1")
    for i in range(NTH):
        par = i % 2
        S.wait("sync", "pAsc")
        S.dma("sync", "pAh4ld", h4sb[:, par * 1024:(par + 1) * 1024],
              sc["h4tok"][i * 128:(i + 1) * 128, :])
        S.wait("gpsimd", "pAh4ld")
        for k in range(2):
            S.inc("gpsimd", "pAsc", 16,
                  lambda e, i=i, k=k, par=par: e.indirect_dma_start(
                      out=sc["bkt"][:, :],
                      out_offset=bass.IndirectOffsetOnAxis(
                          ap=idxi[:, i * 2 + k:i * 2 + k + 1], axis=0),
                      in_=h4sb[:, par * 1024:(par + 1) * 1024],
                      in_offset=None))
    nphase = 0
    for ee in range(E):
        for hf in range(2):
            hpar = nphase % 2
            hb0 = hpar * 32 * 384
            nphase += 1
            for jt in range(3):
                par = jt % 2
                S.wait("sync", "pAtr")
                S.wait("sync", "pAsc")
                S.dma("sync", "pAbld", bktt[:, par * 1024:(par + 1) * 1024],
                      sc["bkt"][ee * BCP + hf * 384 + jt * 128:
                                ee * BCP + hf * 384 + (jt + 1) * 128, :])
                S.wait("tensor", "pAbld")
                S.wait("tensor", "pAge")
                for ccj in range(8):
                    S.wait("tensor", "pAbev")
                    S.inc("tensor", "pAtr", 1,
                          lambda e, ccj=ccj, par=par: e.transpose(
                              ps_b[:, 0:128],
                              bktt[:, par * 1024 + ccj * 128:
                                   par * 1024 + (ccj + 1) * 128],
                              cb["ident"][:, :]))
                    S.wait("scalar", "pAtr")
                    S.inc("scalar", "pAbev", 1,
                          lambda e, ccj=ccj, jt=jt: e.copy(
                              bktC[:, ccj * 384 + jt * 128:
                                   ccj * 384 + (jt + 1) * 128],
                              ps_b[:, 0:128]))
            for ft in range(32):
                par = ft % 2
                S.wait("sync", "pAh1", target=max(0, S.cnt.get("pAh1", 0) - 8))
                S.dma("sync", "pAw1%d" % par,
                      w1b[:, par * 1024:(par + 1) * 1024], di["W1t"][ee, ft])
                S.wait("tensor", "pAw1%d" % par)
                S.wait("tensor", "pAbev")
                hps = ps_a if ft % 2 == 0 else ps_b
                S.wait_lag("tensor", "pAge", 1)
                for cc in range(8):
                    S.inc("tensor", "pAh1", 1,
                          lambda e, cc=cc, par=par, hps=hps: e.matmul(
                              hps[:, 0:384],
                              w1b[:, par * 1024 + cc * 128:
                                  par * 1024 + (cc + 1) * 128],
                              bktC[:, cc * 384:(cc + 1) * 384],
                              start=(cc == 0), stop=(cc == 7)))
                S.wait("scalar", "pAh1")
                S.wait("scalar", "pAw2mm",
                       target=max(0, S.cnt.get("pAw2mm", 0) - 192))
                S.inc("scalar", "pAge", 1,
                      lambda e, ft=ft, ee=ee, hb0=hb0, hps=hps: e.activation(
                          hidb[:, hb0 + ft * 384:hb0 + (ft + 1) * 384],
                          hps[:, 0:384], AF.Gelu,
                          bias=b1t_sb[:, ee * 32 + ft:ee * 32 + ft + 1],
                          scale=1.0))
            w2ps = [ps_big[:, 0:512], ps_big[:, 512:1024],
                    ps_big[:, 1024:1536], ps_big[:, 1536:2048],
                    ps_mid[:, 0:512], ps_mid[:, 512:1024]]
            for ft in range(32):
                par = ft % 2
                S.wait("sync", "pAw2mm",
                       target=max(0, S.cnt.get("pAw2mm", 0) - 6))
                S.dma("sync", "pAw2%d" % par,
                      w2b[:, par * 1024:(par + 1) * 1024],
                      di["W2"][ee, ft * 128:(ft + 1) * 128, :])
                S.wait("tensor", "pAw2%d" % par)
                S.wait("tensor", "pAge")
                if ft == 0:
                    S.wait("tensor", "pAeev")
                for jt in range(3):
                    for oh in range(2):
                        S.inc("tensor", "pAw2mm", 1,
                              lambda e, jt=jt, oh=oh, ft=ft, par=par, hb0=hb0:
                              e.matmul(
                                  w2ps[jt * 2 + oh],
                                  hidb[:, hb0 + ft * 384 + jt * 128:
                                       hb0 + ft * 384 + (jt + 1) * 128],
                                  w2b[:, par * 1024 + oh * 512:
                                      par * 1024 + (oh + 1) * 512],
                                  start=(ft == 0), stop=(ft == 31)))
            S.wait("scalar", "pAw2mm")
            for jt in range(3):
                jp = jt % 2
                S.wait("scalar", "pAest%d" % jp)
                S.inc("scalar", "pAeev", 1,
                      lambda e, jt=jt, jp=jp: e.copy(
                          eo_sb[:, jp * 1024:(jp + 1) * 1024],
                          ps_mid[:, :] if jt == 2 else
                          ps_big[:, jt * 1024:(jt + 1) * 1024]))
                S.wait("gpsimd", "pAeev")
                S.dma("gpsimd", "pAest%d" % jp,
                      sc["eout"][ee * BCP + hf * 384 + jt * 128:
                                 ee * BCP + hf * 384 + (jt + 1) * 128, :],
                      eo_sb[:, jp * 1024:(jp + 1) * 1024])
    S.wait("gpsimd", "pAest0")
    S.wait("gpsimd", "pAest1")
    for i in range(NTH):
        S.wait("tensor", "pAgkT")
        S.wait("tensor", "pAfin")
        S.inc("tensor", "pAtr2", 1,
              lambda e, i=i: e.transpose(ps_b[0:8, 0:128],
                                         gk_all[:, i * 8:(i + 1) * 8],
                                         cb["ident"][:, :]))
        S.wait("scalar", "pAtr2")
        S.inc("scalar", "pAgkT", 1, lambda e: e.copy(gkT[:, :], ps_b[0:8, 0:128]))
        S.wait("tensor", "pAgkT")
        S.wait("tensor", "pAc")
        S.wait("tensor", "pAinev")
        for oh in range(2):
            S.inc("tensor", "pAinmm", 1,
                  lambda e, oh=oh: e.matmul(
                      ps_mid[:, oh * 512:(oh + 1) * 512], gkT[:, :],
                      b2_sb[:, oh * 512:(oh + 1) * 512], start=True, stop=True))
        S.wait("scalar", "pAinmm")
        S.wait("scalar", "pAfin")
        S.inc("scalar", "pAinev", 1,
              lambda e, i=i: e.copy(moeC[:, (i % 2) * 1024:(i % 2 + 1) * 1024],
                                    ps_mid[:, :]))
        S.wait("gpsimd", "pAfin")
        for k, gbuf in ((0, ga1), (1, ga2)):
            S.inc("gpsimd", "pAga", 16,
                  lambda e, i=i, k=k, gbuf=gbuf: e.indirect_dma_start(
                      out=gbuf[:, :], out_offset=None,
                      in_=sc["eout"][:, :],
                      in_offset=bass.IndirectOffsetOnAxis(
                          ap=idxi[:, i * 2 + k:i * 2 + k + 1], axis=0)))
        S.wait("sync", "pAfin")
        S.wait("sync", "pAost")
        S.dma("sync", "pAfld", xf1[:, :], sc["x2own"][i * 128:(i + 1) * 128, :])
        S.dma("sync", "pAfld", xf2[:, :], sc["x3"][i * 128:(i + 1) * 128, :])
        S.wait("vector", "pAfld")
        S.wait("vector", "pAga")
        S.wait("vector", "pAinev")
        S.inc("vector", "pAfin", 1,
              lambda e, i=i: e.scalar_tensor_tensor(
                  moeC[:, (i % 2) * 1024:(i % 2 + 1) * 1024], ga1[:, :],
                  g12[:, i * 2:i * 2 + 1],
                  moeC[:, (i % 2) * 1024:(i % 2 + 1) * 1024], OP.mult, OP.add))
        S.inc("vector", "pAfin", 1,
              lambda e, i=i: e.scalar_tensor_tensor(
                  moeC[:, (i % 2) * 1024:(i % 2 + 1) * 1024], ga2[:, :],
                  g12[:, i * 2 + 1:i * 2 + 2],
                  moeC[:, (i % 2) * 1024:(i % 2 + 1) * 1024], OP.mult, OP.add))
        S.inc("vector", "pAfin", 1,
              lambda e, i=i: e.tensor_tensor(
                  xf2[:, :], moeC[:, (i % 2) * 1024:(i % 2 + 1) * 1024],
                  xf2[:, :], OP.add))
        S.inc("vector", "pAfin", 1,
              lambda e: e.tensor_tensor(xf2[:, :], xf1[:, :], xf2[:, :], OP.add))
        S.wait("gpsimd", "pAfin")
        S.dma("gpsimd", "pAost", out[i * 128:(i + 1) * 128, :], xf2[:, :])

    S.wait("gpsimd", "pAost")
    with nc.Block() as blk:
        S.run(blk)
    close_all()
    return nc, dbg_out


def _final_inputs(ins):
    return [
        _core_inputs(ins, core) for core in range(8)
    ]


def kernel(**inputs):
    nc, _ = build_program()
    in_maps = _final_inputs(inputs)
    res = run_bass_kernel_spmd(nc, in_maps, core_ids=list(range(8)))
    o = np.zeros((B, T, C), np.float32)
    for core in range(8):
        b = core // 2
        hg = core % 2
        o[b, hg * TH:(hg + 1) * TH, :] = np.asarray(res.results[core]["out"])
    return o
